# version 08
# baseline (speedup 1.0000x reference)
import numpy as np
import os as _os2
import ml_dtypes
import concourse.bass as bass
import concourse.mybir as mybir
from concourse.bass_utils import run_bass_kernel_spmd

F32 = mybir.dt.float32
BF16 = mybir.dt.bfloat16
U8 = mybir.dt.uint8
AF = mybir.ActivationFunctionType
ALU = mybir.AluOpType
AX = mybir.AxisListType

N, D, TOK, NTT, UH, E = 8192, 1024, 2048, 16, 4, 8
EPS = 1e-5
INV_D = 1.0 / D

STOP_AFTER = None
RS_EARLY = True
PRE_B = False
SAME_ENGINE_WAITS = True
DBG = False


class Buf:
    __slots__ = ("name", "writer", "readers")

    def __init__(self, name):
        self.name = name
        self.writer = None
        self.readers = {}


class Prog:
    COMPUTE_SEM = {"tensor": "pe", "vector": "dv", "scalar": "ac", "gpsimd": "gp"}
    DMA_SEM = {"sync": "qs", "vector": "qv", "scalar": "qc", "gpsimd": "qg"}
    SECTIONS = ("tensor", "vector", "scalar", "gpsimd", "sync")

    def __init__(self):
        self.ops = {s: [] for s in self.SECTIONS}
        self.counts = {s: 0 for s in
                       ("pe", "dv", "ac", "gp", "qs", "qv", "qc", "qg", "cc")}
        self.synced = {}

    def _dep(self, waits, section, sem, val, psection):
        if val <= 0:
            return
        if psection == section and self.COMPUTE_SEM.get(section) == sem:
            if section == "tensor" or not SAME_ENGINE_WAITS:
                return
        if self.synced.get((section, sem), 0) >= val:
            return
        if waits.get(sem, 0) < val:
            waits[sem] = val

    def op(self, section, emit, reads=(), writes=(), sem=None, inc=None, tag=""):
        if sem is None:
            sem = self.COMPUTE_SEM[section]
        if inc is None:
            inc = 16 if sem in ("qs", "qv", "qc", "qg") else 1
        waits = {}
        for b in reads:
            if b.writer is not None:
                self._dep(waits, section, b.writer[0], b.writer[1], b.writer[2])
        for b in writes:
            if b.writer is not None:
                self._dep(waits, section, b.writer[0], b.writer[1], b.writer[2])
            for rsem, (rval, rsec) in b.readers.items():
                self._dep(waits, section, rsem, rval, rsec)
        for sname, val in waits.items():
            key = (section, sname)
            self.synced[key] = max(self.synced.get(key, 0), val)
        self.counts[sem] += inc
        newval = self.counts[sem]
        self.ops[section].append((waits, emit, sem, inc, tag))
        for b in reads:
            old = b.readers.get(sem)
            if old is None or old[0] < newval:
                b.readers[sem] = (newval, section)
        for b in writes:
            b.writer = (sem, newval, section)
            b.readers = {}
        return newval

    def dma(self, section, out_ap, in_ap, reads=(), writes=(), tag="",
            slow=False):
        sem = self.DMA_SEM[section]

        def emit(eng, o=out_ap, i=in_ap, slow=slow):
            return eng.dma_start(out=o, in_=i,
                                 allow_slow_non_contiguous=slow)
        return self.op(section, emit, reads=reads, writes=writes, sem=sem,
                       inc=16, tag=tag)

    def barrier(self, tag="barrier"):
        snap = dict(self.counts)
        for sec in self.SECTIONS:
            waits = {}
            for sname, val in snap.items():
                if val > 0 and self.synced.get((sec, sname), 0) < val:
                    waits[sname] = val
            for sname, val in waits.items():
                self.synced[(sec, sname)] = val
            if sec == "sync":
                sem, inc = "qs", 0
            else:
                sem, inc = self.COMPUTE_SEM[sec], 1
            if inc:
                self.counts[sem] += inc

            def emit(eng):
                return eng.nop()
            self.ops[sec].append((waits, emit, sem, inc, tag))


def build_nc(stop_after=None):
    nc = bass.Bass("TRN2", target_bir_lowering=False, debug=False)
    p = Prog()

    def ein(name, shape):
        return nc.dram_tensor(name, list(shape), F32, kind="ExternalInput").ap()

    x_in = ein("x", (N, D))
    xown_in = ein("xown", (TOK, D))
    wq_slice = ein("wq_slice", (D, 768))
    wp_slice = ein("wp_slice", (256, D))
    wqkv = ein("wqkv", (D, 3 * D))
    wqkvT = ein("wqkvT", (3 * D, D))
    u_qkv = ein("u_qkv", (1, 3 * D))
    sp_qkv = ein("sp_qkv", (1, 1))
    wproj = ein("wproj", (D, D))
    wprojT = ein("wprojT", (D, D))
    u_proj = ein("u_proj", (1, D))
    sp_proj = ein("sp_proj", (1, 1))
    w1t_in = ein("w1t", (E, D, 1024))
    w3t_in = ein("w3t", (E, D, 1024))
    w2t_in = ein("w2t", (E, 1024, D))
    ws1t_in = ein("ws1t", (D, 2048))
    ws3t_in = ein("ws3t", (D, 2048))
    ws2t_in = ein("ws2t", (2048, D))
    gwt_in = ein("gwt", (D, E))
    gate_b_in = ein("gate_b", (1, E))
    b1c_in = ein("b1c", (128, E * 8))
    b3c_in = ein("b3c", (128, E * 8))
    bs1c_in = ein("bs1c", (128, 16))
    bs3c_in = ein("bs3c", (128, 16))
    b2bs2_in = ein("b2bs2", (9, D))
    ab_w1t_in = ein("ab_w1t", (D, D))
    ab_w2t_in = ein("ab_w2t", (D, 16))
    ab_w3t_in = ein("ab_w3t", (D, 16))
    ab_b1_in = ein("ab_b1", (1, D))
    ab_b2_in = ein("ab_b2", (1, 16))
    ab_b3_in = ein("ab_b3", (1, 16))
    fmp_in = ein("fmp", (128, 1024))

    y_out = nc.dram_tensor("y", [TOK, D], F32, kind="ExternalOutput").ap()

    cinA = nc.dram_tensor("cinA", [N // 2, D], F32).ap()
    cinB = nc.dram_tensor("cinB", [N // 2, D], F32).ap()
    coutA = nc.dram_tensor("coutA", [TOK // 2, D], F32).ap()
    coutB = nc.dram_tensor("coutB", [TOK // 2, D], F32).ap()
    scr_w1t = nc.dram_tensor("scr_w1t", [E, 128, 8 * 1024], BF16).ap()
    scr_w3t = nc.dram_tensor("scr_w3t", [E, 128, 8 * 1024], BF16).ap()
    scr_w2t = nc.dram_tensor("scr_w2t", [E, 128, 8 * D], BF16).ap()
    scr_ws1t = nc.dram_tensor("scr_ws1t", [128, 8 * 2048], BF16).ap()
    scr_ws3t = nc.dram_tensor("scr_ws3t", [128, 8 * 2048], BF16).ap()
    scr_ws2t = nc.dram_tensor("scr_ws2t", [128, 16 * D], BF16).ap()
    scr_abw1 = nc.dram_tensor("scr_abw1", [128, 8 * D], BF16).ap()

    ARENA = 206848
    arena = nc.alloc_sbuf_tensor("arena", [128, ARENA], U8)
    arena_f = arena[:].bitcast(F32)
    arena_b = arena[:].bitcast(BF16)

    def av(off, parts, cols, dtype):
        assert off % 32 == 0, off
        if dtype == F32:
            assert off + cols * 4 <= ARENA, (off, cols)
            return arena_f[0:parts, off // 4: off // 4 + cols]
        assert off + cols * 2 <= ARENA, (off, cols)
        return arena_b[0:parts, off // 2: off // 2 + cols]

    ones_f = av(0, 128, 128, F32)
    ones_b = av(512, 128, 128, BF16)
    ident_f = av(768, 128, 128, F32)
    ident_b = av(1280, 128, 128, BF16)
    c28b = av(1536, 128, 8, F32)
    ccb = av(1568, 128, 8, F32)
    scal = av(1600, 1, 24, F32)
    stat = av(1696, 128, 120, F32)
    gwt_sb = av(2176, 128, 8 * E, BF16)
    gate_b_sb = av(2304, 128, E, F32)
    b1c_sb = av(2336, 128, 64, F32)
    b3c_sb = av(2592, 128, 64, F32)
    bs1c_sb = av(2848, 128, 16, F32)
    bs3c_sb = av(2912, 128, 16, F32)
    ab_b1_sb = av(2976, 128, D, F32)
    ab_b2_sb = av(7072, 128, 16, F32)
    ab_b3_sb = av(7136, 128, 16, F32)
    b2bs2_sb = av(7200, 9, D, BF16)
    negst = av(11296, 128, 256, BF16)
    negm_w = av(11808, 128, 2048, BF16)
    lr = av(15904, 128, 16, F32)
    linv = av(15968, 1, 1024, F32)
    abw2t_sb = av(20064, 128, 8 * 16, BF16)
    abw3t_sb = av(20320, 128, 8 * 16, BF16)
    wsmall = av(20576, 128, 40, F32)
    junk = av(20736, 128, 1024, F32)
    mstat = av(24832, 128, 256, F32)
    gta_t = av(25856, 9, 128, F32)
    gate_sb = av(26368, 128, NTT * E, F32)
    gsc = av(26880, 128, 8, F32)
    bsb = av(26912, 128, 16, F32)
    bcv = av(26976, 128, 16, F32)
    e0b = av(27040, 128, 128, BF16)
    e0f = av(27296, 128, 128, F32)

    OFF_W = 28928
    x_own = av(OFF_W, 128, NTT * D, F32)
    xa = av(OFF_W, 128, 4 * D, F32)
    xnc = av(OFF_W + 16384, 128, 4 * D, BF16)
    xnTc = av(OFF_W + 24576, 128, 4 * D, BF16)
    wp2_sb = av(OFF_W + 32768, 128, 2 * D, BF16)
    obfT = av(OFF_W, 128, 2 * N, BF16)
    ptile = [av(OFF_W + 36864 + i * 2048, 128, 1024, BF16) for i in range(4)]
    o_acc = [av(OFF_W + 45056 + i * 4096, 128, 1024, F32) for i in range(4)]
    o_acc4 = av(OFF_W + 45056, 128, 4096, F32)
    ystage = av(94464, 128, 4 * D, F32)
    wq_sb = av(OFF_W + 36864, 128, 8 * 768, BF16)
    fmp_sb = av(OFF_W + 61440, 128, 1024, F32)

    OFF_Q = 94464
    q_fm = av(OFF_Q, 128, 2 * N, BF16)
    sgst = [av(OFF_Q, 128, 3 * D, F32), av(OFF_Q + 12288, 128, 3 * D, F32)]
    srow = av(OFF_Q + 24576, 1, 3 * D, F32)
    arow = av(OFF_Q + 28672, 1, 3 * D, F32)
    OFF_K = 127232
    k_fm = av(OFF_K, 128, 2 * N, BF16)
    srow = av(OFF_K, 1, 3 * D, F32)
    arow = av(OFF_K + 12288, 1, 3 * D, F32)
    xn_t = av(OFF_K, 128, NTT * D, BF16)
    xn3 = xn_t
    w1sb = [av(OFF_K, 128, 8 * 1024, BF16),
            av(OFF_K + 16384, 128, 8 * 1024, BF16)]
    xnT = av(OFF_Q, 128, 8 * TOK, BF16)

    OFF_V = 160000
    v_fm = av(OFF_V, 128, 64 * 260, BF16)
    w3sb = av(OFF_V, 128, 8 * 1024, BF16)
    w2sb = av(OFF_V + 16384, 128, 8 * D, BF16)
    abw1_sb = av(OFF_V, 128, 8 * D, BF16)
    delta_v = av(OFF_V + 16384, 128, D, F32)
    dsc1 = av(OFF_V + 20480, 128, D, F32)
    dsc2 = av(OFF_V + 24576, 128, D, F32)

    OFF_T = 193280
    wst = [av(OFF_T, 128, 1024, F32), av(OFF_T + 4096, 128, 1024, F32)]
    wsbst = [av(OFF_T + 8192, 128, 1024, BF16),
             av(OFF_T + 10240, 128, 1024, BF16)]
    h1sb = av(OFF_T, 128, 8 * 512, BF16)

    P = [nc.alloc_psum_tensor(f"P{i}", [128, 1024], F32) for i in range(4)]
    PB3 = P[3][:].bitcast(BF16)
    PB2A = P[2][:].bitcast(BF16)

    sems = {n: nc.alloc_semaphore(n)
            for n in ("pe", "dv", "ac", "gp", "qs", "qv", "qc", "qg", "cc")}

    bufs = {}

    def BF(name):
        if name not in bufs:
            bufs[name] = Buf(name)
        return bufs[name]

    PBK = [(BF(f"P{i}a"), BF(f"P{i}b")) for i in range(4)]

    def t_op(emit, reads=(), writes=(), tag=""):
        return p.op("tensor", emit, reads, writes, tag=tag)

    def v_op(emit, reads=(), writes=(), tag=""):
        return p.op("vector", emit, reads, writes, tag=tag)

    def s_op(emit, reads=(), writes=(), tag=""):
        return p.op("scalar", emit, reads, writes, tag=tag)

    def g_op(emit, reads=(), writes=(), tag=""):
        return p.op("gpsimd", emit, reads, writes, tag=tag)

    g_op(lambda g: g.memset(ones_f, 1.0), writes=[BF("ones_f")])
    g_op(lambda g: g.memset(ones_b, 1.0), writes=[BF("ones_b")])
    g_op(lambda g: g.memset(gta_t, 1.0), writes=[BF("gta_ones")])
    g_op(lambda g: g.memset(v_fm, 1.0), writes=[BF("v_fm")])
    def emit_e0(t, buf):
        def e(g, t=t):
            g.memset(t, 0.0)
            g.memset(t[0:1, :], 1.0)
            return g.memset(t[64:65, :], 1.0)
        g_op(e, writes=[buf])
    emit_e0(e0b, BF("e0b"))
    emit_e0(e0f, BF("e0f"))
    g_op(lambda g: g.memset(negm_w, 0.0), writes=[BF("negm_w")])
    for t_, b_ in ((gate_b_sb, "gate_b_sb"), (ab_b1_sb, "ab_b1_sb"),
                   (ab_b2_sb, "ab_b2_sb"), (ab_b3_sb, "ab_b3_sb")):
        g_op(lambda g, t_=t_: g.memset(t_, 0.0), writes=[BF(b_)])

    def mk_ident(t, buf):
        def emit(g, t=t):
            g.memset(t, 0.0)
            return g.affine_select(out=t, in_=t, compare_op=ALU.not_equal,
                                   fill=1.0, base=0, pattern=[[-1, 128]],
                                   channel_multiplier=1)
        g_op(emit, writes=[buf])
    mk_ident(ident_b, BF("ident_b"))
    mk_ident(ident_f, BF("ident_f"))

    p.dma("sync", fmp_sb, fmp_in, writes=[BF("fmp")])
    for name, dst, src in (
            ("gate_b_sb", gate_b_sb[0:1, :], gate_b_in),
            ("b1c_sb", b1c_sb, b1c_in), ("b3c_sb", b3c_sb, b3c_in),
            ("bs1c_sb", bs1c_sb, bs1c_in), ("bs3c_sb", bs3c_sb, bs3c_in),
            ("ab_b1_sb", ab_b1_sb[0:1, :], ab_b1_in),
            ("ab_b2_sb", ab_b2_sb[0:1, :], ab_b2_in), ("ab_b3_sb", ab_b3_sb[0:1, :], ab_b3_in)):
        p.dma("sync", dst, src, writes=[BF(name)])

    def load_cast(dst_sb, dst_buf, hbm, nchunks, cols, tag):
        view = hbm.rearrange("(c p) e -> p c e", p=128)
        for c in range(nchunks):
            stb = BF(f"wst{c % 2}")
            p.dma("sync", wst[c % 2][:, 0:cols], view[:, c, :],
                  writes=[stb], tag=tag)
            v_op(lambda v, c=c: v.tensor_copy(
                    dst_sb[:, c * cols:(c + 1) * cols], wst[c % 2][:, 0:cols]),
                 reads=[stb], writes=[dst_buf], tag=tag)

    p.dma("sync", wst[0][0:9, 0:D], b2bs2_in, writes=[BF("wst0")], tag="b2b")
    v_op(lambda v: v.tensor_copy(b2bs2_sb, wst[0][0:9, 0:D]),
         reads=[BF("wst0")], writes=[BF("b2bs2_sb")], tag="b2b")
    load_cast(gwt_sb, BF("gwt_sb"), gwt_in, 8, E, "gwt")
    load_cast(abw2t_sb, BF("abw2t_sb"), ab_w2t_in, 8, 16, "abw2t")
    load_cast(abw3t_sb, BF("abw3t_sb"), ab_w3t_in, 8, 16, "abw3t")
    load_cast(wq_sb, BF("wq_sb"), wq_slice, 8, 768, "wqs")

    def emit_sigma(w_nat, w_T, u_row, sp_row, m, idx, tg):
        mc = m // 128
        p.dma("sync", srow[0:1, 0:m], u_row, writes=[BF("srow")], tag=tg)
        for i in range(mc):
            t_op(lambda t, i=i: t.transpose(P[3][0:128, 0:1],
                                            srow[0:1, i * 128:(i + 1) * 128],
                                            ident_f[0:1, 0:1]),
                 reads=[BF("srow"), BF("ident_f")], writes=[PBK[3][0]], tag=tg)
            v_op(lambda v, i=i: v.tensor_copy(wsmall[:, i:i + 1],
                                              P[3][0:128, 0:1]),
                 reads=[PBK[3][0]], writes=[BF("ucol")], tag=tg)
        for i in range(mc):
            stb = BF(f"sgst{i % 2}")
            p.dma("sync", sgst[i % 2][:, 0:D], w_T[i * 128:(i + 1) * 128, :],
                  writes=[stb], tag=tg)
            for sl in range(2):
                t_op(lambda t, i=i, sl=sl: t.matmul(
                        P[2][0:1, sl * 512:(sl + 1) * 512],
                        wsmall[:, i:i + 1],
                        sgst[i % 2][:, sl * 512:(sl + 1) * 512],
                        start=(i == 0), stop=(i == mc - 1)),
                     reads=[BF("ucol"), stb], writes=[PBK[2][sl]], tag=tg)
        for sl in range(2):
            v_op(lambda v, sl=sl: v.tensor_copy(
                    arow[0:1, sl * 512:(sl + 1) * 512],
                    P[2][0:1, sl * 512:(sl + 1) * 512]),
                 reads=[PBK[2][sl]], writes=[BF("arow")], tag=tg)
        if tg == "sp" and DBG:
            dbgs = nc.dram_tensor("dbgs", [128, 1040], F32,
                                  kind="ExternalOutput").ap()
            p.dma("sync", dbgs[0:1, 0:1024], arow[0:1, 0:D],
                  reads=[BF("arow")], writes=[BF("dbgs")], tag=tg)
            p.dma("sync", dbgs[:, 1024:1032], wsmall[:, 0:8],
                  reads=[BF("ucol")], writes=[BF("dbgs")], tag=tg)
            p.dma("sync", dbgs[:, 1032:1040], wsmall[:, 24:32],
                  reads=[BF("acol")], writes=[BF("dbgs")], tag=tg)
        s_op(lambda s: s.activation(junk[0:1, 0:D], arow[0:1, 0:D], AF.Square,
                                    accum_out=scal[0:1, 8 + idx:9 + idx]),
             reads=[BF("arow")], writes=[BF("junk"), BF("n1" + tg)], tag=tg)
        for i in range(8):
            t_op(lambda t, i=i: t.transpose(P[3][0:128, 0:1],
                                            arow[0:1, i * 128:(i + 1) * 128],
                                            ident_f[0:1, 0:1]),
                 reads=[BF("arow"), BF("ident_f")], writes=[PBK[3][0]], tag=tg)
            v_op(lambda v, i=i: v.tensor_copy(wsmall[:, 24 + i:25 + i],
                                              P[3][0:128, 0:1]),
                 reads=[PBK[3][0]], writes=[BF("acol")], tag=tg)
        nsl = m // 512
        bank_of = [PBK[0][0], PBK[0][1], PBK[1][0], PBK[1][1],
                   PBK[2][0], PBK[2][1]]
        pt_of = [P[0], P[0], P[1], P[1], P[2], P[2]]
        for i in range(8):
            stb = BF(f"sgst{i % 2}")
            p.dma("sync", sgst[i % 2][:, 0:m], w_nat[i * 128:(i + 1) * 128, :],
                  writes=[stb], tag=tg)
            for sl in range(nsl):
                t_op(lambda t, i=i, sl=sl: t.matmul(
                        pt_of[sl][0:1, (sl % 2) * 512:(sl % 2) * 512 + 512],
                        wsmall[:, 24 + i:25 + i],
                        sgst[i % 2][:, sl * 512:(sl + 1) * 512],
                        start=(i == 0), stop=(i == 7)),
                     reads=[BF("acol"), stb], writes=[bank_of[sl]], tag=tg)
        for sl in range(nsl):
            v_op(lambda v, sl=sl: v.tensor_copy(
                    srow[0:1, sl * 512:(sl + 1) * 512],
                    pt_of[sl][0:1, (sl % 2) * 512:(sl % 2) * 512 + 512]),
                 reads=[bank_of[sl]], writes=[BF("srow")], tag=tg)
        s_op(lambda s, m=m: s.activation(arow[0:1, 0:m], srow[0:1, 0:m],
                                         AF.Square,
                                         accum_out=scal[0:1, 10 + idx:11 + idx]),
             reads=[BF("srow")], writes=[BF("arow"), BF("n2" + tg)], tag=tg)
        p.dma("sync", scal[0:1, 12 + idx:13 + idx], sp_row,
              writes=[BF("sp" + tg)], tag=tg)
        s_op(lambda s: s.activation(scal[0:1, 14 + idx:15 + idx],
                                    scal[0:1, 8 + idx:9 + idx], AF.Ln),
             reads=[BF("n1" + tg)], writes=[BF("l1" + tg)], tag=tg)
        s_op(lambda s: s.activation(scal[0:1, 16 + idx:17 + idx],
                                    scal[0:1, 10 + idx:11 + idx], AF.Ln),
             reads=[BF("n2" + tg)], writes=[BF("l2" + tg)], tag=tg)
        v_op(lambda v: v.tensor_sub(scal[0:1, 18 + idx:19 + idx],
                                    scal[0:1, 14 + idx:15 + idx],
                                    scal[0:1, 16 + idx:17 + idx]),
             reads=[BF("l1" + tg), BF("l2" + tg)], writes=[BF("df" + tg)],
             tag=tg)
        s_op(lambda s: s.activation(scal[0:1, 20 + idx:21 + idx],
                                    scal[0:1, 18 + idx:19 + idx], AF.Exp,
                                    scale=0.5),
             reads=[BF("df" + tg)], writes=[BF("ex" + tg)], tag=tg)
        v_op(lambda v: v.tensor_mul(scal[0:1, idx:idx + 1],
                                    scal[0:1, 20 + idx:21 + idx],
                                    scal[0:1, 12 + idx:13 + idx]),
             reads=[BF("ex" + tg), BF("sp" + tg)], writes=[BF("c" + tg)],
             tag=tg)

    emit_sigma(wqkv, wqkvT, u_qkv, sp_qkv, 3 * D, 0, "sq")
    emit_sigma(wproj, wprojT, u_proj, sp_proj, D, 1, "sp")

    v_op(lambda v: v.tensor_scalar(scal[0:1, 2:3], scal[0:1, 0:1],
                                   scal[0:1, 0:1], 0.125,
                                   op0=ALU.mult, op1=ALU.mult),
         reads=[BF("csq")], writes=[BF("c28s")])
    v_op(lambda v: v.tensor_mul(scal[0:1, 3:4], scal[0:1, 0:1],
                                scal[0:1, 1:2]),
         reads=[BF("csq"), BF("csp")], writes=[BF("ccs")])
    t_op(lambda t: t.matmul(P[3][0:128, 0:1], ones_f[0:1, :], scal[0:1, 2:3],
                            start=True, stop=True),
         reads=[BF("ones_f"), BF("c28s")], writes=[PBK[3][0]])
    v_op(lambda v: v.tensor_copy(c28b[:, 0:1], P[3][0:128, 0:1]),
         reads=[PBK[3][0]], writes=[BF("c28b")])
    t_op(lambda t: t.matmul(P[3][0:128, 1:2], ones_f[0:1, :], scal[0:1, 3:4],
                            start=True, stop=True),
         reads=[BF("ones_f"), BF("ccs")], writes=[PBK[3][0]])
    v_op(lambda v: v.tensor_copy(ccb[:, 0:1], P[3][0:128, 1:2]),
         reads=[PBK[3][0]], writes=[BF("ccb")])

    p.barrier("post-sigma")

    prep_jobs = []
    for e in range(E):
        for c in range(8):
            prep_jobs.append((w1t_in[e].rearrange("(c p) i -> p c i", p=128),
                              (c, 0), scr_w1t[e][:, c * 1024:(c + 1) * 1024]))
            prep_jobs.append((w3t_in[e].rearrange("(c p) i -> p c i", p=128),
                              (c, 0), scr_w3t[e][:, c * 1024:(c + 1) * 1024]))
            prep_jobs.append((w2t_in[e].rearrange("(c p) i -> p c i", p=128),
                              (c, 0), scr_w2t[e][:, c * 1024:(c + 1) * 1024]))
    vws1 = ws1t_in.rearrange("(c p) i -> p c i", p=128)
    vws3 = ws3t_in.rearrange("(c p) i -> p c i", p=128)
    vws2 = ws2t_in.rearrange("(c p) i -> p c i", p=128)
    vab1 = ab_w1t_in.rearrange("(c p) i -> p c i", p=128)
    for c in range(8):
        for half in range(2):
            prep_jobs.append((vws1, (c, half * 1024),
                              scr_ws1t[:, half * 8192 + c * 1024:
                                       half * 8192 + c * 1024 + 1024]))
            prep_jobs.append((vws3, (c, half * 1024),
                              scr_ws3t[:, half * 8192 + c * 1024:
                                       half * 8192 + c * 1024 + 1024]))
    for c in range(16):
        prep_jobs.append((vws2, (c, 0), scr_ws2t[:, c * 1024:(c + 1) * 1024]))
    for c in range(8):
        prep_jobs.append((vab1, (c, 0), scr_abw1[:, c * 1024:(c + 1) * 1024]))

    prep_i = [0]

    def emit_prep(npieces):
        for _ in range(npieces):
            j = prep_i[0]
            if j >= len(prep_jobs):
                return
            prep_i[0] += 1
            view, (c, off), dst = prep_jobs[j]
            src = view[:, c, off:off + 1024]
            i = j % 2
            stb, obb = BF(f"wst{i}"), BF(f"wsbst{i}")
            p.dma("scalar", wst[i][:, 0:1024], src, writes=[stb], tag="prep")
            s_op(lambda s, i=i: s.activation(wsbst[i][:, 0:1024],
                                             wst[i][:, 0:1024], AF.Copy),
                 reads=[stb], writes=[obb], tag="prep")
            p.dma("scalar", dst, wsbst[i][:, 0:1024], reads=[obb],
                  writes=[BF("scr")], tag="prep")

    def emit_ln(src, src_buf, ntiles, xn_dst, xn_buf, tag, base=0, xn_w=None):
        xw = list(xn_w) if xn_w is not None else [xn_buf]
        for i in range(base, base + ntiles):
            v_op(lambda v, i=i: v.reduce_sum(stat[:, i:i + 1],
                                             src[:, i * D:(i + 1) * D],
                                             axis=AX.X),
                 reads=[src_buf], writes=[BF("stat")], tag=tag)
            s_op(lambda s, i=i: s.activation(junk, src[:, i * D:(i + 1) * D],
                                             AF.Square,
                                             accum_out=stat[:, 24 + i:25 + i]),
                 reads=[src_buf], writes=[BF("junk"), BF("stat")], tag=tag)
        nt, b0 = ntiles, base
        v_op(lambda v, nt=nt, b0=b0: v.tensor_scalar(stat[:, 48 + b0:48 + b0 + nt],
                                              stat[:, b0:b0 + nt], INV_D, None,
                                              op0=ALU.mult),
             reads=[BF("stat")], writes=[BF("stat")], tag=tag)
        v_op(lambda v, nt=nt, b0=b0: v.tensor_mul(stat[:, 72 + b0:72 + b0 + nt],
                                           stat[:, 48 + b0:48 + b0 + nt],
                                           stat[:, 48 + b0:48 + b0 + nt]),
             reads=[BF("stat")], writes=[BF("stat")], tag=tag)
        v_op(lambda v, nt=nt, b0=b0: v.scalar_tensor_tensor(
                stat[:, b0:b0 + nt], stat[:, 24 + b0:24 + b0 + nt], INV_D,
                stat[:, 72 + b0:72 + b0 + nt], op0=ALU.mult, op1=ALU.subtract),
             reads=[BF("stat")], writes=[BF("stat")], tag=tag)
        v_op(lambda v, nt=nt, b0=b0: v.tensor_scalar(stat[:, b0:b0 + nt],
                                              stat[:, b0:b0 + nt],
                                              EPS, None, op0=ALU.add),
             reads=[BF("stat")], writes=[BF("stat")], tag=tag)
        s_op(lambda s, nt=nt, b0=b0: s.activation(stat[:, 24 + b0:24 + b0 + nt],
                                           stat[:, b0:b0 + nt], AF.Ln),
             reads=[BF("stat")], writes=[BF("stat")], tag=tag)
        s_op(lambda s, nt=nt, b0=b0: s.activation(stat[:, 96 + b0:96 + b0 + nt],
                                           stat[:, 24 + b0:24 + b0 + nt], AF.Exp,
                                           scale=-0.5),
             reads=[BF("stat")], writes=[BF("stat")], tag=tag)
        for i in range(base, base + ntiles):
            v_op(lambda v, i=i: v.tensor_scalar(
                    xn_dst[:, i * D:(i + 1) * D], src[:, i * D:(i + 1) * D],
                    stat[:, 48 + i:49 + i], stat[:, 96 + i:97 + i],
                    op0=ALU.subtract, op1=ALU.mult),
                 reads=[src_buf, BF("stat")], writes=xw, tag=tag)

    def emit_transpose(xn_src, xn_buf, ntiles, dst, dst_buf, dst_tok, tag,
                       tbank=3, base=0):
        pbv = PB3 if tbank == 3 else PB2A
        for i in range(base, base + ntiles):
            off = (i % 2) * 1024
            bank = PBK[tbank][i % 2]
            for dc in range(8):
                t_op(lambda t, i=i, dc=dc, off=off, pbv=pbv: t.transpose(
                        pbv[:, off + dc * 128: off + dc * 128 + 128],
                        xn_src[:, i * D + dc * 128: i * D + dc * 128 + 128],
                        ident_b),
                     reads=[xn_buf, BF("ident_b")], writes=[bank], tag=tag)
            v_op(lambda v, i=i, off=off, pbv=pbv: v.tensor_copy(
                    dst[:].rearrange("p (c t) -> p c t", c=8)[
                        :, :, i * 128:(i + 1) * 128],
                    pbv[:, off:off + 1024].rearrange("p (c t) -> p c t", c=8)),
                 reads=[bank], writes=[dst_buf], tag=tag)

    QF, KF, VF = BF("q_fm"), BF("k_fm"), BF("v_fm")
    xin_v = x_in.rearrange("(c r p) d -> c p r d", p=128, r=4)
    xa_v = xa.rearrange("p (r d) -> p r d", d=D)
    for ch in range(16):
        xab = BF("xa")
        p.dma("sync", xa_v, xin_v[ch], writes=[xab], tag=f"A{ch}")
        emit_ln(xa, xab, 4, xnc, BF("xnc"), f"Aln{ch}")
        if DBG and ch == 0:
            dbg0 = nc.dram_tensor("dbg0", [128, 2176], F32,
                                  kind="ExternalOutput").ap()
            p.dma("sync", dbg0[:, 0:1024], xa[:, 0:1024],
                  reads=[xab], writes=[BF("dbg0")])
            p.dma("sync", dbg0[:, 1024:1144].bitcast(F32) if False else
                  dbg0[:, 1024:1144], stat[:, 0:120],
                  reads=[BF("stat")], writes=[BF("dbg0")])
            p.dma("sync", dbg0[:, 1152:2176].bitcast(BF16),
                  xnc[:, 0:2048], reads=[BF("xnc")], writes=[BF("dbg0")])
        emit_transpose(xnc, BF("xnc"), 4, xnTc, BF("xnTc"), 512, f"Atr{ch}", tbank=2)
        emit_prep(7)
        for dst, dbuf, colo, pi in ((q_fm, QF, 0, 3), (k_fm, KF, 256, 2)):
            for pair in range(2):
                bank = PBK[pi][pair]
                for dc in range(8):
                    t_op(lambda t, pair=pair, dc=dc, colo=colo, pi=pi: t.matmul(
                            P[pi][:, pair * 512:pair * 512 + 512],
                            wq_sb[:, dc * 768 + colo + pair * 128:
                                  dc * 768 + colo + pair * 128 + 128],
                            xnTc[:, dc * 512:(dc + 1) * 512],
                            start=(dc == 0), stop=(dc == 7)),
                         reads=[BF("wq_sb"), BF("xnTc")], writes=[bank],
                         tag=f"Aqk{ch}")
                s_op(lambda s, pair=pair, dst=dst, ch=ch, pi=pi: s.activation(
                        dst[:, pair * N + ch * 512: pair * N + ch * 512 + 512],
                        P[pi][:, pair * 512:pair * 512 + 512], AF.Copy),
                     reads=[bank], writes=[dbuf], tag=f"Aqk{ch}")
        for sub in range(4):
            bank = PBK[3][sub % 2]
            for dc in range(8):
                t_op(lambda t, sub=sub, dc=dc: t.matmul(
                        P[3][:, (sub % 2) * 512:(sub % 2) * 512 + 256],
                        xnTc[:, dc * 512 + sub * 128: dc * 512 + sub * 128 + 128],
                        wq_sb[:, dc * 768 + 512: dc * 768 + 768],
                        start=(dc == 0), stop=(dc == 7)),
                     reads=[BF("wq_sb"), BF("xnTc")], writes=[bank],
                     tag=f"Av{ch}")
            v_op(lambda v, sub=sub, ch=ch: v.tensor_copy(
                    v_fm[:].rearrange("p (s h e) -> p s h e", h=4, e=65)[
                        :, ch * 4 + sub, :, 0:64],
                    P[3][:, (sub % 2) * 512:(sub % 2) * 512 + 256].rearrange(
                        "p (h e) -> p h e", h=4)),
                 reads=[bank], writes=[VF], tag=f"Av{ch}")
    p.barrier("post-A")

    OBF = BF("obfT")
    PB2 = P[2][:].bitcast(BF16)
    PREP_B = [1, 2, 3, 4, 6, 7, 8, 9]
    load_cast(wp2_sb, BF("wp2_sb"), wp_slice, 2, D, "wp2")
    cinA_v = cinA.rearrange("(q p) d -> p q d", p=128)
    cinB_v = cinB.rearrange("(q p) d -> p q d", p=128)
    o_acc4_v = o_acc4[:].rearrange("p (q d) -> p q d", d=D)
    OACCB = [BF("oacc0"), BF("oacc1"), BF("oacc2"), BF("oacc3")]

    def emit_proj(tq):
        pb = PBK[tq % 2]
        for i in range(4):
            tt = tq * 4 + i
            for half in range(2):
                for k in range(2):
                    t_op(lambda t, tt=tt, half=half, k=k, tq=tq: t.matmul(
                            P[tq % 2][:, half * 512:half * 512 + 512],
                            obfT[:, k * N + tt * 128: k * N + tt * 128 + 128],
                            wp2_sb[:, k * 1024 + half * 512:
                                   k * 1024 + half * 512 + 512],
                            start=(k == 0), stop=(k == 1)),
                         reads=[OBF, BF("wp2_sb")], writes=[pb[half]],
                         tag="proj")
            v_op(lambda v, i=i, tq=tq: v.tensor_scalar(
                    o_acc[i][:, 0:1024],
                    P[tq % 2][:, 0:1024], ccb[:, 0:1], None, op0=ALU.mult),
                 reads=[pb[0], pb[1], BF("ccb")], writes=[OACCB[i]],
                 tag="proj")
        j = tq % 4
        chunk_v, cb = (cinA_v, "cinA") if j < 2 else (cinB_v, "cinB")
        qt0 = (tq // 4) * 8 + (j % 2) * 4
        p.dma("sync", chunk_v[:, qt0:qt0 + 4, :], o_acc4_v,
              reads=list(OACCB), writes=[BF(cb)], tag="proj")

    def emit_rs(src, dst, sb, db):
        def emit(g, src=src, dst=dst):
            return g.collective_compute(
                "ReduceScatter", ALU.add,
                replica_groups=[[0, 1, 2, 3], [4, 5, 6, 7]],
                ins=[src.opt()], outs=[dst.opt()])
        p.op("gpsimd", emit, reads=[BF(sb)], writes=[BF(db)],
             sem="cc", inc=1, tag="rs")

    for qw in range(8):
        nkc = 2 * (qw + 1)
        for h in range(UH):
            Ppv = P[2 + h % 2]
            pvb = [PBK[2 + h % 2][0], PBK[2 + h % 2][1]]
            nks = nkc * 4
            units = [(kc * 512 + ksub * 128, kc * 4 + ksub)
                     for kc in range(nkc) for ksub in range(4)]

            def emit_qk_exp(ks):
                ka, sub_abs = units[ks]
                Pst = P[ks % 2]
                stb = [PBK[ks % 2][0], PBK[ks % 2][1]]
                for half in range(2):
                    t_op(lambda t, h=h, ka=ka, qw=qw, half=half,
                         Pst=Pst: t.matmul(
                            Pst[:, half * 512:half * 512 + 512],
                            k_fm[(h % 2) * 64:(h % 2) * 64 + 64,
                                 (h // 2) * N + ka:
                                 (h // 2) * N + ka + 128],
                            q_fm[(h % 2) * 64:(h % 2) * 64 + 64,
                                 (h // 2) * N + qw * 1024 + half * 512:
                                 (h // 2) * N + qw * 1024
                                 + half * 512 + 512],
                            start=True, stop=True),
                         reads=[KF, QF], writes=[stb[half]], tag=f"B{qw}")
                pt = ptile[ks % 4]
                ptb = BF(f"ptile{ks % 4}")
                s_op(lambda s, pt=pt, Pst=Pst: s.activation(
                        pt, Pst[:, 0:1024], AF.Exp, scale=c28b[:, 0:1]),
                     reads=[stb[0], stb[1], BF("c28b")],
                     writes=[ptb], tag=f"B{qw}")
                if ka + 127 >= qw * 1024:
                    thr = float(ka - qw * 1024)
                    v_op(lambda v, pt=pt, thr=thr:
                         v.scalar_tensor_tensor(
                            pt, fmp_sb, thr, pt,
                            op0=ALU.is_ge, op1=ALU.mult),
                         reads=[ptb, BF("fmp")], writes=[ptb],
                         tag=f"B{qw}")

            def emit_pv(ks):
                ka, sub_abs = units[ks]
                pt = ptile[ks % 4]
                ptb = BF(f"ptile{ks % 4}")
                for half in range(2):
                    t_op(lambda t, h=h, sub_abs=sub_abs, half=half,
                         Ppv=Ppv, pt=pt, ks=ks, nks=nks: t.matmul(
                            Ppv[0:65, half * 512:half * 512 + 512],
                            v_fm[:, sub_abs * 260 + h * 65:
                                 sub_abs * 260 + h * 65 + 65],
                            pt[:, half * 512:half * 512 + 512],
                            start=(ks == 0), stop=(ks == nks - 1)),
                         reads=[VF, ptb], writes=[pvb[half]], tag=f"B{qw}")

            LAG = 2
            for step in range(nks + LAG):
                if step < nks:
                    emit_qk_exp(step)
                if step >= LAG:
                    emit_pv(step - LAG)
            ob = BF(f"oacc{h}")
            v_op(lambda v, Ppv=Ppv: v.reciprocal(
                    linv[0:1, 0:1024], Ppv[64:65, 0:1024]),
                 reads=pvb, writes=[BF("linv")], tag=f"Bf{qw}")
            v_op(lambda v, h=h, Ppv=Ppv: v.tensor_copy(
                    o_acc[h][0:65, :], Ppv[0:65, 0:1024]),
                 reads=pvb, writes=[ob], tag=f"Bf{qw}")
            for half in range(2):
                t_op(lambda t, half=half, Ppv=Ppv: t.matmul(
                        Ppv[0:64, half * 512:half * 512 + 512],
                        ones_f[0:1, 0:64],
                        linv[0:1, half * 512:half * 512 + 512],
                        start=True, stop=True),
                     reads=[BF("ones_f"), BF("linv")], writes=[pvb[half]],
                     tag=f"Bf{qw}")
                v_op(lambda v, h=h, qw=qw, half=half, Ppv=Ppv: v.tensor_mul(
                        obfT[(h % 2) * 64:(h % 2) * 64 + 64,
                             (h // 2) * N + qw * 1024 + half * 512:
                             (h // 2) * N + qw * 1024 + half * 512 + 512],
                        o_acc[h][0:64, half * 512:half * 512 + 512],
                        Ppv[0:64, half * 512:half * 512 + 512]),
                     reads=[ob, pvb[half]], writes=[OBF], tag=f"Bf{qw}")
            emit_prep(PREP_B[qw])
        emit_proj(2 * qw)
        emit_proj(2 * qw + 1)
        if RS_EARLY and qw == 6:
            emit_rs(cinA, coutA, "cinA", "coutA")

    assert prep_i[0] == len(prep_jobs), (prep_i[0], len(prep_jobs))
    if not RS_EARLY:
        emit_rs(cinA, coutA, "cinA", "coutA")
    emit_rs(cinB, coutB, "cinB", "coutB")
    if not PRE_B:
        p.barrier("post-rs")

    XO_A, XO_B = BF("x_ownA"), BF("x_ownB")
    XNb_A, XNb_B = BF("xn_tA"), BF("xn_tB")
    XNTb_A, XNTb_B = BF("xnTA"), BF("xnTB")
    GS_A, GS_B = BF("gateA"), BF("gateB")
    XO_H = (XO_A, XO_B)
    XNb_H = (XNb_A, XNb_B)
    XNTb_H = (XNTb_A, XNTb_B)
    GS_H = (GS_A, GS_B)
    stgA = av(OFF_K, 128, 4096, F32)
    stgB = av(OFF_K + 16384, 128, 4096, F32)
    stgA_v = stgA[:].rearrange("p (q d) -> p q d", d=D)
    stgB_v = stgB[:].rearrange("p (q d) -> p q d", d=D)
    coutA_v = coutA.rearrange("(q p) d -> p q d", p=128)
    coutB_v = coutB.rearrange("(q p) d -> p q d", p=128)

    xo_v = xown_in.rearrange("(q p) d -> p q d", p=128)
    p.dma("sync", x_own[:].rearrange("p (q d) -> p q d", d=D), xo_v,
          writes=[XO_A, XO_B, OBF, BF("ptile0"), BF("ptile1"), BF("ptile2"),
                  BF("ptile3")] + list(OACCB),
          tag="xown")
    for blk in range(2):
        p.dma("sync", stgA_v, coutA_v[:, blk * 4:(blk + 1) * 4, :],
              reads=[BF("coutA")], writes=[XNb_A, KF], tag="rsrd")
        v_op(lambda v, blk=blk: v.tensor_add(
                x_own[:, blk * 4096:(blk + 1) * 4096],
                x_own[:, blk * 4096:(blk + 1) * 4096], stgA),
             reads=[XNb_A, XO_A], writes=[XO_A], tag="rsrd")

    def emit_add_b():
        for blk in range(2, 4):
            p.dma("gpsimd", stgB_v, coutB_v[:, (blk - 2) * 4:(blk - 1) * 4, :],
                  reads=[BF("coutB")], writes=[XNb_B], tag="rsrd")
            v_op(lambda v, blk=blk: v.tensor_add(
                    x_own[:, blk * 4096:(blk + 1) * 4096],
                    x_own[:, blk * 4096:(blk + 1) * 4096], stgB),
                 reads=[XNb_B, XO_B], writes=[XO_B], tag="rsrd")

    if DBG:
        dbgf = nc.dram_tensor("dbgf", [128, 1024], F32, kind="ExternalOutput").ap()
        dbgb = nc.dram_tensor("dbgb", [128, 2048], BF16, kind="ExternalOutput").ap()
        p.barrier("dbg")
        p.dma("sync", dbgf[0:1, 0:24], scal, reads=[], writes=[BF("dbgf")])
        p.dma("sync", dbgf[:, 24:32], c28b, writes=[BF("dbgf")])
        p.dma("sync", dbgf[:, 32:288], mstat, writes=[BF("dbgf")])
        p.dma("sync", dbgf[:, 288:544], o_acc[0][:, 0:256], writes=[BF("dbgf")])
        p.dma("sync", dbgf[:, 544:560], lr, writes=[BF("dbgf")])
        p.dma("sync", dbgf[0:1, 560:816], linv[0:1, 0:256], writes=[BF("dbgf")])
        p.dma("sync", dbgb[:, 0:512], q_fm[:, 0:512], writes=[BF("dbgb")])
        p.dma("sync", dbgb[:, 512:1024], k_fm[:, 0:512], writes=[BF("dbgb")])
        p.dma("sync", dbgb[:, 1024:1280], negst, writes=[BF("dbgb")])
        p.dma("sync", dbgb[:, 1280:1792], v_fm[:, 0:512], writes=[BF("dbgb")])
        p.dma("sync", dbgb[:, 1792:2048], obfT[:, 0:256], writes=[BF("dbgb")])
        p.barrier("dbg2")

    def emit_store():
        p.dma("sync", y_out.rearrange("(q p) d -> p q d", p=128),
              x_own[:].rearrange("p (q d) -> p q d", d=D),
              reads=[XO_A, XO_B], writes=[BF("y")], tag="store")
        p.barrier("final")

    if stop_after == "attn":
        emit_add_b()
        emit_store()
        return nc, p, sems

    FPART = int(_os2.environ.get("FPART", "3")) if True else 3
    FSUB = _os2.environ.get("FSUB", "c")
    def emit_ffn(call, pre_b=None):
        tg = f"F{call}"
        for hf in range(2):
            if hf == 1 and pre_b is not None:
                pre_b()
            emit_ln(x_own, XO_H[hf], 8, xn_t, XNb_H[hf], tg, base=8 * hf,
                    xn_w=(XNb_H[hf], BF(f"w1sb{hf}")) if call == 1 else None)
            emit_transpose(xn_t, XNb_H[hf], 8, xnT, XNTb_H[hf], TOK, tg,
                           base=8 * hf)
            for tt in range(8 * hf, 8 * hf + 8):
                emit_gate(call, tt, tg)
            if call == 0 and hf == 0 and FPART >= 3:
                emit_expert_part(0, tg, first=True)
        if call == 1:
            p.barrier(tg + "-w")
        emit_experts(call, tg)
        emit_biascombo(call, tg)

    def emit_gate(call, tt, tg):
        XNTb = XNTb_H[tt // 8]
        gate_w_buf = GS_H[tt // 8]
        if True:
            gb = PBK[2][tt % 2]
            goff = (tt % 2) * 512
            for dc in range(8):
                t_op(lambda t, tt=tt, dc=dc, goff=goff: t.matmul(
                        P[2][:, goff:goff + 8],
                        xnT[:, dc * TOK + tt * 128: dc * TOK + tt * 128 + 128],
                        gwt_sb[:, dc * 8:(dc + 1) * 8],
                        start=(dc == 0), stop=False),
                     reads=[XNTb, BF("gwt_sb")], writes=[gb], tag=tg)
            t_op(lambda t, goff=goff: t.matmul(
                    P[2][:, goff:goff + 8], e0f, gate_b_sb,
                    start=False, stop=True),
                 reads=[BF("e0f"), BF("gate_b_sb")], writes=[gb], tag=tg)
            v_op(lambda v, goff=goff: v.tensor_reduce(
                    wsmall[:, 0:1], P[2][:, goff:goff + 8], axis=AX.X,
                    op=ALU.max, negate=True),
                 reads=[gb], writes=[BF("wg")], tag=tg)
            s_op(lambda s, goff=goff: s.activation(
                    gsc, P[2][:, goff:goff + 8], AF.Exp,
                    bias=wsmall[:, 0:1], accum_out=wsmall[:, 1:2]),
                 reads=[gb, BF("wg")], writes=[BF("gsc"), BF("wg")], tag=tg)
            v_op(lambda v: v.reciprocal(wsmall[:, 2:3], wsmall[:, 1:2]),
                 reads=[BF("wg")], writes=[BF("wg")], tag=tg)
            v_op(lambda v: v.tensor_scalar(gsc, gsc, wsmall[:, 2:3], None,
                                           op0=ALU.mult),
                 reads=[BF("wg"), BF("gsc")], writes=[BF("gsc")], tag=tg)
            v_op(lambda v: v.tensor_reduce(wsmall[:, 3:4], gsc, axis=AX.X,
                                           op=ALU.max),
                 reads=[BF("gsc")], writes=[BF("wg")], tag=tg)
            v_op(lambda v: v.tensor_single_scalar(wsmall[:, 8:16], gsc,
                                                  wsmall[:, 3:4], ALU.is_ge),
                 reads=[BF("gsc"), BF("wg")], writes=[BF("wg8")], tag=tg)
            v_op(lambda v: v.scalar_tensor_tensor(wsmall[:, 16:24],
                                                  wsmall[:, 8:16], -1.0e30,
                                                  gsc, op0=ALU.mult,
                                                  op1=ALU.add),
                 reads=[BF("wg8"), BF("gsc")], writes=[BF("wg16")], tag=tg)
            v_op(lambda v: v.tensor_reduce(wsmall[:, 4:5], wsmall[:, 16:24],
                                           axis=AX.X, op=ALU.max),
                 reads=[BF("wg16")], writes=[BF("wg")], tag=tg)
            v_op(lambda v: v.tensor_single_scalar(wsmall[:, 8:16], gsc,
                                                  wsmall[:, 4:5], ALU.is_ge),
                 reads=[BF("gsc"), BF("wg")], writes=[BF("wg8")], tag=tg)
            v_op(lambda v, tt=tt: v.tensor_mul(gate_sb[:, tt * 8:tt * 8 + 8],
                                               gsc, wsmall[:, 8:16]),
                 reads=[BF("gsc"), BF("wg8")], writes=[gate_w_buf], tag=tg)

    def emit_biascombo(call, tg):
        if FPART < 2:
            return
        gta_full = av(193280 + 8192, 9, TOK, BF16)
        g_op(lambda g: g.memset(gta_full, 1.0),
             writes=[BF("gtaf"), BF("wsbst0"), BF("wsbst1")], tag=tg)
        for tt in range(NTT):
            bank = PBK[3][tt % 2]
            t_op(lambda t, tt=tt: t.transpose(
                    P[3][0:8, (tt % 2) * 512:(tt % 2) * 512 + 128],
                    gate_sb[:, tt * 8:tt * 8 + 8], ident_f),
                 reads=[GS_H[tt // 8], BF("ident_f")], writes=[bank],
                 tag=tg)
            v_op(lambda v, tt=tt: v.tensor_copy(
                    gta_full[0:8, tt * 128:(tt + 1) * 128],
                    P[3][0:8, (tt % 2) * 512:(tt % 2) * 512 + 128]),
                 reads=[bank, BF("gtaf")], writes=[BF("gtaf")], tag=tg)
        for tt in range(NTT):
            bank = PBK[3][tt % 2]
            for half in range(2):
                t_op(lambda t, tt=tt, half=half: t.matmul(
                        P[3][:, (tt % 2) * 512:(tt % 2) * 512 + 512],
                        gta_full[0:9, tt * 128:(tt + 1) * 128],
                        b2bs2_sb[0:9, half * 512:half * 512 + 512],
                        start=True, stop=True),
                     reads=[BF("gtaf"), BF("b2bs2_sb")],
                     writes=[bank], tag=tg)
                v_op(lambda v, tt=tt, half=half: v.tensor_add(
                        x_own[:, tt * 1024 + half * 512:
                              tt * 1024 + half * 512 + 512],
                        x_own[:, tt * 1024 + half * 512:
                              tt * 1024 + half * 512 + 512],
                        P[3][:, (tt % 2) * 512:(tt % 2) * 512 + 512]),
                     reads=[bank, XO_H[tt // 8]], writes=[XO_H[tt // 8]],
                     tag=tg)

    def _elist_of():
        _elist = list(range(10 if FPART >= 3 else 0))
        if _os2.environ.get("FONLY") == "s":
            _elist = [0, 8, 9]
        elif _os2.environ.get("FONLY") == "e":
            _elist = list(range(8))
        return _elist

    def emit_w1_load(e, slot, tg, extra=()):
        src = scr_w1t[e] if e < 8 else scr_ws1t[:, (e - 8) * 8192:
                                                (e - 8) * 8192 + 8192]
        p.dma("sync", w1sb[slot][:, 0:8192], src,
              writes=[BF(f"w1sb{slot}")] + list(extra), tag=tg)

    def emit_w32_load(e, tg):
        if e < 8:
            p.dma("sync", w3sb[:, 0:8192], scr_w3t[e], writes=[BF("w3sb")],
                  tag=tg)
            p.dma("sync", w2sb[:, 0:8192], scr_w2t[e], writes=[BF("w2sb")],
                  tag=tg)
        else:
            p.dma("sync", w3sb[:, 0:8192],
                  scr_ws3t[:, (e - 8) * 8192:(e - 8) * 8192 + 8192],
                  writes=[BF("w3sb")], tag=tg)
            p.dma("sync", w2sb[:, 0:8192],
                  scr_ws2t[:, (e - 8) * 8192:(e - 8) * 8192 + 8192],
                  writes=[BF("w2sb")], tag=tg)

    def emit_expert_compute(e, ei, tg, tbs):
        if e < 8:
            b1col, b3col, bco = b1c_sb, b3c_sb, e * 8
        else:
            b1col, b3col, bco = bs1c_sb, bs3c_sb, (e - 8) * 8
        w1b = BF(f"w1sb{ei % 2}")
        w1cur = w1sb[ei % 2]
        for tb in tbs:
            XNTb = XNTb_H[tb // 2]
            for ic in range(8):
                bi = ic % 4
                pt_ = P[bi // 2]
                bank = PBK[bi // 2][bi % 2]
                for dc in range(8):
                    t_op(lambda t, ic=ic, dc=dc, tb=tb, bi=bi, pt_=pt_,
                         w1cur=w1cur: t.matmul(
                            pt_[:, (bi % 2) * 512:(bi % 2) * 512 + 512],
                            w1cur[:, dc * 1024 + ic * 128:
                                  dc * 1024 + ic * 128 + 128],
                            xnT[:, dc * TOK + tb * 512:
                                dc * TOK + tb * 512 + 512],
                            start=(dc == 0), stop=(dc == 7)),
                         reads=[w1b, XNTb], writes=[bank], tag=tg)
                s_op(lambda s, ic=ic, bi=bi, pt_=pt_, b1col=b1col,
                     bco=bco: s.activation(
                        h1sb[:, ic * 512:(ic + 1) * 512],
                        pt_[:, (bi % 2) * 512:(bi % 2) * 512 + 512],
                        AF.Silu, bias=b1col[:, bco + ic:bco + ic + 1]),
                     reads=[bank, BF("b1c_sb" if e < 8 else "bs1c_sb")],
                     writes=[BF("h1sb")], tag=tg)
                bank3 = PBK[2 + bi // 2][bi % 2]
                pt3 = P[2 + bi // 2]
                for dc in range(8 if FSUB >= "b" else 0):
                    t_op(lambda t, ic=ic, dc=dc, tb=tb, bi=bi,
                         pt3=pt3: t.matmul(
                            pt3[:, (bi % 2) * 512:(bi % 2) * 512 + 512],
                            w3sb[:, dc * 1024 + ic * 128:
                                 dc * 1024 + ic * 128 + 128],
                            xnT[:, dc * TOK + tb * 512:
                                dc * TOK + tb * 512 + 512],
                            start=(dc == 0), stop=(dc == 7)),
                         reads=[BF("w3sb"), XNTb], writes=[bank3], tag=tg)
                if FSUB >= "b":
                    v_op(lambda v, ic=ic, bi=bi, pt3=pt3, b3col=b3col,
                         bco=bco: v.scalar_tensor_tensor(
                            h1sb[:, ic * 512:(ic + 1) * 512],
                            pt3[:, (bi % 2) * 512:(bi % 2) * 512 + 512],
                            b3col[:, bco + ic:bco + ic + 1],
                            h1sb[:, ic * 512:(ic + 1) * 512],
                            op0=ALU.add, op1=ALU.mult),
                         reads=[bank3, BF("h1sb"),
                                BF("b3c_sb" if e < 8 else "bs3c_sb")],
                         writes=[BF("h1sb")], tag=tg)
            for ts in range(4 if FSUB >= "c" else 0):
                tt = tb * 4 + ts
                Pw = P[2 + ts % 2]
                pwb = [PBK[2 + ts % 2][0], PBK[2 + ts % 2][1]]
                for ic in range(8):
                    for half in range(2):
                        t_op(lambda t, ic=ic, ts=ts, half=half,
                             Pw=Pw: t.matmul(
                                Pw[:, half * 512:half * 512 + 512],
                                h1sb[:, ic * 512 + ts * 128:
                                     ic * 512 + ts * 128 + 128],
                                w2sb[:, ic * 1024 + half * 512:
                                     ic * 1024 + half * 512 + 512],
                                start=(ic == 0), stop=(ic == 7)),
                             reads=[BF("h1sb"), BF("w2sb")],
                             writes=[pwb[half]], tag=tg)
                s_op(lambda s, Pw=Pw: s.activation(junk, Pw[:, 0:1024],
                                                   AF.Copy),
                     reads=pwb, writes=[BF("junk")], tag=tg)
                if e < 8:
                    v_op(lambda v, tt=tt, e=e: v.scalar_tensor_tensor(
                            x_own[:, tt * 1024:(tt + 1) * 1024],
                            junk,
                            gate_sb[:, tt * 8 + e:tt * 8 + e + 1],
                            x_own[:, tt * 1024:(tt + 1) * 1024],
                            op0=ALU.mult, op1=ALU.add),
                         reads=[BF("junk"), GS_H[tt // 8], XO_H[tt // 8]],
                         writes=[XO_H[tt // 8]], tag=tg)
                else:
                    v_op(lambda v, tt=tt: v.tensor_add(
                            x_own[:, tt * 1024:(tt + 1) * 1024],
                            x_own[:, tt * 1024:(tt + 1) * 1024],
                            junk),
                         reads=[BF("junk"), XO_H[tt // 8]],
                         writes=[XO_H[tt // 8]], tag=tg)

    def emit_expert_part(e, tg, first):
        emit_w1_load(e, 0, tg, extra=(XNb_A,))
        emit_w32_load(e, tg)
        emit_expert_compute(e, 0, tg, [0, 1])

    def emit_experts(call, tg):
        _elist = _elist_of()
        if not _elist:
            return
        if call == 0:
            if len(_elist) > 1:
                emit_w1_load(_elist[1], 1, tg, extra=(XNb_B,))
            emit_expert_compute(_elist[0], 0, tg, [2, 3])
            rest = list(enumerate(_elist))[1:]
        else:
            emit_w1_load(_elist[0], 0, tg)
            rest = list(enumerate(_elist))
        for ei, e in rest:
            if ei + 1 < len(_elist) and not (call == 0 and ei == 0):
                emit_w1_load(_elist[ei + 1], (ei + 1) % 2, tg)
            emit_w32_load(e, tg)
            emit_expert_compute(e, ei, tg, [0, 1, 2, 3])

    if PRE_B:
        emit_ffn(0, pre_b=emit_add_b)
    else:
        emit_add_b()
        emit_ffn(0)
    if stop_after == "ffn1":
        emit_store()
        return nc, p, sems

    import os as _os
    ABPART = int(_os.environ.get("ABPART", "4"))
    tg = "ab"
    for hf in range(2):
        emit_ln(x_own, XO_H[hf], 8, xn3, XNb_H[hf], tg, base=8 * hf)
        emit_transpose(xn3, XNb_H[hf], 8, xnT, XNTb_H[hf], TOK, tg,
                       base=8 * hf)
    p.barrier("ab-w")
    p.dma("sync", abw1_sb[:, 0:8192], scr_abw1, writes=[BF("abw1_sb")], tag=tg)
    for tt in range(NTT if (ABPART >= 2) else 0):
        XNTb = XNTb_H[tt // 8]
        XO = XO_H[tt // 8]
        XNB = XNb_H[tt // 8]
        dbank = [PBK[0][0], PBK[0][1]]
        for dc in range(8):
            for half in range(2):
                t_op(lambda t, tt=tt, dc=dc, half=half: t.matmul(
                        P[0][:, half * 512:half * 512 + 512],
                        xnT[:, dc * TOK + tt * 128: dc * TOK + tt * 128 + 128],
                        abw1_sb[:, dc * 1024 + half * 512:
                                dc * 1024 + half * 512 + 512],
                        start=(dc == 0), stop=False),
                     reads=[XNTb, BF("abw1_sb")], writes=[dbank[half]], tag=tg)
            t_op(lambda t, tt=tt, dc=dc: t.matmul(
                    P[2][:, 0:16],
                    xnT[:, dc * TOK + tt * 128: dc * TOK + tt * 128 + 128],
                    abw2t_sb[:, dc * 16:(dc + 1) * 16],
                    start=(dc == 0), stop=False),
                 reads=[XNTb, BF("abw2t_sb")], writes=[PBK[2][0]], tag=tg)
            t_op(lambda t, tt=tt, dc=dc: t.matmul(
                    P[2][:, 512:528],
                    xnT[:, dc * TOK + tt * 128: dc * TOK + tt * 128 + 128],
                    abw3t_sb[:, dc * 16:(dc + 1) * 16],
                    start=(dc == 0), stop=False),
                 reads=[XNTb, BF("abw3t_sb")], writes=[PBK[2][1]], tag=tg)
        for half in range(2):
            t_op(lambda t, half=half: t.matmul(
                    P[0][:, half * 512:half * 512 + 512], e0f,
                    ab_b1_sb[:, half * 512:half * 512 + 512],
                    start=False, stop=True),
                 reads=[BF("e0f"), BF("ab_b1_sb")], writes=[dbank[half]],
                 tag=tg)
        t_op(lambda t: t.matmul(P[2][:, 0:16], e0f, ab_b2_sb,
                                start=False, stop=True),
             reads=[BF("e0f"), BF("ab_b2_sb")], writes=[PBK[2][0]], tag=tg)
        t_op(lambda t: t.matmul(P[2][:, 512:528], e0f, ab_b3_sb,
                                start=False, stop=True),
             reads=[BF("e0f"), BF("ab_b3_sb")], writes=[PBK[2][1]], tag=tg)
        if ABPART == 2:
            v_op(lambda v, tt=tt: v.tensor_add(
                    x_own[:, tt * 1024:(tt + 1) * 1024],
                    x_own[:, tt * 1024:(tt + 1) * 1024], P[0][:, 0:1024]),
                 reads=[dbank[0], dbank[1], XO], writes=[XO], tag=tg)
            v_op(lambda v: v.tensor_copy(bsb, P[2][:, 0:16]),
                 reads=[PBK[2][0]], writes=[BF("bsb")], tag=tg)
            v_op(lambda v: v.tensor_copy(wsmall[:, 24:40], P[2][:, 512:528]),
                 reads=[PBK[2][1]], writes=[BF("w24")], tag=tg)
            continue
        s_op(lambda s: s.activation(dsc1, P[0][:, 0:1024], AF.Exp),
             reads=dbank, writes=[BF("dsc1")], tag=tg)
        s_op(lambda s: s.activation(dsc2, P[0][:, 0:1024], AF.Exp, scale=-1.0),
             reads=dbank, writes=[BF("dsc2")], tag=tg)
        v_op(lambda v: v.tensor_tensor(dsc2, dsc1, dsc2, ALU.min),
             reads=[BF("dsc1"), BF("dsc2")], writes=[BF("dsc2")], tag=tg)
        s_op(lambda s: s.activation(dsc1, dsc2, AF.Ln, bias=1.0),
             reads=[BF("dsc2")], writes=[BF("dsc1")], tag=tg)
        v_op(lambda v: v.tensor_scalar_max(delta_v, P[0][:, 0:1024], 0.0),
             reads=dbank, writes=[BF("delta_v")], tag=tg)
        v_op(lambda v: v.tensor_add(delta_v, delta_v, dsc1),
             reads=[BF("dsc1"), BF("delta_v")], writes=[BF("delta_v")], tag=tg)
        v_op(lambda v: v.tensor_copy(bsb, P[2][:, 0:16]),
             reads=[PBK[2][0]], writes=[BF("bsb")], tag=tg)
        v_op(lambda v: v.tensor_mul(wsmall[:, 24:40], P[2][:, 512:528], bsb),
             reads=[PBK[2][1], BF("bsb")], writes=[BF("w24")], tag=tg)
        v_op(lambda v, tt=tt: v.reduce_sum(bcv[:, tt:tt + 1],
                                           wsmall[:, 24:40], axis=AX.X),
             reads=[BF("w24")], writes=[BF("bcv")], tag=tg)
        v_op(lambda v, tt=tt: v.tensor_mul(
                dsc2, xn3[:, tt * 1024:(tt + 1) * 1024], delta_v),
             reads=[XNB, BF("delta_v")], writes=[BF("dsc2")], tag=tg)
        if ABPART == 3 or ABPART >= 20:
            v_op(lambda v, tt=tt: v.tensor_add(
                    x_own[:, tt * 1024:(tt + 1) * 1024],
                    x_own[:, tt * 1024:(tt + 1) * 1024], delta_v),
                 reads=[BF("delta_v"), XO], writes=[XO], tag=tg)
            continue
        v_op(lambda v, tt=tt: v.scalar_tensor_tensor(
                x_own[:, tt * 1024:(tt + 1) * 1024], dsc2,
                bcv[:, tt:tt + 1], x_own[:, tt * 1024:(tt + 1) * 1024],
                op0=ALU.mult, op1=ALU.add),
             reads=[BF("dsc2"), BF("bcv"), XO], writes=[XO], tag=tg)

    if stop_after == "attnb":
        emit_store()
        return nc, p, sems

    emit_ffn(1)
    emit_store()
    return nc, p, sems


def assemble(nc, p, sems):
    with nc.Block() as blk:
        def make(section):
            def body(eng):
                for waits, emit, sem, inc, tag in p.ops[section]:
                    for sname, val in waits.items():
                        eng.wait_ge(sems[sname], val)
                    inst = emit(eng)
                    if inc:
                        inst.then_inc(sems[sem], inc)
            return body
        blk.tensor(make("tensor"))
        blk.vector(make("vector"))
        blk.scalar(make("scalar"))
        blk.gpsimd(make("gpsimd"))
        blk.sync(make("sync"))
    return nc


_BUILD_CACHE = {}


def _get_nc(stop_after):
    if stop_after not in _BUILD_CACHE:
        nc, p, sems = build_nc(stop_after)
        _BUILD_CACHE[stop_after] = assemble(nc, p, sems)
    return _BUILD_CACHE[stop_after]


def kernel(**inputs):
    f32 = np.float32
    x = np.asarray(inputs["L0_attn_a_batch"], f32)
    Wqkv = np.asarray(inputs["L0_attn_a_W_qkv"], f32)
    u_qkv = np.asarray(inputs["L0_attn_a_u_qkv"], f32)
    sp_qkv = np.asarray(inputs["L0_attn_a_sigma_qkv"], f32)
    Wproj = np.asarray(inputs["L0_attn_a_W_proj"], f32)
    u_proj = np.asarray(inputs["L0_attn_a_u_proj"], f32)
    sp_proj = np.asarray(inputs["L0_attn_a_sigma_proj"], f32)
    abW1 = np.asarray(inputs["L0_attn_b_W1"], f32)
    abb1 = np.asarray(inputs["L0_attn_b_b1"], f32)
    abW2 = np.asarray(inputs["L0_attn_b_W2"], f32)
    abb2 = np.asarray(inputs["L0_attn_b_b2"], f32)
    abW3 = np.asarray(inputs["L0_attn_b_W3"], f32)
    abb3 = np.asarray(inputs["L0_attn_b_b3"], f32)
    gate_w = np.asarray(inputs["L0_ffn_gate_w"], f32)
    gate_b = np.asarray(inputs["L0_ffn_gate_b"], f32)
    w1 = np.asarray(inputs["L0_ffn_w1"], f32)
    b1 = np.asarray(inputs["L0_ffn_b1"], f32)
    w3 = np.asarray(inputs["L0_ffn_w3"], f32)
    b3 = np.asarray(inputs["L0_ffn_b3"], f32)
    w2 = np.asarray(inputs["L0_ffn_w2"], f32)
    b2 = np.asarray(inputs["L0_ffn_b2"], f32)
    ws1 = np.asarray(inputs["L0_ffn_ws1"], f32)
    bs1 = np.asarray(inputs["L0_ffn_bs1"], f32)
    ws3 = np.asarray(inputs["L0_ffn_ws3"], f32)
    bs3 = np.asarray(inputs["L0_ffn_bs3"], f32)
    ws2 = np.asarray(inputs["L0_ffn_ws2"], f32)
    bs2 = np.asarray(inputs["L0_ffn_bs2"], f32)

    _ = inputs["L0_attn_b_A"]
    C = np.ascontiguousarray
    fmp = (np.arange(1024, dtype=np.float32)[None, :]
           - np.arange(128, dtype=np.float32)[:, None])
    shared = {
        "fmp": C(fmp),
        "wqkv": C(Wqkv), "wqkvT": C(Wqkv.T),
        "u_qkv": C(u_qkv.reshape(1, -1)), "sp_qkv": C(sp_qkv.reshape(1, 1)),
        "wproj": C(Wproj), "wprojT": C(Wproj.T),
        "u_proj": C(u_proj.reshape(1, -1)), "sp_proj": C(sp_proj.reshape(1, 1)),
        "w1t": C(w1.transpose(0, 2, 1)), "w3t": C(w3.transpose(0, 2, 1)),
        "w2t": C(w2.transpose(0, 2, 1)),
        "ws1t": C(ws1.T), "ws3t": C(ws3.T), "ws2t": C(ws2.T),
        "gwt": C(gate_w.T), "gate_b": C(gate_b.reshape(1, -1)),
        "b1c": C(b1.reshape(E, 8, 128).transpose(2, 0, 1).reshape(128, E * 8)),
        "b3c": C(b3.reshape(E, 8, 128).transpose(2, 0, 1).reshape(128, E * 8)),
        "bs1c": C(bs1.reshape(16, 128).T),
        "bs3c": C(bs3.reshape(16, 128).T),
        "b2bs2": C(np.concatenate([b2, bs2.reshape(1, -1)], axis=0)),
        "ab_w1t": C(abW1.T), "ab_w2t": C(abW2.T), "ab_w3t": C(abW3.T),
        "ab_b1": C(abb1.reshape(1, -1)), "ab_b2": C(abb2.reshape(1, -1)),
        "ab_b3": C(abb3.reshape(1, -1)),
    }
    in_maps = []
    for c in range(8):
        b, g = c // 4, c % 4
        hs = 4 * g
        qcols = slice(hs * 64, hs * 64 + 256)
        m = dict(shared)
        m["x"] = C(x[b])
        m["xown"] = C(x[b, g * TOK:(g + 1) * TOK])
        m["wq_slice"] = C(np.concatenate(
            [Wqkv[:, qcols], Wqkv[:, 1024 + hs * 64:1024 + hs * 64 + 256],
             Wqkv[:, 2048 + hs * 64:2048 + hs * 64 + 256]], axis=1))
        m["wp_slice"] = C(Wproj[hs * 64:hs * 64 + 256, :])
        in_maps.append(m)

    nc = _get_nc(STOP_AFTER)
    res = run_bass_kernel_spmd(nc, in_maps, core_ids=list(range(8)))
    y = np.empty((2, N, D), np.float32)
    for c in range(8):
        b, g = c // 4, c % 4
        y[b, g * TOK:(g + 1) * TOK] = res.results[c]["y"]
    return y


# version 09
# speedup vs baseline: 1.0852x; 1.0852x over previous
import numpy as np
import os as _os2
import ml_dtypes
import concourse.bass as bass
import concourse.mybir as mybir
from concourse.bass_utils import run_bass_kernel_spmd

F32 = mybir.dt.float32
BF16 = mybir.dt.bfloat16
U8 = mybir.dt.uint8
AF = mybir.ActivationFunctionType
ALU = mybir.AluOpType
AX = mybir.AxisListType

N, D, TOK, NTT, UH, E = 8192, 1024, 2048, 16, 4, 8
EPS = 1e-5
INV_D = 1.0 / D

STOP_AFTER = None
RS_EARLY = True
PB_MODE = 1
SAME_ENGINE_WAITS = True
DBG = False


class Buf:
    __slots__ = ("name", "writer", "readers")

    def __init__(self, name):
        self.name = name
        self.writer = None
        self.readers = {}


class Prog:
    COMPUTE_SEM = {"tensor": "pe", "vector": "dv", "scalar": "ac", "gpsimd": "gp"}
    DMA_SEM = {"sync": "qs", "vector": "qv", "scalar": "qc", "gpsimd": "qg"}
    SECTIONS = ("tensor", "vector", "scalar", "gpsimd", "sync")

    def __init__(self):
        self.ops = {s: [] for s in self.SECTIONS}
        self.counts = {s: 0 for s in
                       ("pe", "dv", "ac", "gp", "qs", "qv", "qc", "qg", "cc")}
        self.synced = {}

    def _dep(self, waits, section, sem, val, psection):
        if val <= 0:
            return
        if psection == section and self.COMPUTE_SEM.get(section) == sem:
            if section == "tensor" or not SAME_ENGINE_WAITS:
                return
        if self.synced.get((section, sem), 0) >= val:
            return
        if waits.get(sem, 0) < val:
            waits[sem] = val

    def op(self, section, emit, reads=(), writes=(), sem=None, inc=None, tag=""):
        if sem is None:
            sem = self.COMPUTE_SEM[section]
        if inc is None:
            inc = 16 if sem in ("qs", "qv", "qc", "qg") else 1
        waits = {}
        for b in reads:
            if b.writer is not None:
                self._dep(waits, section, b.writer[0], b.writer[1], b.writer[2])
        for b in writes:
            if b.writer is not None:
                self._dep(waits, section, b.writer[0], b.writer[1], b.writer[2])
            for rsem, (rval, rsec) in b.readers.items():
                self._dep(waits, section, rsem, rval, rsec)
        for sname, val in waits.items():
            key = (section, sname)
            self.synced[key] = max(self.synced.get(key, 0), val)
        self.counts[sem] += inc
        newval = self.counts[sem]
        self.ops[section].append((waits, emit, sem, inc, tag))
        for b in reads:
            old = b.readers.get(sem)
            if old is None or old[0] < newval:
                b.readers[sem] = (newval, section)
        for b in writes:
            b.writer = (sem, newval, section)
            b.readers = {}
        return newval

    def dma(self, section, out_ap, in_ap, reads=(), writes=(), tag="",
            slow=False):
        sem = self.DMA_SEM[section]

        def emit(eng, o=out_ap, i=in_ap, slow=slow):
            return eng.dma_start(out=o, in_=i,
                                 allow_slow_non_contiguous=slow)
        return self.op(section, emit, reads=reads, writes=writes, sem=sem,
                       inc=16, tag=tag)

    def barrier(self, tag="barrier"):
        snap = dict(self.counts)
        for sec in self.SECTIONS:
            waits = {}
            for sname, val in snap.items():
                if val > 0 and self.synced.get((sec, sname), 0) < val:
                    waits[sname] = val
            for sname, val in waits.items():
                self.synced[(sec, sname)] = val
            if sec == "sync":
                sem, inc = "qs", 0
            else:
                sem, inc = self.COMPUTE_SEM[sec], 1
            if inc:
                self.counts[sem] += inc

            def emit(eng):
                return eng.nop()
            self.ops[sec].append((waits, emit, sem, inc, tag))


def build_nc(stop_after=None):
    nc = bass.Bass("TRN2", target_bir_lowering=False, debug=False)
    p = Prog()

    def ein(name, shape):
        return nc.dram_tensor(name, list(shape), F32, kind="ExternalInput").ap()

    x_in = ein("x", (N, D))
    xown_in = ein("xown", (TOK, D))
    wq_slice = ein("wq_slice", (D, 768))
    wp_slice = ein("wp_slice", (256, D))
    wqkv = ein("wqkv", (D, 3 * D))
    wqkvT = ein("wqkvT", (3 * D, D))
    u_qkv = ein("u_qkv", (1, 3 * D))
    sp_qkv = ein("sp_qkv", (1, 1))
    wproj = ein("wproj", (D, D))
    wprojT = ein("wprojT", (D, D))
    u_proj = ein("u_proj", (1, D))
    sp_proj = ein("sp_proj", (1, 1))
    w1t_in = ein("w1t", (E, D, 1024))
    w3t_in = ein("w3t", (E, D, 1024))
    w2t_in = ein("w2t", (E, 1024, D))
    ws1t_in = ein("ws1t", (D, 2048))
    ws3t_in = ein("ws3t", (D, 2048))
    ws2t_in = ein("ws2t", (2048, D))
    gwt_in = ein("gwt", (D, E))
    gate_b_in = ein("gate_b", (1, E))
    b1c_in = ein("b1c", (128, E * 8))
    b3c_in = ein("b3c", (128, E * 8))
    bs1c_in = ein("bs1c", (128, 16))
    bs3c_in = ein("bs3c", (128, 16))
    b2bs2_in = ein("b2bs2", (9, D))
    ab_w1t_in = ein("ab_w1t", (D, D))
    ab_w2t_in = ein("ab_w2t", (D, 16))
    ab_w3t_in = ein("ab_w3t", (D, 16))
    ab_b1_in = ein("ab_b1", (1, D))
    ab_b2_in = ein("ab_b2", (1, 16))
    ab_b3_in = ein("ab_b3", (1, 16))
    fmp_in = ein("fmp", (128, 1024))

    y_out = nc.dram_tensor("y", [TOK, D], F32, kind="ExternalOutput").ap()

    cinA = nc.dram_tensor("cinA", [N // 2, D], F32).ap()
    cinB = nc.dram_tensor("cinB", [N // 2, D], F32).ap()
    coutA = nc.dram_tensor("coutA", [TOK // 2, D], F32).ap()
    coutB = nc.dram_tensor("coutB", [TOK // 2, D], F32).ap()
    scr_w1t = nc.dram_tensor("scr_w1t", [E, 128, 8 * 1024], BF16).ap()
    scr_w3t = nc.dram_tensor("scr_w3t", [E, 128, 8 * 1024], BF16).ap()
    scr_w2t = nc.dram_tensor("scr_w2t", [E, 128, 8 * D], BF16).ap()
    scr_ws1t = nc.dram_tensor("scr_ws1t", [128, 8 * 2048], BF16).ap()
    scr_ws3t = nc.dram_tensor("scr_ws3t", [128, 8 * 2048], BF16).ap()
    scr_ws2t = nc.dram_tensor("scr_ws2t", [128, 16 * D], BF16).ap()
    scr_abw1 = nc.dram_tensor("scr_abw1", [128, 8 * D], BF16).ap()

    ARENA = 206848
    arena = nc.alloc_sbuf_tensor("arena", [128, ARENA], U8)
    arena_f = arena[:].bitcast(F32)
    arena_b = arena[:].bitcast(BF16)

    def av(off, parts, cols, dtype):
        assert off % 32 == 0, off
        if dtype == F32:
            assert off + cols * 4 <= ARENA, (off, cols)
            return arena_f[0:parts, off // 4: off // 4 + cols]
        assert off + cols * 2 <= ARENA, (off, cols)
        return arena_b[0:parts, off // 2: off // 2 + cols]

    ones_f = av(0, 128, 128, F32)
    ones_b = av(512, 128, 128, BF16)
    ident_f = av(768, 128, 128, F32)
    ident_b = av(1280, 128, 128, BF16)
    c28b = av(1536, 128, 8, F32)
    ccb = av(1568, 128, 8, F32)
    scal = av(1600, 1, 24, F32)
    stat = av(1696, 128, 120, F32)
    gwt_sb = av(2176, 128, 8 * E, BF16)
    gate_b_sb = av(2304, 128, E, F32)
    b1c_sb = av(2336, 128, 64, F32)
    b3c_sb = av(2592, 128, 64, F32)
    bs1c_sb = av(2848, 128, 16, F32)
    bs3c_sb = av(2912, 128, 16, F32)
    ab_b1_sb = av(2976, 128, D, F32)
    ab_b2_sb = av(7072, 128, 16, F32)
    ab_b3_sb = av(7136, 128, 16, F32)
    b2bs2_sb = av(7200, 9, D, BF16)
    negst = av(11296, 128, 256, BF16)
    negm_w = av(11808, 128, 2048, BF16)
    lr = av(15904, 128, 16, F32)
    linv = av(15968, 1, 1024, F32)
    abw2t_sb = av(20064, 128, 8 * 16, BF16)
    abw3t_sb = av(20320, 128, 8 * 16, BF16)
    wsmall = av(20576, 128, 40, F32)
    junk = av(20736, 128, 1024, F32)
    mstat = av(24832, 128, 256, F32)
    gta_t = av(25856, 9, 128, F32)
    gate_sb = av(26368, 128, NTT * E, F32)
    gsc = av(26880, 128, 8, F32)
    bsb = av(26912, 128, 16, F32)
    bcv = av(26976, 128, 16, F32)
    e0b = av(27040, 128, 128, BF16)
    e0f = av(27296, 128, 128, F32)

    OFF_W = 28928
    x_own = av(OFF_W, 128, NTT * D, F32)
    xa = av(OFF_W, 128, 4 * D, F32)
    xnc = av(OFF_W + 16384, 128, 4 * D, BF16)
    xnTc = av(OFF_W + 24576, 128, 4 * D, BF16)
    wp2_sb = av(OFF_W + 32768, 128, 2 * D, BF16)
    obfT = av(OFF_W, 128, 2 * N, BF16)
    ptile = [av(OFF_W + 36864 + i * 2048, 128, 1024, BF16) for i in range(4)]
    o_acc = [av(OFF_W + 45056 + i * 4096, 128, 1024, F32) for i in range(4)]
    o_acc4 = av(OFF_W + 45056, 128, 4096, F32)
    ystage = av(94464, 128, 4 * D, F32)
    wq_sb = av(OFF_W + 36864, 128, 8 * 768, BF16)
    fmp_sb = av(OFF_W + 61440, 128, 1024, F32)

    OFF_Q = 94464
    q_fm = av(OFF_Q, 128, 2 * N, BF16)
    sgst = [av(OFF_Q, 128, 3 * D, F32), av(OFF_Q + 12288, 128, 3 * D, F32)]
    srow = av(OFF_Q + 24576, 1, 3 * D, F32)
    arow = av(OFF_Q + 28672, 1, 3 * D, F32)
    OFF_K = 127232
    k_fm = av(OFF_K, 128, 2 * N, BF16)
    srow = av(OFF_K, 1, 3 * D, F32)
    arow = av(OFF_K + 12288, 1, 3 * D, F32)
    xn_t = av(OFF_K, 128, NTT * D, BF16)
    xn3 = xn_t
    w1sb = [av(OFF_K, 128, 8 * 1024, BF16),
            av(OFF_K + 16384, 128, 8 * 1024, BF16)]
    xnT = av(OFF_Q, 128, 8 * TOK, BF16)

    OFF_V = 160000
    v_fm = av(OFF_V, 128, 64 * 260, BF16)
    w3sb = av(OFF_V, 128, 8 * 1024, BF16)
    w2sb = av(OFF_V + 16384, 128, 8 * D, BF16)
    abw1_sb = av(OFF_V, 128, 8 * D, BF16)
    delta_v = av(OFF_V + 16384, 128, D, F32)
    dsc1 = av(OFF_V + 20480, 128, D, F32)
    dsc2 = av(OFF_V + 24576, 128, D, F32)

    OFF_T = 193280
    wst = [av(OFF_T, 128, 1024, F32), av(OFF_T + 4096, 128, 1024, F32)]
    wsbst = [av(OFF_T + 8192, 128, 1024, BF16),
             av(OFF_T + 10240, 128, 1024, BF16)]
    h1sb = av(OFF_T, 128, 8 * 512, BF16)

    P = [nc.alloc_psum_tensor(f"P{i}", [128, 1024], F32) for i in range(4)]
    PB3 = P[3][:].bitcast(BF16)
    PB2A = P[2][:].bitcast(BF16)

    sems = {n: nc.alloc_semaphore(n)
            for n in ("pe", "dv", "ac", "gp", "qs", "qv", "qc", "qg", "cc")}

    bufs = {}

    def BF(name):
        if name not in bufs:
            bufs[name] = Buf(name)
        return bufs[name]

    PBK = [(BF(f"P{i}a"), BF(f"P{i}b")) for i in range(4)]

    def t_op(emit, reads=(), writes=(), tag=""):
        return p.op("tensor", emit, reads, writes, tag=tag)

    def v_op(emit, reads=(), writes=(), tag=""):
        return p.op("vector", emit, reads, writes, tag=tag)

    def s_op(emit, reads=(), writes=(), tag=""):
        return p.op("scalar", emit, reads, writes, tag=tag)

    def g_op(emit, reads=(), writes=(), tag=""):
        return p.op("gpsimd", emit, reads, writes, tag=tag)

    g_op(lambda g: g.memset(ones_f, 1.0), writes=[BF("ones_f")])
    g_op(lambda g: g.memset(ones_b, 1.0), writes=[BF("ones_b")])
    g_op(lambda g: g.memset(gta_t, 1.0), writes=[BF("gta_ones")])
    g_op(lambda g: g.memset(v_fm, 1.0), writes=[BF("v_fm")])
    def emit_e0(t, buf):
        def e(g, t=t):
            g.memset(t, 0.0)
            g.memset(t[0:1, :], 1.0)
            return g.memset(t[64:65, :], 1.0)
        g_op(e, writes=[buf])
    emit_e0(e0b, BF("e0b"))
    emit_e0(e0f, BF("e0f"))
    g_op(lambda g: g.memset(negm_w, 0.0), writes=[BF("negm_w")])
    for t_, b_ in ((gate_b_sb, "gate_b_sb"), (ab_b1_sb, "ab_b1_sb"),
                   (ab_b2_sb, "ab_b2_sb"), (ab_b3_sb, "ab_b3_sb")):
        g_op(lambda g, t_=t_: g.memset(t_, 0.0), writes=[BF(b_)])

    def mk_ident(t, buf):
        def emit(g, t=t):
            g.memset(t, 0.0)
            return g.affine_select(out=t, in_=t, compare_op=ALU.not_equal,
                                   fill=1.0, base=0, pattern=[[-1, 128]],
                                   channel_multiplier=1)
        g_op(emit, writes=[buf])
    mk_ident(ident_b, BF("ident_b"))
    mk_ident(ident_f, BF("ident_f"))

    p.dma("sync", fmp_sb, fmp_in, writes=[BF("fmp")])
    for name, dst, src in (
            ("gate_b_sb", gate_b_sb[0:1, :], gate_b_in),
            ("b1c_sb", b1c_sb, b1c_in), ("b3c_sb", b3c_sb, b3c_in),
            ("bs1c_sb", bs1c_sb, bs1c_in), ("bs3c_sb", bs3c_sb, bs3c_in),
            ("ab_b1_sb", ab_b1_sb[0:1, :], ab_b1_in),
            ("ab_b2_sb", ab_b2_sb[0:1, :], ab_b2_in), ("ab_b3_sb", ab_b3_sb[0:1, :], ab_b3_in)):
        p.dma("sync", dst, src, writes=[BF(name)])

    def load_cast(dst_sb, dst_buf, hbm, nchunks, cols, tag):
        view = hbm.rearrange("(c p) e -> p c e", p=128)
        for c in range(nchunks):
            stb = BF(f"wst{c % 2}")
            p.dma("sync", wst[c % 2][:, 0:cols], view[:, c, :],
                  writes=[stb], tag=tag)
            v_op(lambda v, c=c: v.tensor_copy(
                    dst_sb[:, c * cols:(c + 1) * cols], wst[c % 2][:, 0:cols]),
                 reads=[stb], writes=[dst_buf], tag=tag)

    p.dma("sync", wst[0][0:9, 0:D], b2bs2_in, writes=[BF("wst0")], tag="b2b")
    v_op(lambda v: v.tensor_copy(b2bs2_sb, wst[0][0:9, 0:D]),
         reads=[BF("wst0")], writes=[BF("b2bs2_sb")], tag="b2b")
    load_cast(gwt_sb, BF("gwt_sb"), gwt_in, 8, E, "gwt")
    load_cast(abw2t_sb, BF("abw2t_sb"), ab_w2t_in, 8, 16, "abw2t")
    load_cast(abw3t_sb, BF("abw3t_sb"), ab_w3t_in, 8, 16, "abw3t")
    load_cast(wq_sb, BF("wq_sb"), wq_slice, 8, 768, "wqs")

    def emit_sigma(w_nat, w_T, u_row, sp_row, m, idx, tg):
        mc = m // 128
        p.dma("sync", srow[0:1, 0:m], u_row, writes=[BF("srow")], tag=tg)
        for i in range(mc):
            t_op(lambda t, i=i: t.transpose(P[3][0:128, 0:1],
                                            srow[0:1, i * 128:(i + 1) * 128],
                                            ident_f[0:1, 0:1]),
                 reads=[BF("srow"), BF("ident_f")], writes=[PBK[3][0]], tag=tg)
            v_op(lambda v, i=i: v.tensor_copy(wsmall[:, i:i + 1],
                                              P[3][0:128, 0:1]),
                 reads=[PBK[3][0]], writes=[BF("ucol")], tag=tg)
        for i in range(mc):
            stb = BF(f"sgst{i % 2}")
            p.dma("sync", sgst[i % 2][:, 0:D], w_T[i * 128:(i + 1) * 128, :],
                  writes=[stb], tag=tg)
            for sl in range(2):
                t_op(lambda t, i=i, sl=sl: t.matmul(
                        P[2][0:1, sl * 512:(sl + 1) * 512],
                        wsmall[:, i:i + 1],
                        sgst[i % 2][:, sl * 512:(sl + 1) * 512],
                        start=(i == 0), stop=(i == mc - 1)),
                     reads=[BF("ucol"), stb], writes=[PBK[2][sl]], tag=tg)
        for sl in range(2):
            v_op(lambda v, sl=sl: v.tensor_copy(
                    arow[0:1, sl * 512:(sl + 1) * 512],
                    P[2][0:1, sl * 512:(sl + 1) * 512]),
                 reads=[PBK[2][sl]], writes=[BF("arow")], tag=tg)
        if tg == "sp" and DBG:
            dbgs = nc.dram_tensor("dbgs", [128, 1040], F32,
                                  kind="ExternalOutput").ap()
            p.dma("sync", dbgs[0:1, 0:1024], arow[0:1, 0:D],
                  reads=[BF("arow")], writes=[BF("dbgs")], tag=tg)
            p.dma("sync", dbgs[:, 1024:1032], wsmall[:, 0:8],
                  reads=[BF("ucol")], writes=[BF("dbgs")], tag=tg)
            p.dma("sync", dbgs[:, 1032:1040], wsmall[:, 24:32],
                  reads=[BF("acol")], writes=[BF("dbgs")], tag=tg)
        s_op(lambda s: s.activation(junk[0:1, 0:D], arow[0:1, 0:D], AF.Square,
                                    accum_out=scal[0:1, 8 + idx:9 + idx]),
             reads=[BF("arow")], writes=[BF("junk"), BF("n1" + tg)], tag=tg)
        for i in range(8):
            t_op(lambda t, i=i: t.transpose(P[3][0:128, 0:1],
                                            arow[0:1, i * 128:(i + 1) * 128],
                                            ident_f[0:1, 0:1]),
                 reads=[BF("arow"), BF("ident_f")], writes=[PBK[3][0]], tag=tg)
            v_op(lambda v, i=i: v.tensor_copy(wsmall[:, 24 + i:25 + i],
                                              P[3][0:128, 0:1]),
                 reads=[PBK[3][0]], writes=[BF("acol")], tag=tg)
        nsl = m // 512
        bank_of = [PBK[0][0], PBK[0][1], PBK[1][0], PBK[1][1],
                   PBK[2][0], PBK[2][1]]
        pt_of = [P[0], P[0], P[1], P[1], P[2], P[2]]
        for i in range(8):
            stb = BF(f"sgst{i % 2}")
            p.dma("sync", sgst[i % 2][:, 0:m], w_nat[i * 128:(i + 1) * 128, :],
                  writes=[stb], tag=tg)
            for sl in range(nsl):
                t_op(lambda t, i=i, sl=sl: t.matmul(
                        pt_of[sl][0:1, (sl % 2) * 512:(sl % 2) * 512 + 512],
                        wsmall[:, 24 + i:25 + i],
                        sgst[i % 2][:, sl * 512:(sl + 1) * 512],
                        start=(i == 0), stop=(i == 7)),
                     reads=[BF("acol"), stb], writes=[bank_of[sl]], tag=tg)
        for sl in range(nsl):
            v_op(lambda v, sl=sl: v.tensor_copy(
                    srow[0:1, sl * 512:(sl + 1) * 512],
                    pt_of[sl][0:1, (sl % 2) * 512:(sl % 2) * 512 + 512]),
                 reads=[bank_of[sl]], writes=[BF("srow")], tag=tg)
        s_op(lambda s, m=m: s.activation(arow[0:1, 0:m], srow[0:1, 0:m],
                                         AF.Square,
                                         accum_out=scal[0:1, 10 + idx:11 + idx]),
             reads=[BF("srow")], writes=[BF("arow"), BF("n2" + tg)], tag=tg)
        p.dma("sync", scal[0:1, 12 + idx:13 + idx], sp_row,
              writes=[BF("sp" + tg)], tag=tg)
        s_op(lambda s: s.activation(scal[0:1, 14 + idx:15 + idx],
                                    scal[0:1, 8 + idx:9 + idx], AF.Ln),
             reads=[BF("n1" + tg)], writes=[BF("l1" + tg)], tag=tg)
        s_op(lambda s: s.activation(scal[0:1, 16 + idx:17 + idx],
                                    scal[0:1, 10 + idx:11 + idx], AF.Ln),
             reads=[BF("n2" + tg)], writes=[BF("l2" + tg)], tag=tg)
        v_op(lambda v: v.tensor_sub(scal[0:1, 18 + idx:19 + idx],
                                    scal[0:1, 14 + idx:15 + idx],
                                    scal[0:1, 16 + idx:17 + idx]),
             reads=[BF("l1" + tg), BF("l2" + tg)], writes=[BF("df" + tg)],
             tag=tg)
        s_op(lambda s: s.activation(scal[0:1, 20 + idx:21 + idx],
                                    scal[0:1, 18 + idx:19 + idx], AF.Exp,
                                    scale=0.5),
             reads=[BF("df" + tg)], writes=[BF("ex" + tg)], tag=tg)
        v_op(lambda v: v.tensor_mul(scal[0:1, idx:idx + 1],
                                    scal[0:1, 20 + idx:21 + idx],
                                    scal[0:1, 12 + idx:13 + idx]),
             reads=[BF("ex" + tg), BF("sp" + tg)], writes=[BF("c" + tg)],
             tag=tg)

    emit_sigma(wqkv, wqkvT, u_qkv, sp_qkv, 3 * D, 0, "sq")
    emit_sigma(wproj, wprojT, u_proj, sp_proj, D, 1, "sp")

    v_op(lambda v: v.tensor_scalar(scal[0:1, 2:3], scal[0:1, 0:1],
                                   scal[0:1, 0:1], 0.125,
                                   op0=ALU.mult, op1=ALU.mult),
         reads=[BF("csq")], writes=[BF("c28s")])
    v_op(lambda v: v.tensor_mul(scal[0:1, 3:4], scal[0:1, 0:1],
                                scal[0:1, 1:2]),
         reads=[BF("csq"), BF("csp")], writes=[BF("ccs")])
    t_op(lambda t: t.matmul(P[3][0:128, 0:1], ones_f[0:1, :], scal[0:1, 2:3],
                            start=True, stop=True),
         reads=[BF("ones_f"), BF("c28s")], writes=[PBK[3][0]])
    v_op(lambda v: v.tensor_copy(c28b[:, 0:1], P[3][0:128, 0:1]),
         reads=[PBK[3][0]], writes=[BF("c28b")])
    t_op(lambda t: t.matmul(P[3][0:128, 1:2], ones_f[0:1, :], scal[0:1, 3:4],
                            start=True, stop=True),
         reads=[BF("ones_f"), BF("ccs")], writes=[PBK[3][0]])
    v_op(lambda v: v.tensor_copy(ccb[:, 0:1], P[3][0:128, 1:2]),
         reads=[PBK[3][0]], writes=[BF("ccb")])

    p.barrier("post-sigma")

    prep_jobs = []
    for e in range(E):
        for c in range(8):
            prep_jobs.append((w1t_in[e].rearrange("(c p) i -> p c i", p=128),
                              (c, 0), scr_w1t[e][:, c * 1024:(c + 1) * 1024]))
            prep_jobs.append((w3t_in[e].rearrange("(c p) i -> p c i", p=128),
                              (c, 0), scr_w3t[e][:, c * 1024:(c + 1) * 1024]))
            prep_jobs.append((w2t_in[e].rearrange("(c p) i -> p c i", p=128),
                              (c, 0), scr_w2t[e][:, c * 1024:(c + 1) * 1024]))
    vws1 = ws1t_in.rearrange("(c p) i -> p c i", p=128)
    vws3 = ws3t_in.rearrange("(c p) i -> p c i", p=128)
    vws2 = ws2t_in.rearrange("(c p) i -> p c i", p=128)
    vab1 = ab_w1t_in.rearrange("(c p) i -> p c i", p=128)
    for c in range(8):
        for half in range(2):
            prep_jobs.append((vws1, (c, half * 1024),
                              scr_ws1t[:, half * 8192 + c * 1024:
                                       half * 8192 + c * 1024 + 1024]))
            prep_jobs.append((vws3, (c, half * 1024),
                              scr_ws3t[:, half * 8192 + c * 1024:
                                       half * 8192 + c * 1024 + 1024]))
    for c in range(16):
        prep_jobs.append((vws2, (c, 0), scr_ws2t[:, c * 1024:(c + 1) * 1024]))
    for c in range(8):
        prep_jobs.append((vab1, (c, 0), scr_abw1[:, c * 1024:(c + 1) * 1024]))

    prep_i = [0]

    def emit_prep(npieces):
        for _ in range(npieces):
            j = prep_i[0]
            if j >= len(prep_jobs):
                return
            prep_i[0] += 1
            view, (c, off), dst = prep_jobs[j]
            src = view[:, c, off:off + 1024]
            i = j % 2
            stb, obb = BF(f"wst{i}"), BF(f"wsbst{i}")
            p.dma("scalar", wst[i][:, 0:1024], src, writes=[stb], tag="prep")
            s_op(lambda s, i=i: s.activation(wsbst[i][:, 0:1024],
                                             wst[i][:, 0:1024], AF.Copy),
                 reads=[stb], writes=[obb], tag="prep")
            p.dma("scalar", dst, wsbst[i][:, 0:1024], reads=[obb],
                  writes=[BF("scr")], tag="prep")

    def emit_ln(src, src_buf, ntiles, xn_dst, xn_buf, tag, base=0, xn_w=None):
        xw = list(xn_w) if xn_w is not None else [xn_buf]
        for i in range(base, base + ntiles):
            v_op(lambda v, i=i: v.reduce_sum(stat[:, i:i + 1],
                                             src[:, i * D:(i + 1) * D],
                                             axis=AX.X),
                 reads=[src_buf], writes=[BF("stat")], tag=tag)
            s_op(lambda s, i=i: s.activation(junk, src[:, i * D:(i + 1) * D],
                                             AF.Square,
                                             accum_out=stat[:, 24 + i:25 + i]),
                 reads=[src_buf], writes=[BF("junk"), BF("stat")], tag=tag)
        nt, b0 = ntiles, base
        v_op(lambda v, nt=nt, b0=b0: v.tensor_scalar(stat[:, 48 + b0:48 + b0 + nt],
                                              stat[:, b0:b0 + nt], INV_D, None,
                                              op0=ALU.mult),
             reads=[BF("stat")], writes=[BF("stat")], tag=tag)
        v_op(lambda v, nt=nt, b0=b0: v.tensor_mul(stat[:, 72 + b0:72 + b0 + nt],
                                           stat[:, 48 + b0:48 + b0 + nt],
                                           stat[:, 48 + b0:48 + b0 + nt]),
             reads=[BF("stat")], writes=[BF("stat")], tag=tag)
        v_op(lambda v, nt=nt, b0=b0: v.scalar_tensor_tensor(
                stat[:, b0:b0 + nt], stat[:, 24 + b0:24 + b0 + nt], INV_D,
                stat[:, 72 + b0:72 + b0 + nt], op0=ALU.mult, op1=ALU.subtract),
             reads=[BF("stat")], writes=[BF("stat")], tag=tag)
        v_op(lambda v, nt=nt, b0=b0: v.tensor_scalar(stat[:, b0:b0 + nt],
                                              stat[:, b0:b0 + nt],
                                              EPS, None, op0=ALU.add),
             reads=[BF("stat")], writes=[BF("stat")], tag=tag)
        s_op(lambda s, nt=nt, b0=b0: s.activation(stat[:, 24 + b0:24 + b0 + nt],
                                           stat[:, b0:b0 + nt], AF.Ln),
             reads=[BF("stat")], writes=[BF("stat")], tag=tag)
        s_op(lambda s, nt=nt, b0=b0: s.activation(stat[:, 96 + b0:96 + b0 + nt],
                                           stat[:, 24 + b0:24 + b0 + nt], AF.Exp,
                                           scale=-0.5),
             reads=[BF("stat")], writes=[BF("stat")], tag=tag)
        for i in range(base, base + ntiles):
            v_op(lambda v, i=i: v.tensor_scalar(
                    xn_dst[:, i * D:(i + 1) * D], src[:, i * D:(i + 1) * D],
                    stat[:, 48 + i:49 + i], stat[:, 96 + i:97 + i],
                    op0=ALU.subtract, op1=ALU.mult),
                 reads=[src_buf, BF("stat")], writes=xw, tag=tag)

    def emit_transpose(xn_src, xn_buf, ntiles, dst, dst_buf, dst_tok, tag,
                       tbank=3, base=0):
        pbv = PB3 if tbank == 3 else PB2A
        for i in range(base, base + ntiles):
            off = (i % 2) * 1024
            bank = PBK[tbank][i % 2]
            for dc in range(8):
                t_op(lambda t, i=i, dc=dc, off=off, pbv=pbv: t.transpose(
                        pbv[:, off + dc * 128: off + dc * 128 + 128],
                        xn_src[:, i * D + dc * 128: i * D + dc * 128 + 128],
                        ident_b),
                     reads=[xn_buf, BF("ident_b")], writes=[bank], tag=tag)
            v_op(lambda v, i=i, off=off, pbv=pbv: v.tensor_copy(
                    dst[:].rearrange("p (c t) -> p c t", c=8)[
                        :, :, i * 128:(i + 1) * 128],
                    pbv[:, off:off + 1024].rearrange("p (c t) -> p c t", c=8)),
                 reads=[bank], writes=[dst_buf], tag=tag)

    QF, KF, VF = BF("q_fm"), BF("k_fm"), BF("v_fm")
    xin_v = x_in.rearrange("(c r p) d -> c p r d", p=128, r=4)
    xa_v = xa.rearrange("p (r d) -> p r d", d=D)
    for ch in range(16):
        xab = BF("xa")
        p.dma("sync", xa_v, xin_v[ch], writes=[xab], tag=f"A{ch}")
        emit_ln(xa, xab, 4, xnc, BF("xnc"), f"Aln{ch}")
        if DBG and ch == 0:
            dbg0 = nc.dram_tensor("dbg0", [128, 2176], F32,
                                  kind="ExternalOutput").ap()
            p.dma("sync", dbg0[:, 0:1024], xa[:, 0:1024],
                  reads=[xab], writes=[BF("dbg0")])
            p.dma("sync", dbg0[:, 1024:1144].bitcast(F32) if False else
                  dbg0[:, 1024:1144], stat[:, 0:120],
                  reads=[BF("stat")], writes=[BF("dbg0")])
            p.dma("sync", dbg0[:, 1152:2176].bitcast(BF16),
                  xnc[:, 0:2048], reads=[BF("xnc")], writes=[BF("dbg0")])
        emit_transpose(xnc, BF("xnc"), 4, xnTc, BF("xnTc"), 512, f"Atr{ch}", tbank=2)
        emit_prep(7)
        for dst, dbuf, colo, pi in ((q_fm, QF, 0, 3), (k_fm, KF, 256, 2)):
            for pair in range(2):
                bank = PBK[pi][pair]
                for dc in range(8):
                    t_op(lambda t, pair=pair, dc=dc, colo=colo, pi=pi: t.matmul(
                            P[pi][:, pair * 512:pair * 512 + 512],
                            wq_sb[:, dc * 768 + colo + pair * 128:
                                  dc * 768 + colo + pair * 128 + 128],
                            xnTc[:, dc * 512:(dc + 1) * 512],
                            start=(dc == 0), stop=(dc == 7)),
                         reads=[BF("wq_sb"), BF("xnTc")], writes=[bank],
                         tag=f"Aqk{ch}")
                s_op(lambda s, pair=pair, dst=dst, ch=ch, pi=pi: s.activation(
                        dst[:, pair * N + ch * 512: pair * N + ch * 512 + 512],
                        P[pi][:, pair * 512:pair * 512 + 512], AF.Copy),
                     reads=[bank], writes=[dbuf], tag=f"Aqk{ch}")
        for sub in range(4):
            bank = PBK[3][sub % 2]
            for dc in range(8):
                t_op(lambda t, sub=sub, dc=dc: t.matmul(
                        P[3][:, (sub % 2) * 512:(sub % 2) * 512 + 256],
                        xnTc[:, dc * 512 + sub * 128: dc * 512 + sub * 128 + 128],
                        wq_sb[:, dc * 768 + 512: dc * 768 + 768],
                        start=(dc == 0), stop=(dc == 7)),
                     reads=[BF("wq_sb"), BF("xnTc")], writes=[bank],
                     tag=f"Av{ch}")
            v_op(lambda v, sub=sub, ch=ch: v.tensor_copy(
                    v_fm[:].rearrange("p (s h e) -> p s h e", h=4, e=65)[
                        :, ch * 4 + sub, :, 0:64],
                    P[3][:, (sub % 2) * 512:(sub % 2) * 512 + 256].rearrange(
                        "p (h e) -> p h e", h=4)),
                 reads=[bank], writes=[VF], tag=f"Av{ch}")
    p.barrier("post-A")

    OBF = BF("obfT")
    PB2 = P[2][:].bitcast(BF16)
    PREP_B = [1, 2, 3, 4, 6, 7, 8, 9]
    load_cast(wp2_sb, BF("wp2_sb"), wp_slice, 2, D, "wp2")
    cinA_v = cinA.rearrange("(q p) d -> p q d", p=128)
    cinB_v = cinB.rearrange("(q p) d -> p q d", p=128)
    o_acc4_v = o_acc4[:].rearrange("p (q d) -> p q d", d=D)
    OACCB = [BF("oacc0"), BF("oacc1"), BF("oacc2"), BF("oacc3")]

    def emit_proj(tq):
        pb = PBK[tq % 2]
        for i in range(4):
            tt = tq * 4 + i
            for half in range(2):
                for k in range(2):
                    t_op(lambda t, tt=tt, half=half, k=k, tq=tq: t.matmul(
                            P[tq % 2][:, half * 512:half * 512 + 512],
                            obfT[:, k * N + tt * 128: k * N + tt * 128 + 128],
                            wp2_sb[:, k * 1024 + half * 512:
                                   k * 1024 + half * 512 + 512],
                            start=(k == 0), stop=(k == 1)),
                         reads=[OBF, BF("wp2_sb")], writes=[pb[half]],
                         tag="proj")
            v_op(lambda v, i=i, tq=tq: v.tensor_scalar(
                    o_acc[i][:, 0:1024],
                    P[tq % 2][:, 0:1024], ccb[:, 0:1], None, op0=ALU.mult),
                 reads=[pb[0], pb[1], BF("ccb")], writes=[OACCB[i]],
                 tag="proj")
        j = tq % 4
        chunk_v, cb = (cinA_v, "cinA") if j < 2 else (cinB_v, "cinB")
        qt0 = (tq // 4) * 8 + (j % 2) * 4
        p.dma("sync", chunk_v[:, qt0:qt0 + 4, :], o_acc4_v,
              reads=list(OACCB), writes=[BF(cb)], tag="proj")

    def emit_rs(src, dst, sb, db):
        def emit(g, src=src, dst=dst):
            return g.collective_compute(
                "ReduceScatter", ALU.add,
                replica_groups=[[0, 1, 2, 3], [4, 5, 6, 7]],
                ins=[src.opt()], outs=[dst.opt()])
        p.op("gpsimd", emit, reads=[BF(sb)], writes=[BF(db)],
             sem="cc", inc=1, tag="rs")

    for qw in range(8):
        nkc = 2 * (qw + 1)
        for h in range(UH):
            Ppv = P[2 + h % 2]
            pvb = [PBK[2 + h % 2][0], PBK[2 + h % 2][1]]
            nks = nkc * 4
            units = [(kc * 512 + ksub * 128, kc * 4 + ksub)
                     for kc in range(nkc) for ksub in range(4)]

            def emit_qk_exp(ks):
                ka, sub_abs = units[ks]
                Pst = P[ks % 2]
                stb = [PBK[ks % 2][0], PBK[ks % 2][1]]
                for half in range(2):
                    t_op(lambda t, h=h, ka=ka, qw=qw, half=half,
                         Pst=Pst: t.matmul(
                            Pst[:, half * 512:half * 512 + 512],
                            k_fm[(h % 2) * 64:(h % 2) * 64 + 64,
                                 (h // 2) * N + ka:
                                 (h // 2) * N + ka + 128],
                            q_fm[(h % 2) * 64:(h % 2) * 64 + 64,
                                 (h // 2) * N + qw * 1024 + half * 512:
                                 (h // 2) * N + qw * 1024
                                 + half * 512 + 512],
                            start=True, stop=True),
                         reads=[KF, QF], writes=[stb[half]], tag=f"B{qw}")
                pt = ptile[ks % 4]
                ptb = BF(f"ptile{ks % 4}")
                s_op(lambda s, pt=pt, Pst=Pst: s.activation(
                        pt, Pst[:, 0:1024], AF.Exp, scale=c28b[:, 0:1]),
                     reads=[stb[0], stb[1], BF("c28b")],
                     writes=[ptb], tag=f"B{qw}")
                if ka + 127 >= qw * 1024:
                    thr = float(ka - qw * 1024)
                    v_op(lambda v, pt=pt, thr=thr:
                         v.scalar_tensor_tensor(
                            pt, fmp_sb, thr, pt,
                            op0=ALU.is_ge, op1=ALU.mult),
                         reads=[ptb, BF("fmp")], writes=[ptb],
                         tag=f"B{qw}")

            def emit_pv(ks):
                ka, sub_abs = units[ks]
                pt = ptile[ks % 4]
                ptb = BF(f"ptile{ks % 4}")
                for half in range(2):
                    t_op(lambda t, h=h, sub_abs=sub_abs, half=half,
                         Ppv=Ppv, pt=pt, ks=ks, nks=nks: t.matmul(
                            Ppv[0:65, half * 512:half * 512 + 512],
                            v_fm[:, sub_abs * 260 + h * 65:
                                 sub_abs * 260 + h * 65 + 65],
                            pt[:, half * 512:half * 512 + 512],
                            start=(ks == 0), stop=(ks == nks - 1)),
                         reads=[VF, ptb], writes=[pvb[half]], tag=f"B{qw}")

            LAG = 2
            for step in range(nks + LAG):
                if step < nks:
                    emit_qk_exp(step)
                if step >= LAG:
                    emit_pv(step - LAG)
            ob = BF(f"oacc{h}")
            v_op(lambda v, Ppv=Ppv: v.reciprocal(
                    linv[0:1, 0:1024], Ppv[64:65, 0:1024]),
                 reads=pvb, writes=[BF("linv")], tag=f"Bf{qw}")
            v_op(lambda v, h=h, Ppv=Ppv: v.tensor_copy(
                    o_acc[h][0:65, :], Ppv[0:65, 0:1024]),
                 reads=pvb, writes=[ob], tag=f"Bf{qw}")
            for half in range(2):
                t_op(lambda t, half=half, Ppv=Ppv: t.matmul(
                        Ppv[0:64, half * 512:half * 512 + 512],
                        ones_f[0:1, 0:64],
                        linv[0:1, half * 512:half * 512 + 512],
                        start=True, stop=True),
                     reads=[BF("ones_f"), BF("linv")], writes=[pvb[half]],
                     tag=f"Bf{qw}")
                v_op(lambda v, h=h, qw=qw, half=half, Ppv=Ppv: v.tensor_mul(
                        obfT[(h % 2) * 64:(h % 2) * 64 + 64,
                             (h // 2) * N + qw * 1024 + half * 512:
                             (h // 2) * N + qw * 1024 + half * 512 + 512],
                        o_acc[h][0:64, half * 512:half * 512 + 512],
                        Ppv[0:64, half * 512:half * 512 + 512]),
                     reads=[ob, pvb[half]], writes=[OBF], tag=f"Bf{qw}")
            emit_prep(PREP_B[qw])
        emit_proj(2 * qw)
        emit_proj(2 * qw + 1)
        if RS_EARLY and qw == 6:
            emit_rs(cinA, coutA, "cinA", "coutA")

    assert prep_i[0] == len(prep_jobs), (prep_i[0], len(prep_jobs))
    if not RS_EARLY:
        emit_rs(cinA, coutA, "cinA", "coutA")
    emit_rs(cinB, coutB, "cinB", "coutB")
    if PB_MODE == 0:
        p.barrier("post-rs")

    XO_A, XO_B = BF("x_ownA"), BF("x_ownB")
    XNb_A, XNb_B = BF("xn_tA"), BF("xn_tB")
    XNTb_A, XNTb_B = BF("xnTA"), BF("xnTB")
    GS_A, GS_B = BF("gateA"), BF("gateB")
    XO_H = (XO_A, XO_B)
    XNb_H = (XNb_A, XNb_B)
    XNTb_H = (XNTb_A, XNTb_B)
    GS_H = (GS_A, GS_B)
    stgA = av(OFF_K, 128, 4096, F32)
    stgB = av(OFF_K + 16384, 128, 4096, F32)
    stgA_v = stgA[:].rearrange("p (q d) -> p q d", d=D)
    stgB_v = stgB[:].rearrange("p (q d) -> p q d", d=D)
    coutA_v = coutA.rearrange("(q p) d -> p q d", p=128)
    coutB_v = coutB.rearrange("(q p) d -> p q d", p=128)

    xo_v = xown_in.rearrange("(q p) d -> p q d", p=128)
    p.dma("sync", x_own[:].rearrange("p (q d) -> p q d", d=D), xo_v,
          writes=[XO_A, XO_B, OBF, BF("ptile0"), BF("ptile1"), BF("ptile2"),
                  BF("ptile3")] + list(OACCB),
          tag="xown")
    for blk in range(2):
        p.dma("sync", stgA_v, coutA_v[:, blk * 4:(blk + 1) * 4, :],
              reads=[BF("coutA")], writes=[XNb_A, KF], tag="rsrd")
        v_op(lambda v, blk=blk: v.tensor_add(
                x_own[:, blk * 4096:(blk + 1) * 4096],
                x_own[:, blk * 4096:(blk + 1) * 4096], stgA),
             reads=[XNb_A, XO_A], writes=[XO_A], tag="rsrd")

    def emit_add_b():
        for blk in range(2, 4):
            p.dma("gpsimd", stgB_v, coutB_v[:, (blk - 2) * 4:(blk - 1) * 4, :],
                  reads=[BF("coutB")], writes=[XNb_B], tag="rsrd")
            v_op(lambda v, blk=blk: v.tensor_add(
                    x_own[:, blk * 4096:(blk + 1) * 4096],
                    x_own[:, blk * 4096:(blk + 1) * 4096], stgB),
                 reads=[XNb_B, XO_B], writes=[XO_B], tag="rsrd")

    if DBG:
        dbgf = nc.dram_tensor("dbgf", [128, 1024], F32, kind="ExternalOutput").ap()
        dbgb = nc.dram_tensor("dbgb", [128, 2048], BF16, kind="ExternalOutput").ap()
        p.barrier("dbg")
        p.dma("sync", dbgf[0:1, 0:24], scal, reads=[], writes=[BF("dbgf")])
        p.dma("sync", dbgf[:, 24:32], c28b, writes=[BF("dbgf")])
        p.dma("sync", dbgf[:, 32:288], mstat, writes=[BF("dbgf")])
        p.dma("sync", dbgf[:, 288:544], o_acc[0][:, 0:256], writes=[BF("dbgf")])
        p.dma("sync", dbgf[:, 544:560], lr, writes=[BF("dbgf")])
        p.dma("sync", dbgf[0:1, 560:816], linv[0:1, 0:256], writes=[BF("dbgf")])
        p.dma("sync", dbgb[:, 0:512], q_fm[:, 0:512], writes=[BF("dbgb")])
        p.dma("sync", dbgb[:, 512:1024], k_fm[:, 0:512], writes=[BF("dbgb")])
        p.dma("sync", dbgb[:, 1024:1280], negst, writes=[BF("dbgb")])
        p.dma("sync", dbgb[:, 1280:1792], v_fm[:, 0:512], writes=[BF("dbgb")])
        p.dma("sync", dbgb[:, 1792:2048], obfT[:, 0:256], writes=[BF("dbgb")])
        p.barrier("dbg2")

    def emit_store():
        p.dma("sync", y_out.rearrange("(q p) d -> p q d", p=128),
              x_own[:].rearrange("p (q d) -> p q d", d=D),
              reads=[XO_A, XO_B], writes=[BF("y")], tag="store")
        p.barrier("final")

    if stop_after == "attn":
        emit_add_b()
        emit_store()
        return nc, p, sems

    FPART = int(_os2.environ.get("FPART", "3")) if True else 3
    FSUB = _os2.environ.get("FSUB", "c")
    def emit_ffn(call, pre_b=None):
        tg = f"F{call}"
        for hf in range(2):
            if hf == 1 and pre_b is not None:
                pre_b()
            emit_ln(x_own, XO_H[hf], 8, xn_t, XNb_H[hf], tg, base=8 * hf,
                    xn_w=(XNb_H[hf], BF(f"w1sb{hf}")) if call == 1 else None)
            emit_transpose(xn_t, XNb_H[hf], 8, xnT, XNTb_H[hf], TOK, tg,
                           base=8 * hf)
            for tt in range(8 * hf, 8 * hf + 8):
                emit_gate(call, tt, tg)
            if call == 0 and hf == 0 and FPART >= 3:
                emit_expert_part(0, tg, first=True)
        if call == 1:
            p.barrier(tg + "-w")
        emit_experts(call, tg)
        emit_biascombo(call, tg)

    def emit_gate(call, tt, tg):
        XNTb = XNTb_H[tt // 8]
        gate_w_buf = GS_H[tt // 8]
        if True:
            gb = PBK[2][tt % 2]
            goff = (tt % 2) * 512
            for dc in range(8):
                t_op(lambda t, tt=tt, dc=dc, goff=goff: t.matmul(
                        P[2][:, goff:goff + 8],
                        xnT[:, dc * TOK + tt * 128: dc * TOK + tt * 128 + 128],
                        gwt_sb[:, dc * 8:(dc + 1) * 8],
                        start=(dc == 0), stop=False),
                     reads=[XNTb, BF("gwt_sb")], writes=[gb], tag=tg)
            t_op(lambda t, goff=goff: t.matmul(
                    P[2][:, goff:goff + 8], e0f, gate_b_sb,
                    start=False, stop=True),
                 reads=[BF("e0f"), BF("gate_b_sb")], writes=[gb], tag=tg)
            v_op(lambda v, goff=goff: v.tensor_reduce(
                    wsmall[:, 0:1], P[2][:, goff:goff + 8], axis=AX.X,
                    op=ALU.max, negate=True),
                 reads=[gb], writes=[BF("wg")], tag=tg)
            s_op(lambda s, goff=goff: s.activation(
                    gsc, P[2][:, goff:goff + 8], AF.Exp,
                    bias=wsmall[:, 0:1], accum_out=wsmall[:, 1:2]),
                 reads=[gb, BF("wg")], writes=[BF("gsc"), BF("wg")], tag=tg)
            v_op(lambda v: v.reciprocal(wsmall[:, 2:3], wsmall[:, 1:2]),
                 reads=[BF("wg")], writes=[BF("wg")], tag=tg)
            v_op(lambda v: v.tensor_scalar(gsc, gsc, wsmall[:, 2:3], None,
                                           op0=ALU.mult),
                 reads=[BF("wg"), BF("gsc")], writes=[BF("gsc")], tag=tg)
            v_op(lambda v: v.tensor_reduce(wsmall[:, 3:4], gsc, axis=AX.X,
                                           op=ALU.max),
                 reads=[BF("gsc")], writes=[BF("wg")], tag=tg)
            v_op(lambda v: v.tensor_single_scalar(wsmall[:, 8:16], gsc,
                                                  wsmall[:, 3:4], ALU.is_ge),
                 reads=[BF("gsc"), BF("wg")], writes=[BF("wg8")], tag=tg)
            v_op(lambda v: v.scalar_tensor_tensor(wsmall[:, 16:24],
                                                  wsmall[:, 8:16], -1.0e30,
                                                  gsc, op0=ALU.mult,
                                                  op1=ALU.add),
                 reads=[BF("wg8"), BF("gsc")], writes=[BF("wg16")], tag=tg)
            v_op(lambda v: v.tensor_reduce(wsmall[:, 4:5], wsmall[:, 16:24],
                                           axis=AX.X, op=ALU.max),
                 reads=[BF("wg16")], writes=[BF("wg")], tag=tg)
            v_op(lambda v: v.tensor_single_scalar(wsmall[:, 8:16], gsc,
                                                  wsmall[:, 4:5], ALU.is_ge),
                 reads=[BF("gsc"), BF("wg")], writes=[BF("wg8")], tag=tg)
            v_op(lambda v, tt=tt: v.tensor_mul(gate_sb[:, tt * 8:tt * 8 + 8],
                                               gsc, wsmall[:, 8:16]),
                 reads=[BF("gsc"), BF("wg8")], writes=[gate_w_buf], tag=tg)

    def emit_biascombo(call, tg):
        if FPART < 2:
            return
        gta_full = av(193280 + 8192, 9, TOK, BF16)
        g_op(lambda g: g.memset(gta_full, 1.0),
             writes=[BF("gtaf"), BF("wsbst0"), BF("wsbst1")], tag=tg)
        for tt in range(NTT):
            bank = PBK[3][tt % 2]
            t_op(lambda t, tt=tt: t.transpose(
                    P[3][0:8, (tt % 2) * 512:(tt % 2) * 512 + 128],
                    gate_sb[:, tt * 8:tt * 8 + 8], ident_f),
                 reads=[GS_H[tt // 8], BF("ident_f")], writes=[bank],
                 tag=tg)
            v_op(lambda v, tt=tt: v.tensor_copy(
                    gta_full[0:8, tt * 128:(tt + 1) * 128],
                    P[3][0:8, (tt % 2) * 512:(tt % 2) * 512 + 128]),
                 reads=[bank, BF("gtaf")], writes=[BF("gtaf")], tag=tg)
        for tt in range(NTT):
            bank = PBK[3][tt % 2]
            for half in range(2):
                t_op(lambda t, tt=tt, half=half: t.matmul(
                        P[3][:, (tt % 2) * 512:(tt % 2) * 512 + 512],
                        gta_full[0:9, tt * 128:(tt + 1) * 128],
                        b2bs2_sb[0:9, half * 512:half * 512 + 512],
                        start=True, stop=True),
                     reads=[BF("gtaf"), BF("b2bs2_sb")],
                     writes=[bank], tag=tg)
                v_op(lambda v, tt=tt, half=half: v.tensor_add(
                        x_own[:, tt * 1024 + half * 512:
                              tt * 1024 + half * 512 + 512],
                        x_own[:, tt * 1024 + half * 512:
                              tt * 1024 + half * 512 + 512],
                        P[3][:, (tt % 2) * 512:(tt % 2) * 512 + 512]),
                     reads=[bank, XO_H[tt // 8]], writes=[XO_H[tt // 8]],
                     tag=tg)

    def _elist_of():
        _elist = list(range(10 if FPART >= 3 else 0))
        if _os2.environ.get("FONLY") == "s":
            _elist = [0, 8, 9]
        elif _os2.environ.get("FONLY") == "e":
            _elist = list(range(8))
        return _elist

    def emit_w1_load(e, slot, tg, extra=()):
        src = scr_w1t[e] if e < 8 else scr_ws1t[:, (e - 8) * 8192:
                                                (e - 8) * 8192 + 8192]
        p.dma("sync", w1sb[slot][:, 0:8192], src,
              writes=[BF(f"w1sb{slot}")] + list(extra), tag=tg)

    def emit_w32_load(e, tg):
        if e < 8:
            p.dma("sync", w3sb[:, 0:8192], scr_w3t[e], writes=[BF("w3sb")],
                  tag=tg)
            p.dma("sync", w2sb[:, 0:8192], scr_w2t[e], writes=[BF("w2sb")],
                  tag=tg)
        else:
            p.dma("sync", w3sb[:, 0:8192],
                  scr_ws3t[:, (e - 8) * 8192:(e - 8) * 8192 + 8192],
                  writes=[BF("w3sb")], tag=tg)
            p.dma("sync", w2sb[:, 0:8192],
                  scr_ws2t[:, (e - 8) * 8192:(e - 8) * 8192 + 8192],
                  writes=[BF("w2sb")], tag=tg)

    def emit_expert_compute(e, ei, tg, tbs):
        if e < 8:
            b1col, b3col, bco = b1c_sb, b3c_sb, e * 8
        else:
            b1col, b3col, bco = bs1c_sb, bs3c_sb, (e - 8) * 8
        w1b = BF(f"w1sb{ei % 2}")
        w1cur = w1sb[ei % 2]
        for tb in tbs:
            XNTb = XNTb_H[tb // 2]
            for ic in range(8):
                bi = ic % 4
                pt_ = P[bi // 2]
                bank = PBK[bi // 2][bi % 2]
                for dc in range(8):
                    t_op(lambda t, ic=ic, dc=dc, tb=tb, bi=bi, pt_=pt_,
                         w1cur=w1cur: t.matmul(
                            pt_[:, (bi % 2) * 512:(bi % 2) * 512 + 512],
                            w1cur[:, dc * 1024 + ic * 128:
                                  dc * 1024 + ic * 128 + 128],
                            xnT[:, dc * TOK + tb * 512:
                                dc * TOK + tb * 512 + 512],
                            start=(dc == 0), stop=(dc == 7)),
                         reads=[w1b, XNTb], writes=[bank], tag=tg)
                s_op(lambda s, ic=ic, bi=bi, pt_=pt_, b1col=b1col,
                     bco=bco: s.activation(
                        h1sb[:, ic * 512:(ic + 1) * 512],
                        pt_[:, (bi % 2) * 512:(bi % 2) * 512 + 512],
                        AF.Silu, bias=b1col[:, bco + ic:bco + ic + 1]),
                     reads=[bank, BF("b1c_sb" if e < 8 else "bs1c_sb")],
                     writes=[BF("h1sb")], tag=tg)
                bank3 = PBK[2 + bi // 2][bi % 2]
                pt3 = P[2 + bi // 2]
                for dc in range(8 if FSUB >= "b" else 0):
                    t_op(lambda t, ic=ic, dc=dc, tb=tb, bi=bi,
                         pt3=pt3: t.matmul(
                            pt3[:, (bi % 2) * 512:(bi % 2) * 512 + 512],
                            w3sb[:, dc * 1024 + ic * 128:
                                 dc * 1024 + ic * 128 + 128],
                            xnT[:, dc * TOK + tb * 512:
                                dc * TOK + tb * 512 + 512],
                            start=(dc == 0), stop=(dc == 7)),
                         reads=[BF("w3sb"), XNTb], writes=[bank3], tag=tg)
                if FSUB >= "b":
                    v_op(lambda v, ic=ic, bi=bi, pt3=pt3, b3col=b3col,
                         bco=bco: v.scalar_tensor_tensor(
                            h1sb[:, ic * 512:(ic + 1) * 512],
                            pt3[:, (bi % 2) * 512:(bi % 2) * 512 + 512],
                            b3col[:, bco + ic:bco + ic + 1],
                            h1sb[:, ic * 512:(ic + 1) * 512],
                            op0=ALU.add, op1=ALU.mult),
                         reads=[bank3, BF("h1sb"),
                                BF("b3c_sb" if e < 8 else "bs3c_sb")],
                         writes=[BF("h1sb")], tag=tg)
            for ts in range(4 if FSUB >= "c" else 0):
                tt = tb * 4 + ts
                Pw = P[2 + ts % 2]
                pwb = [PBK[2 + ts % 2][0], PBK[2 + ts % 2][1]]
                for ic in range(8):
                    for half in range(2):
                        t_op(lambda t, ic=ic, ts=ts, half=half,
                             Pw=Pw: t.matmul(
                                Pw[:, half * 512:half * 512 + 512],
                                h1sb[:, ic * 512 + ts * 128:
                                     ic * 512 + ts * 128 + 128],
                                w2sb[:, ic * 1024 + half * 512:
                                     ic * 1024 + half * 512 + 512],
                                start=(ic == 0), stop=(ic == 7)),
                             reads=[BF("h1sb"), BF("w2sb")],
                             writes=[pwb[half]], tag=tg)
                s_op(lambda s, Pw=Pw: s.activation(junk, Pw[:, 0:1024],
                                                   AF.Copy),
                     reads=pwb, writes=[BF("junk")], tag=tg)
                if e < 8:
                    v_op(lambda v, tt=tt, e=e: v.scalar_tensor_tensor(
                            x_own[:, tt * 1024:(tt + 1) * 1024],
                            junk,
                            gate_sb[:, tt * 8 + e:tt * 8 + e + 1],
                            x_own[:, tt * 1024:(tt + 1) * 1024],
                            op0=ALU.mult, op1=ALU.add),
                         reads=[BF("junk"), GS_H[tt // 8], XO_H[tt // 8]],
                         writes=[XO_H[tt // 8]], tag=tg)
                else:
                    v_op(lambda v, tt=tt: v.tensor_add(
                            x_own[:, tt * 1024:(tt + 1) * 1024],
                            x_own[:, tt * 1024:(tt + 1) * 1024],
                            junk),
                         reads=[BF("junk"), XO_H[tt // 8]],
                         writes=[XO_H[tt // 8]], tag=tg)

    def emit_expert_part(e, tg, first):
        emit_w1_load(e, 0, tg, extra=(XNb_A,))
        emit_w32_load(e, tg)
        emit_expert_compute(e, 0, tg, [0, 1])

    def emit_experts(call, tg):
        _elist = _elist_of()
        if not _elist:
            return
        if call == 0:
            if len(_elist) > 1:
                emit_w1_load(_elist[1], 1, tg, extra=(XNb_B,))
            emit_expert_compute(_elist[0], 0, tg, [2, 3])
            rest = list(enumerate(_elist))[1:]
        else:
            emit_w1_load(_elist[0], 0, tg)
            rest = list(enumerate(_elist))
        for ei, e in rest:
            if ei + 1 < len(_elist) and not (call == 0 and ei == 0):
                emit_w1_load(_elist[ei + 1], (ei + 1) % 2, tg)
            emit_w32_load(e, tg)
            emit_expert_compute(e, ei, tg, [0, 1, 2, 3])

    if PB_MODE == 2:
        emit_ffn(0, pre_b=emit_add_b)
    else:
        emit_add_b()
        emit_ffn(0)
    if stop_after == "ffn1":
        emit_store()
        return nc, p, sems

    import os as _os
    ABPART = int(_os.environ.get("ABPART", "4"))
    tg = "ab"
    for hf in range(2):
        emit_ln(x_own, XO_H[hf], 8, xn3, XNb_H[hf], tg, base=8 * hf)
        emit_transpose(xn3, XNb_H[hf], 8, xnT, XNTb_H[hf], TOK, tg,
                       base=8 * hf)
    p.barrier("ab-w")
    p.dma("sync", abw1_sb[:, 0:8192], scr_abw1, writes=[BF("abw1_sb")], tag=tg)
    for tt in range(NTT if (ABPART >= 2) else 0):
        XNTb = XNTb_H[tt // 8]
        XO = XO_H[tt // 8]
        XNB = XNb_H[tt // 8]
        dbank = [PBK[0][0], PBK[0][1]]
        for dc in range(8):
            for half in range(2):
                t_op(lambda t, tt=tt, dc=dc, half=half: t.matmul(
                        P[0][:, half * 512:half * 512 + 512],
                        xnT[:, dc * TOK + tt * 128: dc * TOK + tt * 128 + 128],
                        abw1_sb[:, dc * 1024 + half * 512:
                                dc * 1024 + half * 512 + 512],
                        start=(dc == 0), stop=False),
                     reads=[XNTb, BF("abw1_sb")], writes=[dbank[half]], tag=tg)
            t_op(lambda t, tt=tt, dc=dc: t.matmul(
                    P[2][:, 0:16],
                    xnT[:, dc * TOK + tt * 128: dc * TOK + tt * 128 + 128],
                    abw2t_sb[:, dc * 16:(dc + 1) * 16],
                    start=(dc == 0), stop=False),
                 reads=[XNTb, BF("abw2t_sb")], writes=[PBK[2][0]], tag=tg)
            t_op(lambda t, tt=tt, dc=dc: t.matmul(
                    P[2][:, 512:528],
                    xnT[:, dc * TOK + tt * 128: dc * TOK + tt * 128 + 128],
                    abw3t_sb[:, dc * 16:(dc + 1) * 16],
                    start=(dc == 0), stop=False),
                 reads=[XNTb, BF("abw3t_sb")], writes=[PBK[2][1]], tag=tg)
        for half in range(2):
            t_op(lambda t, half=half: t.matmul(
                    P[0][:, half * 512:half * 512 + 512], e0f,
                    ab_b1_sb[:, half * 512:half * 512 + 512],
                    start=False, stop=True),
                 reads=[BF("e0f"), BF("ab_b1_sb")], writes=[dbank[half]],
                 tag=tg)
        t_op(lambda t: t.matmul(P[2][:, 0:16], e0f, ab_b2_sb,
                                start=False, stop=True),
             reads=[BF("e0f"), BF("ab_b2_sb")], writes=[PBK[2][0]], tag=tg)
        t_op(lambda t: t.matmul(P[2][:, 512:528], e0f, ab_b3_sb,
                                start=False, stop=True),
             reads=[BF("e0f"), BF("ab_b3_sb")], writes=[PBK[2][1]], tag=tg)
        if ABPART == 2:
            v_op(lambda v, tt=tt: v.tensor_add(
                    x_own[:, tt * 1024:(tt + 1) * 1024],
                    x_own[:, tt * 1024:(tt + 1) * 1024], P[0][:, 0:1024]),
                 reads=[dbank[0], dbank[1], XO], writes=[XO], tag=tg)
            v_op(lambda v: v.tensor_copy(bsb, P[2][:, 0:16]),
                 reads=[PBK[2][0]], writes=[BF("bsb")], tag=tg)
            v_op(lambda v: v.tensor_copy(wsmall[:, 24:40], P[2][:, 512:528]),
                 reads=[PBK[2][1]], writes=[BF("w24")], tag=tg)
            continue
        s_op(lambda s: s.activation(dsc1, P[0][:, 0:1024], AF.Exp),
             reads=dbank, writes=[BF("dsc1")], tag=tg)
        s_op(lambda s: s.activation(dsc2, P[0][:, 0:1024], AF.Exp, scale=-1.0),
             reads=dbank, writes=[BF("dsc2")], tag=tg)
        v_op(lambda v: v.tensor_tensor(dsc2, dsc1, dsc2, ALU.min),
             reads=[BF("dsc1"), BF("dsc2")], writes=[BF("dsc2")], tag=tg)
        s_op(lambda s: s.activation(dsc1, dsc2, AF.Ln, bias=1.0),
             reads=[BF("dsc2")], writes=[BF("dsc1")], tag=tg)
        v_op(lambda v: v.tensor_scalar_max(delta_v, P[0][:, 0:1024], 0.0),
             reads=dbank, writes=[BF("delta_v")], tag=tg)
        v_op(lambda v: v.tensor_add(delta_v, delta_v, dsc1),
             reads=[BF("dsc1"), BF("delta_v")], writes=[BF("delta_v")], tag=tg)
        v_op(lambda v: v.tensor_copy(bsb, P[2][:, 0:16]),
             reads=[PBK[2][0]], writes=[BF("bsb")], tag=tg)
        v_op(lambda v: v.tensor_mul(wsmall[:, 24:40], P[2][:, 512:528], bsb),
             reads=[PBK[2][1], BF("bsb")], writes=[BF("w24")], tag=tg)
        v_op(lambda v, tt=tt: v.reduce_sum(bcv[:, tt:tt + 1],
                                           wsmall[:, 24:40], axis=AX.X),
             reads=[BF("w24")], writes=[BF("bcv")], tag=tg)
        v_op(lambda v, tt=tt: v.tensor_mul(
                dsc2, xn3[:, tt * 1024:(tt + 1) * 1024], delta_v),
             reads=[XNB, BF("delta_v")], writes=[BF("dsc2")], tag=tg)
        if ABPART == 3 or ABPART >= 20:
            v_op(lambda v, tt=tt: v.tensor_add(
                    x_own[:, tt * 1024:(tt + 1) * 1024],
                    x_own[:, tt * 1024:(tt + 1) * 1024], delta_v),
                 reads=[BF("delta_v"), XO], writes=[XO], tag=tg)
            continue
        v_op(lambda v, tt=tt: v.scalar_tensor_tensor(
                x_own[:, tt * 1024:(tt + 1) * 1024], dsc2,
                bcv[:, tt:tt + 1], x_own[:, tt * 1024:(tt + 1) * 1024],
                op0=ALU.mult, op1=ALU.add),
             reads=[BF("dsc2"), BF("bcv"), XO], writes=[XO], tag=tg)

    if stop_after == "attnb":
        emit_store()
        return nc, p, sems

    emit_ffn(1)
    emit_store()
    return nc, p, sems


def assemble(nc, p, sems):
    with nc.Block() as blk:
        def make(section):
            def body(eng):
                for waits, emit, sem, inc, tag in p.ops[section]:
                    for sname, val in waits.items():
                        eng.wait_ge(sems[sname], val)
                    inst = emit(eng)
                    if inc:
                        inst.then_inc(sems[sem], inc)
            return body
        blk.tensor(make("tensor"))
        blk.vector(make("vector"))
        blk.scalar(make("scalar"))
        blk.gpsimd(make("gpsimd"))
        blk.sync(make("sync"))
    return nc


_BUILD_CACHE = {}


def _get_nc(stop_after):
    if stop_after not in _BUILD_CACHE:
        nc, p, sems = build_nc(stop_after)
        _BUILD_CACHE[stop_after] = assemble(nc, p, sems)
    return _BUILD_CACHE[stop_after]


def kernel(**inputs):
    f32 = np.float32
    x = np.asarray(inputs["L0_attn_a_batch"], f32)
    Wqkv = np.asarray(inputs["L0_attn_a_W_qkv"], f32)
    u_qkv = np.asarray(inputs["L0_attn_a_u_qkv"], f32)
    sp_qkv = np.asarray(inputs["L0_attn_a_sigma_qkv"], f32)
    Wproj = np.asarray(inputs["L0_attn_a_W_proj"], f32)
    u_proj = np.asarray(inputs["L0_attn_a_u_proj"], f32)
    sp_proj = np.asarray(inputs["L0_attn_a_sigma_proj"], f32)
    abW1 = np.asarray(inputs["L0_attn_b_W1"], f32)
    abb1 = np.asarray(inputs["L0_attn_b_b1"], f32)
    abW2 = np.asarray(inputs["L0_attn_b_W2"], f32)
    abb2 = np.asarray(inputs["L0_attn_b_b2"], f32)
    abW3 = np.asarray(inputs["L0_attn_b_W3"], f32)
    abb3 = np.asarray(inputs["L0_attn_b_b3"], f32)
    gate_w = np.asarray(inputs["L0_ffn_gate_w"], f32)
    gate_b = np.asarray(inputs["L0_ffn_gate_b"], f32)
    w1 = np.asarray(inputs["L0_ffn_w1"], f32)
    b1 = np.asarray(inputs["L0_ffn_b1"], f32)
    w3 = np.asarray(inputs["L0_ffn_w3"], f32)
    b3 = np.asarray(inputs["L0_ffn_b3"], f32)
    w2 = np.asarray(inputs["L0_ffn_w2"], f32)
    b2 = np.asarray(inputs["L0_ffn_b2"], f32)
    ws1 = np.asarray(inputs["L0_ffn_ws1"], f32)
    bs1 = np.asarray(inputs["L0_ffn_bs1"], f32)
    ws3 = np.asarray(inputs["L0_ffn_ws3"], f32)
    bs3 = np.asarray(inputs["L0_ffn_bs3"], f32)
    ws2 = np.asarray(inputs["L0_ffn_ws2"], f32)
    bs2 = np.asarray(inputs["L0_ffn_bs2"], f32)

    _ = inputs["L0_attn_b_A"]
    C = np.ascontiguousarray
    fmp = (np.arange(1024, dtype=np.float32)[None, :]
           - np.arange(128, dtype=np.float32)[:, None])
    shared = {
        "fmp": C(fmp),
        "wqkv": C(Wqkv), "wqkvT": C(Wqkv.T),
        "u_qkv": C(u_qkv.reshape(1, -1)), "sp_qkv": C(sp_qkv.reshape(1, 1)),
        "wproj": C(Wproj), "wprojT": C(Wproj.T),
        "u_proj": C(u_proj.reshape(1, -1)), "sp_proj": C(sp_proj.reshape(1, 1)),
        "w1t": C(w1.transpose(0, 2, 1)), "w3t": C(w3.transpose(0, 2, 1)),
        "w2t": C(w2.transpose(0, 2, 1)),
        "ws1t": C(ws1.T), "ws3t": C(ws3.T), "ws2t": C(ws2.T),
        "gwt": C(gate_w.T), "gate_b": C(gate_b.reshape(1, -1)),
        "b1c": C(b1.reshape(E, 8, 128).transpose(2, 0, 1).reshape(128, E * 8)),
        "b3c": C(b3.reshape(E, 8, 128).transpose(2, 0, 1).reshape(128, E * 8)),
        "bs1c": C(bs1.reshape(16, 128).T),
        "bs3c": C(bs3.reshape(16, 128).T),
        "b2bs2": C(np.concatenate([b2, bs2.reshape(1, -1)], axis=0)),
        "ab_w1t": C(abW1.T), "ab_w2t": C(abW2.T), "ab_w3t": C(abW3.T),
        "ab_b1": C(abb1.reshape(1, -1)), "ab_b2": C(abb2.reshape(1, -1)),
        "ab_b3": C(abb3.reshape(1, -1)),
    }
    in_maps = []
    for c in range(8):
        b, g = c // 4, c % 4
        hs = 4 * g
        qcols = slice(hs * 64, hs * 64 + 256)
        m = dict(shared)
        m["x"] = C(x[b])
        m["xown"] = C(x[b, g * TOK:(g + 1) * TOK])
        m["wq_slice"] = C(np.concatenate(
            [Wqkv[:, qcols], Wqkv[:, 1024 + hs * 64:1024 + hs * 64 + 256],
             Wqkv[:, 2048 + hs * 64:2048 + hs * 64 + 256]], axis=1))
        m["wp_slice"] = C(Wproj[hs * 64:hs * 64 + 256, :])
        in_maps.append(m)

    nc = _get_nc(STOP_AFTER)
    res = run_bass_kernel_spmd(nc, in_maps, core_ids=list(range(8)))
    y = np.empty((2, N, D), np.float32)
    for c in range(8):
        b, g = c // 4, c % 4
        y[b, g * TOK:(g + 1) * TOK] = res.results[c]["y"]
    return y


# version 11
# speedup vs baseline: 1.0929x; 1.0070x over previous
import numpy as np
import os as _os2
import ml_dtypes
import concourse.bass as bass
import concourse.mybir as mybir
from concourse.bass_utils import run_bass_kernel_spmd

F32 = mybir.dt.float32
BF16 = mybir.dt.bfloat16
U8 = mybir.dt.uint8
AF = mybir.ActivationFunctionType
ALU = mybir.AluOpType
AX = mybir.AxisListType

N, D, TOK, NTT, UH, E = 8192, 1024, 2048, 16, 4, 8
EPS = 1e-5
INV_D = 1.0 / D

STOP_AFTER = None
RS_EARLY = True
PB_MODE = 3
SAME_ENGINE_WAITS = True
DBG = False


class Buf:
    __slots__ = ("name", "writer", "readers")

    def __init__(self, name):
        self.name = name
        self.writer = None
        self.readers = {}


class Prog:
    COMPUTE_SEM = {"tensor": "pe", "vector": "dv", "scalar": "ac", "gpsimd": "gp"}
    DMA_SEM = {"sync": "qs", "vector": "qv", "scalar": "qc", "gpsimd": "qg"}
    SECTIONS = ("tensor", "vector", "scalar", "gpsimd", "sync")

    def __init__(self):
        self.ops = {s: [] for s in self.SECTIONS}
        self.counts = {s: 0 for s in
                       ("pe", "dv", "ac", "gp", "qs", "qv", "qc", "qg", "cc")}
        self.synced = {}

    def _dep(self, waits, section, sem, val, psection):
        if val <= 0:
            return
        if psection == section and self.COMPUTE_SEM.get(section) == sem:
            if section == "tensor" or not SAME_ENGINE_WAITS:
                return
        if self.synced.get((section, sem), 0) >= val:
            return
        if waits.get(sem, 0) < val:
            waits[sem] = val

    def op(self, section, emit, reads=(), writes=(), sem=None, inc=None, tag=""):
        if sem is None:
            sem = self.COMPUTE_SEM[section]
        if inc is None:
            inc = 16 if sem in ("qs", "qv", "qc", "qg") else 1
        waits = {}
        for b in reads:
            if b.writer is not None:
                self._dep(waits, section, b.writer[0], b.writer[1], b.writer[2])
        for b in writes:
            if b.writer is not None:
                self._dep(waits, section, b.writer[0], b.writer[1], b.writer[2])
            for rsem, (rval, rsec) in b.readers.items():
                self._dep(waits, section, rsem, rval, rsec)
        for sname, val in waits.items():
            key = (section, sname)
            self.synced[key] = max(self.synced.get(key, 0), val)
        self.counts[sem] += inc
        newval = self.counts[sem]
        self.ops[section].append((waits, emit, sem, inc, tag))
        for b in reads:
            old = b.readers.get(sem)
            if old is None or old[0] < newval:
                b.readers[sem] = (newval, section)
        for b in writes:
            b.writer = (sem, newval, section)
            b.readers = {}
        return newval

    def dma(self, section, out_ap, in_ap, reads=(), writes=(), tag="",
            slow=False):
        sem = self.DMA_SEM[section]

        def emit(eng, o=out_ap, i=in_ap, slow=slow):
            return eng.dma_start(out=o, in_=i,
                                 allow_slow_non_contiguous=slow)
        return self.op(section, emit, reads=reads, writes=writes, sem=sem,
                       inc=16, tag=tag)

    def barrier(self, tag="barrier"):
        snap = dict(self.counts)
        for sec in self.SECTIONS:
            waits = {}
            for sname, val in snap.items():
                if val > 0 and self.synced.get((sec, sname), 0) < val:
                    waits[sname] = val
            for sname, val in waits.items():
                self.synced[(sec, sname)] = val
            if sec == "sync":
                sem, inc = "qs", 0
            else:
                sem, inc = self.COMPUTE_SEM[sec], 1
            if inc:
                self.counts[sem] += inc

            def emit(eng):
                return eng.nop()
            self.ops[sec].append((waits, emit, sem, inc, tag))


def build_nc(stop_after=None):
    nc = bass.Bass("TRN2", target_bir_lowering=False, debug=False)
    p = Prog()

    def ein(name, shape):
        return nc.dram_tensor(name, list(shape), F32, kind="ExternalInput").ap()

    x_in = ein("x", (N, D))
    xown_in = ein("xown", (TOK, D))
    wq_slice = ein("wq_slice", (D, 768))
    wp_slice = ein("wp_slice", (256, D))
    wqkv = ein("wqkv", (D, 3 * D))
    wqkvT = ein("wqkvT", (3 * D, D))
    u_qkv = ein("u_qkv", (1, 3 * D))
    sp_qkv = ein("sp_qkv", (1, 1))
    wproj = ein("wproj", (D, D))
    wprojT = ein("wprojT", (D, D))
    u_proj = ein("u_proj", (1, D))
    sp_proj = ein("sp_proj", (1, 1))
    w1t_in = ein("w1t", (E, D, 1024))
    w3t_in = ein("w3t", (E, D, 1024))
    w2t_in = ein("w2t", (E, 1024, D))
    ws1t_in = ein("ws1t", (D, 2048))
    ws3t_in = ein("ws3t", (D, 2048))
    ws2t_in = ein("ws2t", (2048, D))
    gwt_in = ein("gwt", (D, E))
    gate_b_in = ein("gate_b", (1, E))
    b1c_in = ein("b1c", (128, E * 8))
    b3c_in = ein("b3c", (128, E * 8))
    bs1c_in = ein("bs1c", (128, 16))
    bs3c_in = ein("bs3c", (128, 16))
    b2bs2_in = ein("b2bs2", (9, D))
    ab_w1t_in = ein("ab_w1t", (D, D))
    ab_w2t_in = ein("ab_w2t", (D, 16))
    ab_w3t_in = ein("ab_w3t", (D, 16))
    ab_b1_in = ein("ab_b1", (1, D))
    ab_b2_in = ein("ab_b2", (1, 16))
    ab_b3_in = ein("ab_b3", (1, 16))
    fmp_in = ein("fmp", (128, 1024))

    y_out = nc.dram_tensor("y", [TOK, D], F32, kind="ExternalOutput").ap()

    cinA = nc.dram_tensor("cinA", [N // 2, D], F32).ap()
    cinB = nc.dram_tensor("cinB", [N // 2, D], F32).ap()
    coutA = nc.dram_tensor("coutA", [TOK // 2, D], F32).ap()
    coutB = nc.dram_tensor("coutB", [TOK // 2, D], F32).ap()
    scr_w1t = nc.dram_tensor("scr_w1t", [E, 128, 8 * 1024], BF16).ap()
    scr_w3t = nc.dram_tensor("scr_w3t", [E, 128, 8 * 1024], BF16).ap()
    scr_w2t = nc.dram_tensor("scr_w2t", [E, 128, 8 * D], BF16).ap()
    scr_ws1t = nc.dram_tensor("scr_ws1t", [128, 8 * 2048], BF16).ap()
    scr_ws3t = nc.dram_tensor("scr_ws3t", [128, 8 * 2048], BF16).ap()
    scr_ws2t = nc.dram_tensor("scr_ws2t", [128, 16 * D], BF16).ap()
    scr_abw1 = nc.dram_tensor("scr_abw1", [128, 8 * D], BF16).ap()

    ARENA = 206848
    arena = nc.alloc_sbuf_tensor("arena", [128, ARENA], U8)
    arena_f = arena[:].bitcast(F32)
    arena_b = arena[:].bitcast(BF16)

    def av(off, parts, cols, dtype):
        assert off % 32 == 0, off
        if dtype == F32:
            assert off + cols * 4 <= ARENA, (off, cols)
            return arena_f[0:parts, off // 4: off // 4 + cols]
        assert off + cols * 2 <= ARENA, (off, cols)
        return arena_b[0:parts, off // 2: off // 2 + cols]

    ones_f = av(0, 128, 128, F32)
    ones_b = av(512, 128, 128, BF16)
    ident_f = av(768, 128, 128, F32)
    ident_b = av(1280, 128, 128, BF16)
    c28b = av(1536, 128, 8, F32)
    ccb = av(1568, 128, 8, F32)
    scal = av(1600, 1, 24, F32)
    stat = av(1696, 128, 120, F32)
    gwt_sb = av(2176, 128, 8 * E, BF16)
    gate_b_sb = av(2304, 128, E, F32)
    b1c_sb = av(2336, 128, 64, F32)
    b3c_sb = av(2592, 128, 64, F32)
    bs1c_sb = av(2848, 128, 16, F32)
    bs3c_sb = av(2912, 128, 16, F32)
    ab_b1_sb = av(2976, 128, D, F32)
    ab_b2_sb = av(7072, 128, 16, F32)
    ab_b3_sb = av(7136, 128, 16, F32)
    b2bs2_sb = av(7200, 9, D, BF16)
    negst = av(11296, 128, 256, BF16)
    negm_w = av(11808, 128, 2048, BF16)
    lr = av(15904, 128, 16, F32)
    linv = av(15968, 1, 1024, F32)
    abw2t_sb = av(20064, 128, 8 * 16, BF16)
    abw3t_sb = av(20320, 128, 8 * 16, BF16)
    wsmall = av(20576, 128, 40, F32)
    junk = av(20736, 128, 1024, F32)
    mstat = av(24832, 128, 256, F32)
    gta_t = av(25856, 9, 128, F32)
    gate_sb = av(26368, 128, NTT * E, F32)
    gsc = av(26880, 128, 8, F32)
    bsb = av(26912, 128, 16, F32)
    bcv = av(26976, 128, 16, F32)
    e0b = av(27040, 128, 128, BF16)
    e0f = av(27296, 128, 128, F32)

    OFF_W = 28928
    x_own = av(OFF_W, 128, NTT * D, F32)
    xa = av(OFF_W, 128, 4 * D, F32)
    xnc = av(OFF_W + 16384, 128, 4 * D, BF16)
    xnTc = av(OFF_W + 24576, 128, 4 * D, BF16)
    wp2_sb = av(OFF_W + 32768, 128, 2 * D, BF16)
    obfT = av(OFF_W, 128, 2 * N, BF16)
    ptile = [av(OFF_W + 36864 + i * 2048, 128, 1024, BF16) for i in range(4)]
    o_acc = [av(OFF_W + 45056 + i * 4096, 128, 1024, F32) for i in range(4)]
    o_acc4 = av(OFF_W + 45056, 128, 4096, F32)
    ystage = av(94464, 128, 4 * D, F32)
    wq_sb = av(OFF_W + 36864, 128, 8 * 768, BF16)
    fmp_sb = av(OFF_W + 61440, 128, 1024, F32)

    OFF_Q = 94464
    q_fm = av(OFF_Q, 128, 2 * N, BF16)
    sgst = [av(OFF_Q, 128, 3 * D, F32), av(OFF_Q + 12288, 128, 3 * D, F32)]
    srow = av(OFF_Q + 24576, 1, 3 * D, F32)
    arow = av(OFF_Q + 28672, 1, 3 * D, F32)
    OFF_K = 127232
    k_fm = av(OFF_K, 128, 2 * N, BF16)
    srow = av(OFF_K, 1, 3 * D, F32)
    arow = av(OFF_K + 12288, 1, 3 * D, F32)
    xn_t = av(OFF_K, 128, NTT * D, BF16)
    xn3 = xn_t
    w1sb = [av(OFF_K, 128, 8 * 1024, BF16),
            av(OFF_K + 16384, 128, 8 * 1024, BF16)]
    xnT = av(OFF_Q, 128, 8 * TOK, BF16)

    OFF_V = 160000
    v_fm = av(OFF_V, 128, 64 * 260, BF16)
    w3sb = av(OFF_V, 128, 8 * 1024, BF16)
    w2sb = av(OFF_V + 16384, 128, 8 * D, BF16)
    abw1_sb = av(OFF_V, 128, 8 * D, BF16)
    delta_v = av(OFF_V + 16384, 128, D, F32)
    dsc1 = av(OFF_V + 20480, 128, D, F32)
    dsc2 = av(OFF_V + 24576, 128, D, F32)

    OFF_T = 193280
    wst = [av(OFF_T, 128, 1024, F32), av(OFF_T + 4096, 128, 1024, F32)]
    wsbst = [av(OFF_T + 8192, 128, 1024, BF16),
             av(OFF_T + 10240, 128, 1024, BF16)]
    h1sb = av(OFF_T, 128, 8 * 512, BF16)

    P = [nc.alloc_psum_tensor(f"P{i}", [128, 1024], F32) for i in range(4)]
    PB3 = P[3][:].bitcast(BF16)
    PB2A = P[2][:].bitcast(BF16)

    sems = {n: nc.alloc_semaphore(n)
            for n in ("pe", "dv", "ac", "gp", "qs", "qv", "qc", "qg", "cc")}

    bufs = {}

    def BF(name):
        if name not in bufs:
            bufs[name] = Buf(name)
        return bufs[name]

    PBK = [(BF(f"P{i}a"), BF(f"P{i}b")) for i in range(4)]

    def t_op(emit, reads=(), writes=(), tag=""):
        return p.op("tensor", emit, reads, writes, tag=tag)

    def v_op(emit, reads=(), writes=(), tag=""):
        return p.op("vector", emit, reads, writes, tag=tag)

    def s_op(emit, reads=(), writes=(), tag=""):
        return p.op("scalar", emit, reads, writes, tag=tag)

    def g_op(emit, reads=(), writes=(), tag=""):
        return p.op("gpsimd", emit, reads, writes, tag=tag)

    g_op(lambda g: g.memset(ones_f, 1.0), writes=[BF("ones_f")])
    g_op(lambda g: g.memset(ones_b, 1.0), writes=[BF("ones_b")])
    g_op(lambda g: g.memset(gta_t, 1.0), writes=[BF("gta_ones")])
    g_op(lambda g: g.memset(v_fm, 1.0), writes=[BF("v_fm")])
    def emit_e0(t, buf):
        def e(g, t=t):
            g.memset(t, 0.0)
            g.memset(t[0:1, :], 1.0)
            return g.memset(t[64:65, :], 1.0)
        g_op(e, writes=[buf])
    emit_e0(e0b, BF("e0b"))
    emit_e0(e0f, BF("e0f"))
    g_op(lambda g: g.memset(negm_w, 0.0), writes=[BF("negm_w")])
    for t_, b_ in ((gate_b_sb, "gate_b_sb"), (ab_b1_sb, "ab_b1_sb"),
                   (ab_b2_sb, "ab_b2_sb"), (ab_b3_sb, "ab_b3_sb")):
        g_op(lambda g, t_=t_: g.memset(t_, 0.0), writes=[BF(b_)])

    def mk_ident(t, buf):
        def emit(g, t=t):
            g.memset(t, 0.0)
            return g.affine_select(out=t, in_=t, compare_op=ALU.not_equal,
                                   fill=1.0, base=0, pattern=[[-1, 128]],
                                   channel_multiplier=1)
        g_op(emit, writes=[buf])
    mk_ident(ident_b, BF("ident_b"))
    mk_ident(ident_f, BF("ident_f"))

    p.dma("sync", fmp_sb, fmp_in, writes=[BF("fmp")])
    for name, dst, src in (
            ("gate_b_sb", gate_b_sb[0:1, :], gate_b_in),
            ("b1c_sb", b1c_sb, b1c_in), ("b3c_sb", b3c_sb, b3c_in),
            ("bs1c_sb", bs1c_sb, bs1c_in), ("bs3c_sb", bs3c_sb, bs3c_in),
            ("ab_b1_sb", ab_b1_sb[0:1, :], ab_b1_in),
            ("ab_b2_sb", ab_b2_sb[0:1, :], ab_b2_in), ("ab_b3_sb", ab_b3_sb[0:1, :], ab_b3_in)):
        p.dma("sync", dst, src, writes=[BF(name)])

    def load_cast(dst_sb, dst_buf, hbm, nchunks, cols, tag):
        view = hbm.rearrange("(c p) e -> p c e", p=128)
        for c in range(nchunks):
            stb = BF(f"wst{c % 2}")
            p.dma("sync", wst[c % 2][:, 0:cols], view[:, c, :],
                  writes=[stb], tag=tag)
            v_op(lambda v, c=c: v.tensor_copy(
                    dst_sb[:, c * cols:(c + 1) * cols], wst[c % 2][:, 0:cols]),
                 reads=[stb], writes=[dst_buf], tag=tag)

    p.dma("sync", wst[0][0:9, 0:D], b2bs2_in, writes=[BF("wst0")], tag="b2b")
    v_op(lambda v: v.tensor_copy(b2bs2_sb, wst[0][0:9, 0:D]),
         reads=[BF("wst0")], writes=[BF("b2bs2_sb")], tag="b2b")
    load_cast(gwt_sb, BF("gwt_sb"), gwt_in, 8, E, "gwt")
    load_cast(abw2t_sb, BF("abw2t_sb"), ab_w2t_in, 8, 16, "abw2t")
    load_cast(abw3t_sb, BF("abw3t_sb"), ab_w3t_in, 8, 16, "abw3t")
    load_cast(wq_sb, BF("wq_sb"), wq_slice, 8, 768, "wqs")

    def emit_sigma(w_nat, w_T, u_row, sp_row, m, idx, tg):
        mc = m // 128
        p.dma("sync", srow[0:1, 0:m], u_row, writes=[BF("srow")], tag=tg)
        for i in range(mc):
            t_op(lambda t, i=i: t.transpose(P[3][0:128, 0:1],
                                            srow[0:1, i * 128:(i + 1) * 128],
                                            ident_f[0:1, 0:1]),
                 reads=[BF("srow"), BF("ident_f")], writes=[PBK[3][0]], tag=tg)
            v_op(lambda v, i=i: v.tensor_copy(wsmall[:, i:i + 1],
                                              P[3][0:128, 0:1]),
                 reads=[PBK[3][0]], writes=[BF("ucol")], tag=tg)
        for i in range(mc):
            stb = BF(f"sgst{i % 2}")
            p.dma("sync", sgst[i % 2][:, 0:D], w_T[i * 128:(i + 1) * 128, :],
                  writes=[stb], tag=tg)
            for sl in range(2):
                t_op(lambda t, i=i, sl=sl: t.matmul(
                        P[2][0:1, sl * 512:(sl + 1) * 512],
                        wsmall[:, i:i + 1],
                        sgst[i % 2][:, sl * 512:(sl + 1) * 512],
                        start=(i == 0), stop=(i == mc - 1)),
                     reads=[BF("ucol"), stb], writes=[PBK[2][sl]], tag=tg)
        for sl in range(2):
            v_op(lambda v, sl=sl: v.tensor_copy(
                    arow[0:1, sl * 512:(sl + 1) * 512],
                    P[2][0:1, sl * 512:(sl + 1) * 512]),
                 reads=[PBK[2][sl]], writes=[BF("arow")], tag=tg)
        if tg == "sp" and DBG:
            dbgs = nc.dram_tensor("dbgs", [128, 1040], F32,
                                  kind="ExternalOutput").ap()
            p.dma("sync", dbgs[0:1, 0:1024], arow[0:1, 0:D],
                  reads=[BF("arow")], writes=[BF("dbgs")], tag=tg)
            p.dma("sync", dbgs[:, 1024:1032], wsmall[:, 0:8],
                  reads=[BF("ucol")], writes=[BF("dbgs")], tag=tg)
            p.dma("sync", dbgs[:, 1032:1040], wsmall[:, 24:32],
                  reads=[BF("acol")], writes=[BF("dbgs")], tag=tg)
        s_op(lambda s: s.activation(junk[0:1, 0:D], arow[0:1, 0:D], AF.Square,
                                    accum_out=scal[0:1, 8 + idx:9 + idx]),
             reads=[BF("arow")], writes=[BF("junk"), BF("n1" + tg)], tag=tg)
        for i in range(8):
            t_op(lambda t, i=i: t.transpose(P[3][0:128, 0:1],
                                            arow[0:1, i * 128:(i + 1) * 128],
                                            ident_f[0:1, 0:1]),
                 reads=[BF("arow"), BF("ident_f")], writes=[PBK[3][0]], tag=tg)
            v_op(lambda v, i=i: v.tensor_copy(wsmall[:, 24 + i:25 + i],
                                              P[3][0:128, 0:1]),
                 reads=[PBK[3][0]], writes=[BF("acol")], tag=tg)
        nsl = m // 512
        bank_of = [PBK[0][0], PBK[0][1], PBK[1][0], PBK[1][1],
                   PBK[2][0], PBK[2][1]]
        pt_of = [P[0], P[0], P[1], P[1], P[2], P[2]]
        for i in range(8):
            stb = BF(f"sgst{i % 2}")
            p.dma("sync", sgst[i % 2][:, 0:m], w_nat[i * 128:(i + 1) * 128, :],
                  writes=[stb], tag=tg)
            for sl in range(nsl):
                t_op(lambda t, i=i, sl=sl: t.matmul(
                        pt_of[sl][0:1, (sl % 2) * 512:(sl % 2) * 512 + 512],
                        wsmall[:, 24 + i:25 + i],
                        sgst[i % 2][:, sl * 512:(sl + 1) * 512],
                        start=(i == 0), stop=(i == 7)),
                     reads=[BF("acol"), stb], writes=[bank_of[sl]], tag=tg)
        for sl in range(nsl):
            v_op(lambda v, sl=sl: v.tensor_copy(
                    srow[0:1, sl * 512:(sl + 1) * 512],
                    pt_of[sl][0:1, (sl % 2) * 512:(sl % 2) * 512 + 512]),
                 reads=[bank_of[sl]], writes=[BF("srow")], tag=tg)
        s_op(lambda s, m=m: s.activation(arow[0:1, 0:m], srow[0:1, 0:m],
                                         AF.Square,
                                         accum_out=scal[0:1, 10 + idx:11 + idx]),
             reads=[BF("srow")], writes=[BF("arow"), BF("n2" + tg)], tag=tg)
        p.dma("sync", scal[0:1, 12 + idx:13 + idx], sp_row,
              writes=[BF("sp" + tg)], tag=tg)
        s_op(lambda s: s.activation(scal[0:1, 14 + idx:15 + idx],
                                    scal[0:1, 8 + idx:9 + idx], AF.Ln),
             reads=[BF("n1" + tg)], writes=[BF("l1" + tg)], tag=tg)
        s_op(lambda s: s.activation(scal[0:1, 16 + idx:17 + idx],
                                    scal[0:1, 10 + idx:11 + idx], AF.Ln),
             reads=[BF("n2" + tg)], writes=[BF("l2" + tg)], tag=tg)
        v_op(lambda v: v.tensor_sub(scal[0:1, 18 + idx:19 + idx],
                                    scal[0:1, 14 + idx:15 + idx],
                                    scal[0:1, 16 + idx:17 + idx]),
             reads=[BF("l1" + tg), BF("l2" + tg)], writes=[BF("df" + tg)],
             tag=tg)
        s_op(lambda s: s.activation(scal[0:1, 20 + idx:21 + idx],
                                    scal[0:1, 18 + idx:19 + idx], AF.Exp,
                                    scale=0.5),
             reads=[BF("df" + tg)], writes=[BF("ex" + tg)], tag=tg)
        v_op(lambda v: v.tensor_mul(scal[0:1, idx:idx + 1],
                                    scal[0:1, 20 + idx:21 + idx],
                                    scal[0:1, 12 + idx:13 + idx]),
             reads=[BF("ex" + tg), BF("sp" + tg)], writes=[BF("c" + tg)],
             tag=tg)

    emit_sigma(wqkv, wqkvT, u_qkv, sp_qkv, 3 * D, 0, "sq")
    emit_sigma(wproj, wprojT, u_proj, sp_proj, D, 1, "sp")

    v_op(lambda v: v.tensor_scalar(scal[0:1, 2:3], scal[0:1, 0:1],
                                   scal[0:1, 0:1], 0.125,
                                   op0=ALU.mult, op1=ALU.mult),
         reads=[BF("csq")], writes=[BF("c28s")])
    v_op(lambda v: v.tensor_mul(scal[0:1, 3:4], scal[0:1, 0:1],
                                scal[0:1, 1:2]),
         reads=[BF("csq"), BF("csp")], writes=[BF("ccs")])
    t_op(lambda t: t.matmul(P[3][0:128, 0:1], ones_f[0:1, :], scal[0:1, 2:3],
                            start=True, stop=True),
         reads=[BF("ones_f"), BF("c28s")], writes=[PBK[3][0]])
    v_op(lambda v: v.tensor_copy(c28b[:, 0:1], P[3][0:128, 0:1]),
         reads=[PBK[3][0]], writes=[BF("c28b")])
    t_op(lambda t: t.matmul(P[3][0:128, 1:2], ones_f[0:1, :], scal[0:1, 3:4],
                            start=True, stop=True),
         reads=[BF("ones_f"), BF("ccs")], writes=[PBK[3][0]])
    v_op(lambda v: v.tensor_copy(ccb[:, 0:1], P[3][0:128, 1:2]),
         reads=[PBK[3][0]], writes=[BF("ccb")])

    p.barrier("post-sigma")

    prep_jobs = []
    for e in range(E):
        for c in range(8):
            prep_jobs.append((w1t_in[e].rearrange("(c p) i -> p c i", p=128),
                              (c, 0), scr_w1t[e][:, c * 1024:(c + 1) * 1024]))
            prep_jobs.append((w3t_in[e].rearrange("(c p) i -> p c i", p=128),
                              (c, 0), scr_w3t[e][:, c * 1024:(c + 1) * 1024]))
            prep_jobs.append((w2t_in[e].rearrange("(c p) i -> p c i", p=128),
                              (c, 0), scr_w2t[e][:, c * 1024:(c + 1) * 1024]))
    vws1 = ws1t_in.rearrange("(c p) i -> p c i", p=128)
    vws3 = ws3t_in.rearrange("(c p) i -> p c i", p=128)
    vws2 = ws2t_in.rearrange("(c p) i -> p c i", p=128)
    vab1 = ab_w1t_in.rearrange("(c p) i -> p c i", p=128)
    for c in range(8):
        for half in range(2):
            prep_jobs.append((vws1, (c, half * 1024),
                              scr_ws1t[:, half * 8192 + c * 1024:
                                       half * 8192 + c * 1024 + 1024]))
            prep_jobs.append((vws3, (c, half * 1024),
                              scr_ws3t[:, half * 8192 + c * 1024:
                                       half * 8192 + c * 1024 + 1024]))
    for c in range(16):
        prep_jobs.append((vws2, (c, 0), scr_ws2t[:, c * 1024:(c + 1) * 1024]))
    for c in range(8):
        prep_jobs.append((vab1, (c, 0), scr_abw1[:, c * 1024:(c + 1) * 1024]))

    prep_i = [0]

    def emit_prep(npieces):
        for _ in range(npieces):
            j = prep_i[0]
            if j >= len(prep_jobs):
                return
            prep_i[0] += 1
            view, (c, off), dst = prep_jobs[j]
            src = view[:, c, off:off + 1024]
            i = j % 2
            stb, obb = BF(f"wst{i}"), BF(f"wsbst{i}")
            p.dma("scalar", wst[i][:, 0:1024], src, writes=[stb], tag="prep")
            s_op(lambda s, i=i: s.activation(wsbst[i][:, 0:1024],
                                             wst[i][:, 0:1024], AF.Copy),
                 reads=[stb], writes=[obb], tag="prep")
            p.dma("scalar", dst, wsbst[i][:, 0:1024], reads=[obb],
                  writes=[BF("scr")], tag="prep")

    def emit_ln(src, src_buf, ntiles, xn_dst, xn_buf, tag, base=0, xn_w=None):
        xw = list(xn_w) if xn_w is not None else [xn_buf]
        for i in range(base, base + ntiles):
            v_op(lambda v, i=i: v.reduce_sum(stat[:, i:i + 1],
                                             src[:, i * D:(i + 1) * D],
                                             axis=AX.X),
                 reads=[src_buf], writes=[BF("stat")], tag=tag)
            s_op(lambda s, i=i: s.activation(junk, src[:, i * D:(i + 1) * D],
                                             AF.Square,
                                             accum_out=stat[:, 24 + i:25 + i]),
                 reads=[src_buf], writes=[BF("junk"), BF("stat")], tag=tag)
        nt, b0 = ntiles, base
        v_op(lambda v, nt=nt, b0=b0: v.tensor_scalar(stat[:, 48 + b0:48 + b0 + nt],
                                              stat[:, b0:b0 + nt], INV_D, None,
                                              op0=ALU.mult),
             reads=[BF("stat")], writes=[BF("stat")], tag=tag)
        v_op(lambda v, nt=nt, b0=b0: v.tensor_mul(stat[:, 72 + b0:72 + b0 + nt],
                                           stat[:, 48 + b0:48 + b0 + nt],
                                           stat[:, 48 + b0:48 + b0 + nt]),
             reads=[BF("stat")], writes=[BF("stat")], tag=tag)
        v_op(lambda v, nt=nt, b0=b0: v.scalar_tensor_tensor(
                stat[:, b0:b0 + nt], stat[:, 24 + b0:24 + b0 + nt], INV_D,
                stat[:, 72 + b0:72 + b0 + nt], op0=ALU.mult, op1=ALU.subtract),
             reads=[BF("stat")], writes=[BF("stat")], tag=tag)
        v_op(lambda v, nt=nt, b0=b0: v.tensor_scalar(stat[:, b0:b0 + nt],
                                              stat[:, b0:b0 + nt],
                                              EPS, None, op0=ALU.add),
             reads=[BF("stat")], writes=[BF("stat")], tag=tag)
        s_op(lambda s, nt=nt, b0=b0: s.activation(stat[:, 24 + b0:24 + b0 + nt],
                                           stat[:, b0:b0 + nt], AF.Ln),
             reads=[BF("stat")], writes=[BF("stat")], tag=tag)
        s_op(lambda s, nt=nt, b0=b0: s.activation(stat[:, 96 + b0:96 + b0 + nt],
                                           stat[:, 24 + b0:24 + b0 + nt], AF.Exp,
                                           scale=-0.5),
             reads=[BF("stat")], writes=[BF("stat")], tag=tag)
        for i in range(base, base + ntiles):
            v_op(lambda v, i=i: v.tensor_scalar(
                    xn_dst[:, i * D:(i + 1) * D], src[:, i * D:(i + 1) * D],
                    stat[:, 48 + i:49 + i], stat[:, 96 + i:97 + i],
                    op0=ALU.subtract, op1=ALU.mult),
                 reads=[src_buf, BF("stat")], writes=xw, tag=tag)

    def emit_transpose(xn_src, xn_buf, ntiles, dst, dst_buf, dst_tok, tag,
                       tbank=3, base=0):
        pbv = PB3 if tbank == 3 else PB2A
        for i in range(base, base + ntiles):
            off = (i % 2) * 1024
            bank = PBK[tbank][i % 2]
            for dc in range(8):
                t_op(lambda t, i=i, dc=dc, off=off, pbv=pbv: t.transpose(
                        pbv[:, off + dc * 128: off + dc * 128 + 128],
                        xn_src[:, i * D + dc * 128: i * D + dc * 128 + 128],
                        ident_b),
                     reads=[xn_buf, BF("ident_b")], writes=[bank], tag=tag)
            v_op(lambda v, i=i, off=off, pbv=pbv: v.tensor_copy(
                    dst[:].rearrange("p (c t) -> p c t", c=8)[
                        :, :, i * 128:(i + 1) * 128],
                    pbv[:, off:off + 1024].rearrange("p (c t) -> p c t", c=8)),
                 reads=[bank], writes=[dst_buf], tag=tag)

    QF, KF, VF = BF("q_fm"), BF("k_fm"), BF("v_fm")
    xin_v = x_in.rearrange("(c r p) d -> c p r d", p=128, r=4)
    xa_v = xa.rearrange("p (r d) -> p r d", d=D)
    for ch in range(16):
        xab = BF("xa")
        p.dma("sync", xa_v, xin_v[ch], writes=[xab], tag=f"A{ch}")
        emit_ln(xa, xab, 4, xnc, BF("xnc"), f"Aln{ch}")
        if DBG and ch == 0:
            dbg0 = nc.dram_tensor("dbg0", [128, 2176], F32,
                                  kind="ExternalOutput").ap()
            p.dma("sync", dbg0[:, 0:1024], xa[:, 0:1024],
                  reads=[xab], writes=[BF("dbg0")])
            p.dma("sync", dbg0[:, 1024:1144].bitcast(F32) if False else
                  dbg0[:, 1024:1144], stat[:, 0:120],
                  reads=[BF("stat")], writes=[BF("dbg0")])
            p.dma("sync", dbg0[:, 1152:2176].bitcast(BF16),
                  xnc[:, 0:2048], reads=[BF("xnc")], writes=[BF("dbg0")])
        emit_transpose(xnc, BF("xnc"), 4, xnTc, BF("xnTc"), 512, f"Atr{ch}", tbank=2)
        emit_prep(7)
        for dst, dbuf, colo, pi in ((q_fm, QF, 0, 3), (k_fm, KF, 256, 2)):
            for pair in range(2):
                bank = PBK[pi][pair]
                for dc in range(8):
                    t_op(lambda t, pair=pair, dc=dc, colo=colo, pi=pi: t.matmul(
                            P[pi][:, pair * 512:pair * 512 + 512],
                            wq_sb[:, dc * 768 + colo + pair * 128:
                                  dc * 768 + colo + pair * 128 + 128],
                            xnTc[:, dc * 512:(dc + 1) * 512],
                            start=(dc == 0), stop=(dc == 7)),
                         reads=[BF("wq_sb"), BF("xnTc")], writes=[bank],
                         tag=f"Aqk{ch}")
                s_op(lambda s, pair=pair, dst=dst, ch=ch, pi=pi: s.activation(
                        dst[:, pair * N + ch * 512: pair * N + ch * 512 + 512],
                        P[pi][:, pair * 512:pair * 512 + 512], AF.Copy),
                     reads=[bank], writes=[dbuf], tag=f"Aqk{ch}")
        for sub in range(4):
            bank = PBK[3][sub % 2]
            for dc in range(8):
                t_op(lambda t, sub=sub, dc=dc: t.matmul(
                        P[3][:, (sub % 2) * 512:(sub % 2) * 512 + 256],
                        xnTc[:, dc * 512 + sub * 128: dc * 512 + sub * 128 + 128],
                        wq_sb[:, dc * 768 + 512: dc * 768 + 768],
                        start=(dc == 0), stop=(dc == 7)),
                     reads=[BF("wq_sb"), BF("xnTc")], writes=[bank],
                     tag=f"Av{ch}")
            v_op(lambda v, sub=sub, ch=ch: v.tensor_copy(
                    v_fm[:].rearrange("p (s h e) -> p s h e", h=4, e=65)[
                        :, ch * 4 + sub, :, 0:64],
                    P[3][:, (sub % 2) * 512:(sub % 2) * 512 + 256].rearrange(
                        "p (h e) -> p h e", h=4)),
                 reads=[bank], writes=[VF], tag=f"Av{ch}")
    p.barrier("post-A")

    OBF = BF("obfT")
    PB2 = P[2][:].bitcast(BF16)
    PREP_B = [1, 2, 3, 4, 6, 7, 8, 9]
    load_cast(wp2_sb, BF("wp2_sb"), wp_slice, 2, D, "wp2")
    cinA_v = cinA.rearrange("(q p) d -> p q d", p=128)
    cinB_v = cinB.rearrange("(q p) d -> p q d", p=128)
    o_acc4_v = o_acc4[:].rearrange("p (q d) -> p q d", d=D)
    OACCB = [BF("oacc0"), BF("oacc1"), BF("oacc2"), BF("oacc3")]

    def emit_proj(tq):
        pb = PBK[tq % 2]
        for i in range(4):
            tt = tq * 4 + i
            for half in range(2):
                for k in range(2):
                    t_op(lambda t, tt=tt, half=half, k=k, tq=tq: t.matmul(
                            P[tq % 2][:, half * 512:half * 512 + 512],
                            obfT[:, k * N + tt * 128: k * N + tt * 128 + 128],
                            wp2_sb[:, k * 1024 + half * 512:
                                   k * 1024 + half * 512 + 512],
                            start=(k == 0), stop=(k == 1)),
                         reads=[OBF, BF("wp2_sb")], writes=[pb[half]],
                         tag="proj")
            v_op(lambda v, i=i, tq=tq: v.tensor_scalar(
                    o_acc[i][:, 0:1024],
                    P[tq % 2][:, 0:1024], ccb[:, 0:1], None, op0=ALU.mult),
                 reads=[pb[0], pb[1], BF("ccb")], writes=[OACCB[i]],
                 tag="proj")
        j = tq % 4
        chunk_v, cb = (cinA_v, "cinA") if j < 2 else (cinB_v, "cinB")
        qt0 = (tq // 4) * 8 + (j % 2) * 4
        p.dma("sync", chunk_v[:, qt0:qt0 + 4, :], o_acc4_v,
              reads=list(OACCB), writes=[BF(cb)], tag="proj")

    def emit_rs(src, dst, sb, db):
        def emit(g, src=src, dst=dst):
            return g.collective_compute(
                "ReduceScatter", ALU.add,
                replica_groups=[[0, 1, 2, 3], [4, 5, 6, 7]],
                ins=[src.opt()], outs=[dst.opt()])
        p.op("gpsimd", emit, reads=[BF(sb)], writes=[BF(db)],
             sem="cc", inc=1, tag="rs")

    for qw in [0, 2, 4, 6, 1, 3, 5, 7]:
        nkc = 2 * (qw + 1)
        for h in range(UH):
            Ppv = P[2 + h % 2]
            pvb = [PBK[2 + h % 2][0], PBK[2 + h % 2][1]]
            nks = nkc * 4
            units = [(kc * 512 + ksub * 128, kc * 4 + ksub)
                     for kc in range(nkc) for ksub in range(4)]

            def emit_qk_exp(ks):
                ka, sub_abs = units[ks]
                Pst = P[ks % 2]
                stb = [PBK[ks % 2][0], PBK[ks % 2][1]]
                for half in range(2):
                    t_op(lambda t, h=h, ka=ka, qw=qw, half=half,
                         Pst=Pst: t.matmul(
                            Pst[:, half * 512:half * 512 + 512],
                            k_fm[(h % 2) * 64:(h % 2) * 64 + 64,
                                 (h // 2) * N + ka:
                                 (h // 2) * N + ka + 128],
                            q_fm[(h % 2) * 64:(h % 2) * 64 + 64,
                                 (h // 2) * N + qw * 1024 + half * 512:
                                 (h // 2) * N + qw * 1024
                                 + half * 512 + 512],
                            start=True, stop=True),
                         reads=[KF, QF], writes=[stb[half]], tag=f"B{qw}")
                pt = ptile[ks % 4]
                ptb = BF(f"ptile{ks % 4}")
                s_op(lambda s, pt=pt, Pst=Pst: s.activation(
                        pt, Pst[:, 0:1024], AF.Exp, scale=c28b[:, 0:1]),
                     reads=[stb[0], stb[1], BF("c28b")],
                     writes=[ptb], tag=f"B{qw}")
                if ka + 127 >= qw * 1024:
                    thr = float(ka - qw * 1024)
                    v_op(lambda v, pt=pt, thr=thr:
                         v.scalar_tensor_tensor(
                            pt, fmp_sb, thr, pt,
                            op0=ALU.is_ge, op1=ALU.mult),
                         reads=[ptb, BF("fmp")], writes=[ptb],
                         tag=f"B{qw}")

            def emit_pv(ks):
                ka, sub_abs = units[ks]
                pt = ptile[ks % 4]
                ptb = BF(f"ptile{ks % 4}")
                for half in range(2):
                    t_op(lambda t, h=h, sub_abs=sub_abs, half=half,
                         Ppv=Ppv, pt=pt, ks=ks, nks=nks: t.matmul(
                            Ppv[0:65, half * 512:half * 512 + 512],
                            v_fm[:, sub_abs * 260 + h * 65:
                                 sub_abs * 260 + h * 65 + 65],
                            pt[:, half * 512:half * 512 + 512],
                            start=(ks == 0), stop=(ks == nks - 1)),
                         reads=[VF, ptb], writes=[pvb[half]], tag=f"B{qw}")

            LAG = 2
            for step in range(nks + LAG):
                if step < nks:
                    emit_qk_exp(step)
                if step >= LAG:
                    emit_pv(step - LAG)
            ob = BF(f"oacc{h}")
            v_op(lambda v, Ppv=Ppv: v.reciprocal(
                    linv[0:1, 0:1024], Ppv[64:65, 0:1024]),
                 reads=pvb, writes=[BF("linv")], tag=f"Bf{qw}")
            v_op(lambda v, h=h, Ppv=Ppv: v.tensor_copy(
                    o_acc[h][0:65, :], Ppv[0:65, 0:1024]),
                 reads=pvb, writes=[ob], tag=f"Bf{qw}")
            for half in range(2):
                t_op(lambda t, half=half, Ppv=Ppv: t.matmul(
                        Ppv[0:64, half * 512:half * 512 + 512],
                        ones_f[0:1, 0:64],
                        linv[0:1, half * 512:half * 512 + 512],
                        start=True, stop=True),
                     reads=[BF("ones_f"), BF("linv")], writes=[pvb[half]],
                     tag=f"Bf{qw}")
                v_op(lambda v, h=h, qw=qw, half=half, Ppv=Ppv: v.tensor_mul(
                        obfT[(h % 2) * 64:(h % 2) * 64 + 64,
                             (h // 2) * N + qw * 1024 + half * 512:
                             (h // 2) * N + qw * 1024 + half * 512 + 512],
                        o_acc[h][0:64, half * 512:half * 512 + 512],
                        Ppv[0:64, half * 512:half * 512 + 512]),
                     reads=[ob, pvb[half]], writes=[OBF], tag=f"Bf{qw}")
            emit_prep(PREP_B[qw])
        emit_proj(2 * qw)
        emit_proj(2 * qw + 1)
        if RS_EARLY and qw == 6:
            emit_rs(cinA, coutA, "cinA", "coutA")

    assert prep_i[0] == len(prep_jobs), (prep_i[0], len(prep_jobs))
    if not RS_EARLY:
        emit_rs(cinA, coutA, "cinA", "coutA")
    emit_rs(cinB, coutB, "cinB", "coutB")
    if PB_MODE == 0:
        p.barrier("post-rs")

    XO_A, XO_B = BF("x_ownA"), BF("x_ownB")
    XNb_A, XNb_B = BF("xn_tA"), BF("xn_tB")
    XNTb_A, XNTb_B = BF("xnTA"), BF("xnTB")
    GS_A, GS_B = BF("gateA"), BF("gateB")
    XO_H = (XO_A, XO_B)
    XNb_H = (XNb_A, XNb_B)
    XNTb_H = (XNTb_A, XNTb_B)
    GS_H = (GS_A, GS_B)
    stgA = av(OFF_K, 128, 4096, F32)
    stgB = av(OFF_K + 16384, 128, 4096, F32)
    stgA_v = stgA[:].rearrange("p (q d) -> p q d", d=D)
    stgB_v = stgB[:].rearrange("p (q d) -> p q d", d=D)
    coutA_v = coutA.rearrange("(q p) d -> p q d", p=128)
    coutB_v = coutB.rearrange("(q p) d -> p q d", p=128)

    xo_v = xown_in.rearrange("(q p) d -> p q d", p=128)
    p.dma("sync", x_own[:].rearrange("p (q d) -> p q d", d=D), xo_v,
          writes=[XO_A, XO_B, OBF, BF("ptile0"), BF("ptile1"), BF("ptile2"),
                  BF("ptile3")] + list(OACCB),
          tag="xown")
    for blk in range(2):
        p.dma("sync", stgA_v, coutA_v[:, blk * 4:(blk + 1) * 4, :],
              reads=[BF("coutA")], writes=[XNb_A, KF], tag="rsrd")
        v_op(lambda v, blk=blk: v.tensor_add(
                x_own[:, blk * 4096:(blk + 1) * 4096],
                x_own[:, blk * 4096:(blk + 1) * 4096], stgA),
             reads=[XNb_A, XO_A], writes=[XO_A], tag="rsrd")

    def emit_add_b():
        for blk in range(2, 4):
            p.dma("gpsimd", stgB_v, coutB_v[:, (blk - 2) * 4:(blk - 1) * 4, :],
                  reads=[BF("coutB")], writes=[XNb_B], tag="rsrd")
            v_op(lambda v, blk=blk: v.tensor_add(
                    x_own[:, blk * 4096:(blk + 1) * 4096],
                    x_own[:, blk * 4096:(blk + 1) * 4096], stgB),
                 reads=[XNb_B, XO_B], writes=[XO_B], tag="rsrd")

    if DBG:
        dbgf = nc.dram_tensor("dbgf", [128, 1024], F32, kind="ExternalOutput").ap()
        dbgb = nc.dram_tensor("dbgb", [128, 2048], BF16, kind="ExternalOutput").ap()
        p.barrier("dbg")
        p.dma("sync", dbgf[0:1, 0:24], scal, reads=[], writes=[BF("dbgf")])
        p.dma("sync", dbgf[:, 24:32], c28b, writes=[BF("dbgf")])
        p.dma("sync", dbgf[:, 32:288], mstat, writes=[BF("dbgf")])
        p.dma("sync", dbgf[:, 288:544], o_acc[0][:, 0:256], writes=[BF("dbgf")])
        p.dma("sync", dbgf[:, 544:560], lr, writes=[BF("dbgf")])
        p.dma("sync", dbgf[0:1, 560:816], linv[0:1, 0:256], writes=[BF("dbgf")])
        p.dma("sync", dbgb[:, 0:512], q_fm[:, 0:512], writes=[BF("dbgb")])
        p.dma("sync", dbgb[:, 512:1024], k_fm[:, 0:512], writes=[BF("dbgb")])
        p.dma("sync", dbgb[:, 1024:1280], negst, writes=[BF("dbgb")])
        p.dma("sync", dbgb[:, 1280:1792], v_fm[:, 0:512], writes=[BF("dbgb")])
        p.dma("sync", dbgb[:, 1792:2048], obfT[:, 0:256], writes=[BF("dbgb")])
        p.barrier("dbg2")

    def emit_store():
        p.dma("sync", y_out.rearrange("(q p) d -> p q d", p=128),
              x_own[:].rearrange("p (q d) -> p q d", d=D),
              reads=[XO_A, XO_B], writes=[BF("y")], tag="store")
        p.barrier("final")

    if stop_after == "attn":
        emit_add_b()
        emit_store()
        return nc, p, sems

    FPART = int(_os2.environ.get("FPART", "3")) if True else 3
    FSUB = _os2.environ.get("FSUB", "c")
    def emit_ffn(call, pre_b=None):
        tg = f"F{call}"
        for hf in range(2):
            if hf == 1 and pre_b is not None:
                pre_b()
            emit_ln(x_own, XO_H[hf], 8, xn_t, XNb_H[hf], tg, base=8 * hf,
                    xn_w=(XNb_H[hf], BF(f"w1sb{hf}")) if call == 1 else None)
            emit_transpose(xn_t, XNb_H[hf], 8, xnT, XNTb_H[hf], TOK, tg,
                           base=8 * hf)
            for tt in range(8 * hf, 8 * hf + 8):
                emit_gate(call, tt, tg)
            if call == 0 and hf == 0 and FPART >= 3 and PB_MODE != 3:
                emit_expert_part(0, tg, first=True)
        if call == 1:
            p.barrier(tg + "-w")
        emit_experts(call, tg)
        emit_biascombo(call, tg)

    def emit_gate(call, tt, tg):
        XNTb = XNTb_H[tt // 8]
        gate_w_buf = GS_H[tt // 8]
        if True:
            gb = PBK[2][tt % 2]
            goff = (tt % 2) * 512
            for dc in range(8):
                t_op(lambda t, tt=tt, dc=dc, goff=goff: t.matmul(
                        P[2][:, goff:goff + 8],
                        xnT[:, dc * TOK + tt * 128: dc * TOK + tt * 128 + 128],
                        gwt_sb[:, dc * 8:(dc + 1) * 8],
                        start=(dc == 0), stop=False),
                     reads=[XNTb, BF("gwt_sb")], writes=[gb], tag=tg)
            t_op(lambda t, goff=goff: t.matmul(
                    P[2][:, goff:goff + 8], e0f, gate_b_sb,
                    start=False, stop=True),
                 reads=[BF("e0f"), BF("gate_b_sb")], writes=[gb], tag=tg)
            v_op(lambda v, goff=goff: v.tensor_reduce(
                    wsmall[:, 0:1], P[2][:, goff:goff + 8], axis=AX.X,
                    op=ALU.max, negate=True),
                 reads=[gb], writes=[BF("wg")], tag=tg)
            s_op(lambda s, goff=goff: s.activation(
                    gsc, P[2][:, goff:goff + 8], AF.Exp,
                    bias=wsmall[:, 0:1], accum_out=wsmall[:, 1:2]),
                 reads=[gb, BF("wg")], writes=[BF("gsc"), BF("wg")], tag=tg)
            v_op(lambda v: v.reciprocal(wsmall[:, 2:3], wsmall[:, 1:2]),
                 reads=[BF("wg")], writes=[BF("wg")], tag=tg)
            v_op(lambda v: v.tensor_scalar(gsc, gsc, wsmall[:, 2:3], None,
                                           op0=ALU.mult),
                 reads=[BF("wg"), BF("gsc")], writes=[BF("gsc")], tag=tg)
            v_op(lambda v: v.tensor_reduce(wsmall[:, 3:4], gsc, axis=AX.X,
                                           op=ALU.max),
                 reads=[BF("gsc")], writes=[BF("wg")], tag=tg)
            v_op(lambda v: v.tensor_single_scalar(wsmall[:, 8:16], gsc,
                                                  wsmall[:, 3:4], ALU.is_ge),
                 reads=[BF("gsc"), BF("wg")], writes=[BF("wg8")], tag=tg)
            v_op(lambda v: v.scalar_tensor_tensor(wsmall[:, 16:24],
                                                  wsmall[:, 8:16], -1.0e30,
                                                  gsc, op0=ALU.mult,
                                                  op1=ALU.add),
                 reads=[BF("wg8"), BF("gsc")], writes=[BF("wg16")], tag=tg)
            v_op(lambda v: v.tensor_reduce(wsmall[:, 4:5], wsmall[:, 16:24],
                                           axis=AX.X, op=ALU.max),
                 reads=[BF("wg16")], writes=[BF("wg")], tag=tg)
            v_op(lambda v: v.tensor_single_scalar(wsmall[:, 8:16], gsc,
                                                  wsmall[:, 4:5], ALU.is_ge),
                 reads=[BF("gsc"), BF("wg")], writes=[BF("wg8")], tag=tg)
            v_op(lambda v, tt=tt: v.tensor_mul(gate_sb[:, tt * 8:tt * 8 + 8],
                                               gsc, wsmall[:, 8:16]),
                 reads=[BF("gsc"), BF("wg8")], writes=[gate_w_buf], tag=tg)

    def emit_biascombo(call, tg):
        if FPART < 2:
            return
        gta_full = av(193280 + 8192, 9, TOK, BF16)
        g_op(lambda g: g.memset(gta_full, 1.0),
             writes=[BF("gtaf"), BF("wsbst0"), BF("wsbst1")], tag=tg)
        for tt in range(NTT):
            bank = PBK[3][tt % 2]
            t_op(lambda t, tt=tt: t.transpose(
                    P[3][0:8, (tt % 2) * 512:(tt % 2) * 512 + 128],
                    gate_sb[:, tt * 8:tt * 8 + 8], ident_f),
                 reads=[GS_H[tt // 8], BF("ident_f")], writes=[bank],
                 tag=tg)
            v_op(lambda v, tt=tt: v.tensor_copy(
                    gta_full[0:8, tt * 128:(tt + 1) * 128],
                    P[3][0:8, (tt % 2) * 512:(tt % 2) * 512 + 128]),
                 reads=[bank, BF("gtaf")], writes=[BF("gtaf")], tag=tg)
        for tt in range(NTT):
            bank = PBK[3][tt % 2]
            for half in range(2):
                t_op(lambda t, tt=tt, half=half: t.matmul(
                        P[3][:, (tt % 2) * 512:(tt % 2) * 512 + 512],
                        gta_full[0:9, tt * 128:(tt + 1) * 128],
                        b2bs2_sb[0:9, half * 512:half * 512 + 512],
                        start=True, stop=True),
                     reads=[BF("gtaf"), BF("b2bs2_sb")],
                     writes=[bank], tag=tg)
                v_op(lambda v, tt=tt, half=half: v.tensor_add(
                        x_own[:, tt * 1024 + half * 512:
                              tt * 1024 + half * 512 + 512],
                        x_own[:, tt * 1024 + half * 512:
                              tt * 1024 + half * 512 + 512],
                        P[3][:, (tt % 2) * 512:(tt % 2) * 512 + 512]),
                     reads=[bank, XO_H[tt // 8]], writes=[XO_H[tt // 8]],
                     tag=tg)

    def _elist_of():
        _elist = list(range(10 if FPART >= 3 else 0))
        if _os2.environ.get("FONLY") == "s":
            _elist = [0, 8, 9]
        elif _os2.environ.get("FONLY") == "e":
            _elist = list(range(8))
        return _elist

    def emit_w1_load(e, slot, tg, extra=()):
        src = scr_w1t[e] if e < 8 else scr_ws1t[:, (e - 8) * 8192:
                                                (e - 8) * 8192 + 8192]
        p.dma("sync", w1sb[slot][:, 0:8192], src,
              writes=[BF(f"w1sb{slot}")] + list(extra), tag=tg)

    def emit_w32_load(e, tg):
        if e < 8:
            p.dma("sync", w3sb[:, 0:8192], scr_w3t[e], writes=[BF("w3sb")],
                  tag=tg)
            p.dma("sync", w2sb[:, 0:8192], scr_w2t[e], writes=[BF("w2sb")],
                  tag=tg)
        else:
            p.dma("sync", w3sb[:, 0:8192],
                  scr_ws3t[:, (e - 8) * 8192:(e - 8) * 8192 + 8192],
                  writes=[BF("w3sb")], tag=tg)
            p.dma("sync", w2sb[:, 0:8192],
                  scr_ws2t[:, (e - 8) * 8192:(e - 8) * 8192 + 8192],
                  writes=[BF("w2sb")], tag=tg)

    def emit_expert_compute(e, ei, tg, tbs):
        if e < 8:
            b1col, b3col, bco = b1c_sb, b3c_sb, e * 8
        else:
            b1col, b3col, bco = bs1c_sb, bs3c_sb, (e - 8) * 8
        w1b = BF(f"w1sb{ei % 2}")
        w1cur = w1sb[ei % 2]
        for tb in tbs:
            XNTb = XNTb_H[tb // 2]
            for ic in range(8):
                bi = ic % 4
                pt_ = P[bi // 2]
                bank = PBK[bi // 2][bi % 2]
                for dc in range(8):
                    t_op(lambda t, ic=ic, dc=dc, tb=tb, bi=bi, pt_=pt_,
                         w1cur=w1cur: t.matmul(
                            pt_[:, (bi % 2) * 512:(bi % 2) * 512 + 512],
                            w1cur[:, dc * 1024 + ic * 128:
                                  dc * 1024 + ic * 128 + 128],
                            xnT[:, dc * TOK + tb * 512:
                                dc * TOK + tb * 512 + 512],
                            start=(dc == 0), stop=(dc == 7)),
                         reads=[w1b, XNTb], writes=[bank], tag=tg)
                s_op(lambda s, ic=ic, bi=bi, pt_=pt_, b1col=b1col,
                     bco=bco: s.activation(
                        h1sb[:, ic * 512:(ic + 1) * 512],
                        pt_[:, (bi % 2) * 512:(bi % 2) * 512 + 512],
                        AF.Silu, bias=b1col[:, bco + ic:bco + ic + 1]),
                     reads=[bank, BF("b1c_sb" if e < 8 else "bs1c_sb")],
                     writes=[BF("h1sb")], tag=tg)
                bank3 = PBK[2 + bi // 2][bi % 2]
                pt3 = P[2 + bi // 2]
                for dc in range(8 if FSUB >= "b" else 0):
                    t_op(lambda t, ic=ic, dc=dc, tb=tb, bi=bi,
                         pt3=pt3: t.matmul(
                            pt3[:, (bi % 2) * 512:(bi % 2) * 512 + 512],
                            w3sb[:, dc * 1024 + ic * 128:
                                 dc * 1024 + ic * 128 + 128],
                            xnT[:, dc * TOK + tb * 512:
                                dc * TOK + tb * 512 + 512],
                            start=(dc == 0), stop=(dc == 7)),
                         reads=[BF("w3sb"), XNTb], writes=[bank3], tag=tg)
                if FSUB >= "b":
                    v_op(lambda v, ic=ic, bi=bi, pt3=pt3, b3col=b3col,
                         bco=bco: v.scalar_tensor_tensor(
                            h1sb[:, ic * 512:(ic + 1) * 512],
                            pt3[:, (bi % 2) * 512:(bi % 2) * 512 + 512],
                            b3col[:, bco + ic:bco + ic + 1],
                            h1sb[:, ic * 512:(ic + 1) * 512],
                            op0=ALU.add, op1=ALU.mult),
                         reads=[bank3, BF("h1sb"),
                                BF("b3c_sb" if e < 8 else "bs3c_sb")],
                         writes=[BF("h1sb")], tag=tg)
            for ts in range(4 if FSUB >= "c" else 0):
                tt = tb * 4 + ts
                Pw = P[2 + ts % 2]
                pwb = [PBK[2 + ts % 2][0], PBK[2 + ts % 2][1]]
                for ic in range(8):
                    for half in range(2):
                        t_op(lambda t, ic=ic, ts=ts, half=half,
                             Pw=Pw: t.matmul(
                                Pw[:, half * 512:half * 512 + 512],
                                h1sb[:, ic * 512 + ts * 128:
                                     ic * 512 + ts * 128 + 128],
                                w2sb[:, ic * 1024 + half * 512:
                                     ic * 1024 + half * 512 + 512],
                                start=(ic == 0), stop=(ic == 7)),
                             reads=[BF("h1sb"), BF("w2sb")],
                             writes=[pwb[half]], tag=tg)
                s_op(lambda s, Pw=Pw: s.activation(junk, Pw[:, 0:1024],
                                                   AF.Copy),
                     reads=pwb, writes=[BF("junk")], tag=tg)
                if e < 8:
                    v_op(lambda v, tt=tt, e=e: v.scalar_tensor_tensor(
                            x_own[:, tt * 1024:(tt + 1) * 1024],
                            junk,
                            gate_sb[:, tt * 8 + e:tt * 8 + e + 1],
                            x_own[:, tt * 1024:(tt + 1) * 1024],
                            op0=ALU.mult, op1=ALU.add),
                         reads=[BF("junk"), GS_H[tt // 8], XO_H[tt // 8]],
                         writes=[XO_H[tt // 8]], tag=tg)
                else:
                    v_op(lambda v, tt=tt: v.tensor_add(
                            x_own[:, tt * 1024:(tt + 1) * 1024],
                            x_own[:, tt * 1024:(tt + 1) * 1024],
                            junk),
                         reads=[BF("junk"), XO_H[tt // 8]],
                         writes=[XO_H[tt // 8]], tag=tg)

    def emit_expert_part(e, tg, first):
        emit_w1_load(e, 0, tg, extra=(XNb_A,))
        emit_w32_load(e, tg)
        emit_expert_compute(e, 0, tg, [0, 1])

    def emit_experts(call, tg):
        _elist = _elist_of()
        if not _elist:
            return
        if call == 0 and PB_MODE != 3:
            if len(_elist) > 1:
                emit_w1_load(_elist[1], 1, tg, extra=(XNb_B,))
            emit_expert_compute(_elist[0], 0, tg, [2, 3])
            rest = list(enumerate(_elist))[1:]
            skip0 = True
        else:
            emit_w1_load(_elist[0], 0, tg,
                         extra=(XNb_A,) if call == 0 else ())
            rest = list(enumerate(_elist))
            skip0 = False
        for ei, e in rest:
            if ei + 1 < len(_elist) and not (skip0 and ei == 0):
                emit_w1_load(_elist[ei + 1], (ei + 1) % 2, tg,
                             extra=(XNb_H[(ei + 1) % 2],)
                             if call == 0 and ei == 0 else ())
            emit_w32_load(e, tg)
            emit_expert_compute(e, ei, tg, [0, 1, 2, 3])

    if PB_MODE >= 2:
        emit_ffn(0, pre_b=emit_add_b)
    else:
        emit_add_b()
        emit_ffn(0)
    if stop_after == "ffn1":
        emit_store()
        return nc, p, sems

    import os as _os
    ABPART = int(_os.environ.get("ABPART", "4"))
    tg = "ab"
    for hf in range(2):
        emit_ln(x_own, XO_H[hf], 8, xn3, XNb_H[hf], tg, base=8 * hf)
        emit_transpose(xn3, XNb_H[hf], 8, xnT, XNTb_H[hf], TOK, tg,
                       base=8 * hf)
    p.barrier("ab-w")
    p.dma("sync", abw1_sb[:, 0:8192], scr_abw1, writes=[BF("abw1_sb")], tag=tg)
    for tt in range(NTT if (ABPART >= 2) else 0):
        XNTb = XNTb_H[tt // 8]
        XO = XO_H[tt // 8]
        XNB = XNb_H[tt // 8]
        dbank = [PBK[0][0], PBK[0][1]]
        for dc in range(8):
            for half in range(2):
                t_op(lambda t, tt=tt, dc=dc, half=half: t.matmul(
                        P[0][:, half * 512:half * 512 + 512],
                        xnT[:, dc * TOK + tt * 128: dc * TOK + tt * 128 + 128],
                        abw1_sb[:, dc * 1024 + half * 512:
                                dc * 1024 + half * 512 + 512],
                        start=(dc == 0), stop=False),
                     reads=[XNTb, BF("abw1_sb")], writes=[dbank[half]], tag=tg)
            t_op(lambda t, tt=tt, dc=dc: t.matmul(
                    P[2][:, 0:16],
                    xnT[:, dc * TOK + tt * 128: dc * TOK + tt * 128 + 128],
                    abw2t_sb[:, dc * 16:(dc + 1) * 16],
                    start=(dc == 0), stop=False),
                 reads=[XNTb, BF("abw2t_sb")], writes=[PBK[2][0]], tag=tg)
            t_op(lambda t, tt=tt, dc=dc: t.matmul(
                    P[2][:, 512:528],
                    xnT[:, dc * TOK + tt * 128: dc * TOK + tt * 128 + 128],
                    abw3t_sb[:, dc * 16:(dc + 1) * 16],
                    start=(dc == 0), stop=False),
                 reads=[XNTb, BF("abw3t_sb")], writes=[PBK[2][1]], tag=tg)
        for half in range(2):
            t_op(lambda t, half=half: t.matmul(
                    P[0][:, half * 512:half * 512 + 512], e0f,
                    ab_b1_sb[:, half * 512:half * 512 + 512],
                    start=False, stop=True),
                 reads=[BF("e0f"), BF("ab_b1_sb")], writes=[dbank[half]],
                 tag=tg)
        t_op(lambda t: t.matmul(P[2][:, 0:16], e0f, ab_b2_sb,
                                start=False, stop=True),
             reads=[BF("e0f"), BF("ab_b2_sb")], writes=[PBK[2][0]], tag=tg)
        t_op(lambda t: t.matmul(P[2][:, 512:528], e0f, ab_b3_sb,
                                start=False, stop=True),
             reads=[BF("e0f"), BF("ab_b3_sb")], writes=[PBK[2][1]], tag=tg)
        if ABPART == 2:
            v_op(lambda v, tt=tt: v.tensor_add(
                    x_own[:, tt * 1024:(tt + 1) * 1024],
                    x_own[:, tt * 1024:(tt + 1) * 1024], P[0][:, 0:1024]),
                 reads=[dbank[0], dbank[1], XO], writes=[XO], tag=tg)
            v_op(lambda v: v.tensor_copy(bsb, P[2][:, 0:16]),
                 reads=[PBK[2][0]], writes=[BF("bsb")], tag=tg)
            v_op(lambda v: v.tensor_copy(wsmall[:, 24:40], P[2][:, 512:528]),
                 reads=[PBK[2][1]], writes=[BF("w24")], tag=tg)
            continue
        s_op(lambda s: s.activation(dsc1, P[0][:, 0:1024], AF.Exp),
             reads=dbank, writes=[BF("dsc1")], tag=tg)
        s_op(lambda s: s.activation(dsc2, P[0][:, 0:1024], AF.Exp, scale=-1.0),
             reads=dbank, writes=[BF("dsc2")], tag=tg)
        v_op(lambda v: v.tensor_tensor(dsc2, dsc1, dsc2, ALU.min),
             reads=[BF("dsc1"), BF("dsc2")], writes=[BF("dsc2")], tag=tg)
        s_op(lambda s: s.activation(dsc1, dsc2, AF.Ln, bias=1.0),
             reads=[BF("dsc2")], writes=[BF("dsc1")], tag=tg)
        v_op(lambda v: v.tensor_scalar_max(delta_v, P[0][:, 0:1024], 0.0),
             reads=dbank, writes=[BF("delta_v")], tag=tg)
        v_op(lambda v: v.tensor_add(delta_v, delta_v, dsc1),
             reads=[BF("dsc1"), BF("delta_v")], writes=[BF("delta_v")], tag=tg)
        v_op(lambda v: v.tensor_copy(bsb, P[2][:, 0:16]),
             reads=[PBK[2][0]], writes=[BF("bsb")], tag=tg)
        v_op(lambda v: v.tensor_mul(wsmall[:, 24:40], P[2][:, 512:528], bsb),
             reads=[PBK[2][1], BF("bsb")], writes=[BF("w24")], tag=tg)
        v_op(lambda v, tt=tt: v.reduce_sum(bcv[:, tt:tt + 1],
                                           wsmall[:, 24:40], axis=AX.X),
             reads=[BF("w24")], writes=[BF("bcv")], tag=tg)
        v_op(lambda v, tt=tt: v.tensor_mul(
                dsc2, xn3[:, tt * 1024:(tt + 1) * 1024], delta_v),
             reads=[XNB, BF("delta_v")], writes=[BF("dsc2")], tag=tg)
        if ABPART == 3 or ABPART >= 20:
            v_op(lambda v, tt=tt: v.tensor_add(
                    x_own[:, tt * 1024:(tt + 1) * 1024],
                    x_own[:, tt * 1024:(tt + 1) * 1024], delta_v),
                 reads=[BF("delta_v"), XO], writes=[XO], tag=tg)
            continue
        v_op(lambda v, tt=tt: v.scalar_tensor_tensor(
                x_own[:, tt * 1024:(tt + 1) * 1024], dsc2,
                bcv[:, tt:tt + 1], x_own[:, tt * 1024:(tt + 1) * 1024],
                op0=ALU.mult, op1=ALU.add),
             reads=[BF("dsc2"), BF("bcv"), XO], writes=[XO], tag=tg)

    if stop_after == "attnb":
        emit_store()
        return nc, p, sems

    emit_ffn(1)
    emit_store()
    return nc, p, sems


def assemble(nc, p, sems):
    with nc.Block() as blk:
        def make(section):
            def body(eng):
                for waits, emit, sem, inc, tag in p.ops[section]:
                    for sname, val in waits.items():
                        eng.wait_ge(sems[sname], val)
                    inst = emit(eng)
                    if inc:
                        inst.then_inc(sems[sem], inc)
            return body
        blk.tensor(make("tensor"))
        blk.vector(make("vector"))
        blk.scalar(make("scalar"))
        blk.gpsimd(make("gpsimd"))
        blk.sync(make("sync"))
    return nc


_BUILD_CACHE = {}


def _get_nc(stop_after):
    if stop_after not in _BUILD_CACHE:
        nc, p, sems = build_nc(stop_after)
        _BUILD_CACHE[stop_after] = assemble(nc, p, sems)
    return _BUILD_CACHE[stop_after]


def kernel(**inputs):
    f32 = np.float32
    x = np.asarray(inputs["L0_attn_a_batch"], f32)
    Wqkv = np.asarray(inputs["L0_attn_a_W_qkv"], f32)
    u_qkv = np.asarray(inputs["L0_attn_a_u_qkv"], f32)
    sp_qkv = np.asarray(inputs["L0_attn_a_sigma_qkv"], f32)
    Wproj = np.asarray(inputs["L0_attn_a_W_proj"], f32)
    u_proj = np.asarray(inputs["L0_attn_a_u_proj"], f32)
    sp_proj = np.asarray(inputs["L0_attn_a_sigma_proj"], f32)
    abW1 = np.asarray(inputs["L0_attn_b_W1"], f32)
    abb1 = np.asarray(inputs["L0_attn_b_b1"], f32)
    abW2 = np.asarray(inputs["L0_attn_b_W2"], f32)
    abb2 = np.asarray(inputs["L0_attn_b_b2"], f32)
    abW3 = np.asarray(inputs["L0_attn_b_W3"], f32)
    abb3 = np.asarray(inputs["L0_attn_b_b3"], f32)
    gate_w = np.asarray(inputs["L0_ffn_gate_w"], f32)
    gate_b = np.asarray(inputs["L0_ffn_gate_b"], f32)
    w1 = np.asarray(inputs["L0_ffn_w1"], f32)
    b1 = np.asarray(inputs["L0_ffn_b1"], f32)
    w3 = np.asarray(inputs["L0_ffn_w3"], f32)
    b3 = np.asarray(inputs["L0_ffn_b3"], f32)
    w2 = np.asarray(inputs["L0_ffn_w2"], f32)
    b2 = np.asarray(inputs["L0_ffn_b2"], f32)
    ws1 = np.asarray(inputs["L0_ffn_ws1"], f32)
    bs1 = np.asarray(inputs["L0_ffn_bs1"], f32)
    ws3 = np.asarray(inputs["L0_ffn_ws3"], f32)
    bs3 = np.asarray(inputs["L0_ffn_bs3"], f32)
    ws2 = np.asarray(inputs["L0_ffn_ws2"], f32)
    bs2 = np.asarray(inputs["L0_ffn_bs2"], f32)

    _ = inputs["L0_attn_b_A"]
    C = np.ascontiguousarray
    fmp = (np.arange(1024, dtype=np.float32)[None, :]
           - np.arange(128, dtype=np.float32)[:, None])
    shared = {
        "fmp": C(fmp),
        "wqkv": C(Wqkv), "wqkvT": C(Wqkv.T),
        "u_qkv": C(u_qkv.reshape(1, -1)), "sp_qkv": C(sp_qkv.reshape(1, 1)),
        "wproj": C(Wproj), "wprojT": C(Wproj.T),
        "u_proj": C(u_proj.reshape(1, -1)), "sp_proj": C(sp_proj.reshape(1, 1)),
        "w1t": C(w1.transpose(0, 2, 1)), "w3t": C(w3.transpose(0, 2, 1)),
        "w2t": C(w2.transpose(0, 2, 1)),
        "ws1t": C(ws1.T), "ws3t": C(ws3.T), "ws2t": C(ws2.T),
        "gwt": C(gate_w.T), "gate_b": C(gate_b.reshape(1, -1)),
        "b1c": C(b1.reshape(E, 8, 128).transpose(2, 0, 1).reshape(128, E * 8)),
        "b3c": C(b3.reshape(E, 8, 128).transpose(2, 0, 1).reshape(128, E * 8)),
        "bs1c": C(bs1.reshape(16, 128).T),
        "bs3c": C(bs3.reshape(16, 128).T),
        "b2bs2": C(np.concatenate([b2, bs2.reshape(1, -1)], axis=0)),
        "ab_w1t": C(abW1.T), "ab_w2t": C(abW2.T), "ab_w3t": C(abW3.T),
        "ab_b1": C(abb1.reshape(1, -1)), "ab_b2": C(abb2.reshape(1, -1)),
        "ab_b3": C(abb3.reshape(1, -1)),
    }
    in_maps = []
    for c in range(8):
        b, g = c // 4, c % 4
        hs = 4 * g
        qcols = slice(hs * 64, hs * 64 + 256)
        m = dict(shared)
        m["x"] = C(x[b])
        m["xown"] = C(x[b, g * TOK:(g + 1) * TOK])
        m["wq_slice"] = C(np.concatenate(
            [Wqkv[:, qcols], Wqkv[:, 1024 + hs * 64:1024 + hs * 64 + 256],
             Wqkv[:, 2048 + hs * 64:2048 + hs * 64 + 256]], axis=1))
        m["wp_slice"] = C(Wproj[hs * 64:hs * 64 + 256, :])
        in_maps.append(m)

    nc = _get_nc(STOP_AFTER)
    res = run_bass_kernel_spmd(nc, in_maps, core_ids=list(range(8)))
    y = np.empty((2, N, D), np.float32)
    for c in range(8):
        b, g = c // 4, c % 4
        y[b, g * TOK:(g + 1) * TOK] = res.results[c]["y"]
    return y
